# version 13
# baseline (speedup 1.0000x reference)
import math
import numpy as np
from contextlib import ExitStack

import concourse.bass as bass
import concourse.mybir as mybir
from concourse.bass_utils import run_bass_kernel_spmd

F32 = mybir.dt.float32
BF16 = mybir.dt.bfloat16
AF = mybir.ActivationFunctionType
ALU = mybir.AluOpType
AX = mybir.AxisListType

E = 2048
H = 16
D = 128
C = 128
T_OWN = 1024
T_PRE = 3072
NCH = T_OWN // C
NPC = T_PRE // C
DFF = 8192
NF = E // 128
RMS_EPS = 1e-5
LN_EPS = 1e-5
SY, SC, VE, TE, GP = "sync", "scalar", "vector", "tensor", "gpsimd"
KB = 1024
DBG = []
LAST_RES = None


_R128 = None


def _decay_r128():
    start = 2.0 ** (-(2.0 ** -(math.log2(H) - 3)))
    slopes = np.array([start * start ** i for i in range(H)], dtype=np.float64)
    r = np.exp(-slopes)
    return [float(np.float32(rv ** 128.0)) for rv in r]


def _host_tables():
    start = 2.0 ** (-(2.0 ** -(math.log2(H) - 3)))
    slopes = np.array([start * start ** i for i in range(H)], dtype=np.float64)
    r = np.exp(-slopes)
    c_idx = np.arange(C, dtype=np.float64)
    qdecF = np.zeros((128, E), dtype=np.float64)
    kdec = np.zeros((128, E), dtype=np.float64)
    dmaskT = np.zeros((128, E), dtype=np.float64)
    rCt = np.zeros((128, E), dtype=np.float64)
    for h in range(H):
        qdecF[:, h * 128:(h + 1) * 128] = (r[h] ** (c_idx + 1.0))[None, :]
        kdec[:, h * 128:(h + 1) * 128] = (r[h] ** (C - 1.0 - c_idx))[:, None]
        diff = c_idx[None, :] - c_idx[:, None]
        dm = np.where(diff >= 0, r[h] ** np.maximum(diff, 0.0), 0.0)
        dmaskT[:, h * 128:(h + 1) * 128] = dm
        rCt[:, h * 128:(h + 1) * 128] = r[h] ** float(C)
    freq = (1.0 / 1024.0) ** np.linspace(0.0, 1.0, D // 4, dtype=np.float64)
    return (qdecF.astype(np.float32), kdec.astype(np.float32),
            dmaskT.astype(np.float32), rCt.astype(np.float32), freq)


def _trig(t_abs, freq):
    theta = np.asarray(t_abs, dtype=np.float64)[:, None] * freq[None, :]
    cos = np.tile(np.cos(theta), (1, H)).astype(np.float32)
    sin = np.tile(np.sin(theta), (1, H)).astype(np.float32)
    return np.ascontiguousarray(cos), np.ascontiguousarray(sin)


class Res:
    __slots__ = ("name", "sem", "cnt")

    def __init__(self, name, sem):
        self.name = name
        self.sem = sem
        self.cnt = 0


class Prog:
    CENGS = (SC, VE, TE, GP)

    def __init__(self, nc, ctx):
        self.nc = nc
        self.ctx = ctx
        self.ops = {e: [] for e in (SY, SC, VE, TE, GP)}
        self.csems = {e: ctx.enter_context(nc.semaphore("c_" + e)) for e in self.CENGS}
        self.ccnt = {e: 0 for e in self.CENGS}
        self.waited = {e: {} for e in (SY, SC, VE, TE, GP)}

    def res(self, name):
        return Res(name, self.ctx.enter_context(self.nc.semaphore("r_" + name)))

    def dma(self, eng, out, in_, r):
        r.cnt += 16
        val = r.cnt
        self.ops[eng].append(lambda e, out=out, in_=in_, s=r.sem:
                             e.dma_start(out=out, in_=in_).then_inc(s, 16))
        return ("r_" + r.name, r.sem, val)

    def dmat(self, eng, out, in_, r):
        r.cnt += 16
        val = r.cnt
        self.ops[eng].append(lambda e, out=out, in_=in_, s=r.sem:
                             e.dma_start_transpose(out, in_).then_inc(s, 16))
        return ("r_" + r.name, r.sem, val)

    SELFSYNC = (VE, SC, GP)

    def op(self, eng, fn):
        sem = self.csems[eng]
        if eng in self.SELFSYNC and self.ccnt[eng] > 0:
            prev = self.ccnt[eng]
            self.ops[eng].append(lambda e, sem=sem, v=prev: e.wait_ge(sem, v))
        self.ccnt[eng] += 1
        val = self.ccnt[eng]
        self.ops[eng].append(lambda e, fn=fn, sem=sem: fn(e).then_inc(sem, 1))
        return ("c_" + eng, sem, val)

    def opq(self, eng, fn):
        if eng in self.SELFSYNC:
            self.op(eng, fn)
        else:
            self.ops[eng].append(fn)

    def ctok(self, eng):
        return ("c_" + eng, self.csems[eng], self.ccnt[eng])

    def wait(self, eng, tok):
        if tok is None:
            return
        key, sem, v = tok
        if key == "c_" + eng:
            return
        if self.waited[eng].get(key, 0) >= v:
            return
        self.waited[eng][key] = v
        self.ops[eng].append(lambda e, sem=sem, v=v: e.wait_ge(sem, v))

    def waits(self, eng, toks):
        for t in toks:
            self.wait(eng, t)

    def full(self, r):
        return ("r_" + r.name, r.sem, r.cnt)

    def finalize(self, block):
        for name in (SY, SC, VE, TE, GP):
            ops = self.ops[name]
            getattr(block, name)(lambda e, ops=ops: [f(e) for f in ops])


class Buf:
    def __init__(self):
        self.rd = []
        self.wr = None

    def before_write(self, p, eng):
        p.waits(eng, self.rd)
        if self.wr is not None:
            p.wait(eng, self.wr)

    def wrote(self, tok):
        self.wr = tok
        self.rd = []

    def read(self, tok):
        if tok is not None:
            self.rd.append(tok)


def build_program(ctx):
    global _R128
    _R128 = _decay_r128()
    nc = bass.Bass("TRN2", target_bir_lowering=False, dynamic_dma_scratch_size=16384)

    def din(name, shape, dtype=F32):
        return nc.dram_tensor(name, list(shape), dtype, kind="ExternalInput").ap()

    def dint(name, shape, dtype=BF16):
        kind = "ExternalOutput" if name in DBG else "Internal"
        return nc.dram_tensor(name, list(shape), dtype, kind=kind).ap()

    def sb(name, shape, dtype=F32):
        return ctx.enter_context(nc.sbuf_tensor(name, list(shape), dtype))

    xo = din("xo", (T_OWN, E))
    xp = din("xp", (T_PRE, E))
    wq_d = din("wq", (E, E)); wk_d = din("wk", (E, E))
    wv_d = din("wv", (E, E)); wo_d = din("wo", (E, E))
    w1_d = din("w1", (DFF, E)); w2_d = din("w2", (E, DFF))
    b1c_d = din("b1c", (128, DFF // 128)); b2c_d = din("b2c", (128, NF))
    rmsc_d = din("rmsc", (128, NF))
    cos_o = din("cos_o", (T_OWN, 512)); sin_o = din("sin_o", (T_OWN, 512))
    cos_p = din("cos_p", (T_PRE, 512)); sin_p = din("sin_p", (T_PRE, 512))
    qdecF_d = din("qdecF", (128, E)); kdec_d = din("kdec", (128, E))
    dmaskT_d = din("dmaskT", (128, E))
    out_d = nc.dram_tensor("out", [T_OWN, E], F32, kind="ExternalOutput").ap()

    xn_b = dint("xn_b", (T_OWN, E))
    xnp_b = dint("xnp_b", (T_PRE, E))
    q_b = dint("q_b", (T_OWN, E)); k_b = dint("k_b", (T_OWN, E))
    kd_b = dint("kd_b", (T_OWN, E)); v_b = dint("v_b", (T_OWN, E))
    kdp_b = dint("kdp_b", (T_PRE, E)); vp_b = dint("vp_b", (T_PRE, E))
    w1q_b = dint("w1q_b", (DFF, E)); w2q_b = dint("w2q_b", (E, DFF))
    sg_b = dint("sg_b", (T_OWN, E), F32)
    kvpre_d = dint("kvpre_d", (128, E), F32) if "kvpre_d" in DBG else None
    yn_d = dint("yn_d", (128, NF * 1024)) if "yn_d" in DBG else None
    ysb_d = dint("ysb_d", (T_OWN, E)) if "ysb_d" in DBG else None
    kq_d = dint("kq_d", (2, 128, NF * 512)) if "kq_d" in DBG else None
    pjp_d = dint("pjp_d", (T_PRE, E), F32) if "pjp_d" in DBG else None
    pjr_d = dint("pjr_d", (T_PRE, E), F32) if "pjr_d" in DBG else None
    ksb_d = dint("ksb_d", (128, E)) if "ksb_d" in DBG else None
    kvs_d = dint("kvs_d", (128, 2 * E)) if "kvs_d" in DBG else None
    rct_d = dint("rct_d", (128, E), F32) if "rct_d" in DBG else None
    r_dbg = None

    kdec = sb("kdec_s", (128, E), BF16)
    KV32 = sb("KV32", (128, E))
    KVbf = sb("KVbf", (128, E), BF16)
    kvstr = [sb(f"kvstr{i}", (128, 2 * E), BF16) for i in range(2)]
    g_fm = sb("g_fm", (128, NF, 1024), BF16)
    b2c = sb("b2c_s", (128, NF)); rmsc = sb("rmsc_s", (128, NF))
    b1c = sb("b1c_s", (128, DFF // 128))
    ones128 = sb("ones128", (128, 128), BF16)
    ones128f = sb("ones128f", (128, 128))
    epsln = sb("epsln", (128, 1)); epsrms = sb("epsrms", (128, 1))
    stat = sb("stat", (128, 16))
    part1 = sb("part1", (128, 64)); part2 = sb("part2", (128, 64))
    pred = sb("pred", (128, 2)); sc12 = sb("sc12", (128, 2)); inv12 = sb("inv12", (128, 2))

    base = (nc.sbuf_base + 31) & ~31
    SB_TOP = 229344
    R1, R2, R3 = base, base + 32 * KB, base + 96 * KB
    need = R3 + 40 * KB
    assert need <= SB_TOP, f"scratch overflow: need {need}, top {SB_TOP}"

    def at(name, off, shape, dtype=F32):
        return nc.alloc_sbuf_tensor_at(name, list(shape), dtype, offset=off)

    xn_fm = at("xn_fm", R1, (128, NF, 1024), BF16)
    Yn_all = at("Yn_all", R1, (128, NF, 1024), BF16)
    xbuf = [at(f"xbuf{i}", R2 + i * 8 * KB, (128, E)) for i in range(2)]
    xnbf = [at(f"xnbf{i}", R2 + 16 * KB + i * 4 * KB, (128, E), BF16) for i in range(2)]
    junk = at("junk", R2 + 24 * KB, (128, E))
    wrs = at("wrs", R2, (128, NF, E), BF16)
    g_addr = nc.lookup_mloc("g_fm").addr
    qw32 = [at(f"qw32{i}", g_addr + i * 8 * KB, (128, E)) for i in range(2)]
    junkq = at("junkq", g_addr + 16 * KB, (128, E))
    qout = [at(f"qout{i}", g_addr + 24 * KB + i * 4 * KB, (128, E), BF16) for i in range(2)]
    qmsk = at("qmsk", g_addr + 16 * KB, (128, 2 * E), BF16)
    g32 = at("g32", R2, (128, NF, 1024))
    k4 = at("k4", R2, (128, NF, 512), BF16)
    q4 = at("q4", R2 + 16 * KB, (128, NF, 512), BF16)
    wo_sb = at("wo_sb", R2, (128, NF, E), BF16)
    xnp_g = at("xnp_g", R3, (128, NF, 256), BF16)
    pj32 = [at(f"pj32{i}", R3 + 8 * KB + i * 8 * KB, (128, E)) for i in range(2)]
    trigc = [at(f"trigc{i}", R3 + 24 * KB + i * 2 * KB, (128, 512)) for i in range(2)]
    trigs = [at(f"trigs{i}", R3 + 28 * KB + i * 2 * KB, (128, 512)) for i in range(2)]
    rotmp = at("rotmp", R3 + 32 * KB, (128, 1024))
    h_sp = at("h_sp", R3, (128, 4, 1024), BF16)
    wtile_sp = at("wtile_sp", R3 + 8 * KB, (128, 16, 512), BF16)
    w2T = at("w2T", R3 + 24 * KB, (128, 4, 2048), BF16)
    Y_sb = at("Y_sb", R3, (128, NF, 128), BF16)
    Ysq = at("Ysq", R3 + 4 * KB, (128, NF, 128), BF16)
    qtile = at("qtile", R3 + 8 * KB, (128, 128), BF16)
    P_bf = at("P_bf", R3 + 8 * KB + 256, (128, 128), BF16)
    invb_bf = at("invb_bf", R3 + 8 * KB + 512, (128, 128), BF16)
    invb = at("invb", R3 + 9 * KB, (128, 128))
    rtmp = at("rtmp", R3 + 9 * KB + 512, (128, 128))
    dmaskT = at("dmaskT_s", R3 + 10 * KB, (128, E), BF16)
    qdecF = at("qdecF_s", R3 + 14 * KB, (128, E), BF16)
    sgout = at("sgout", R3, (128, 128))
    ident = at("ident", R3 + KB, (128, 128))
    iota_p = at("iota_p", R3 + 2 * KB, (128, 128))
    iota_f = at("iota_f", R3 + 3 * KB, (128, 128))
    sgld = at("sgld", R3 + 4 * KB, (128, E))
    xb2 = at("xb2", R3 + 12 * KB, (128, E))

    ps = ctx.enter_context(nc.psum_tensor("ps", [128, 4096], F32))

    def bank(i, w=512):
        return ps[:, i * 512:i * 512 + w]

    p = Prog(nc, ctx)

    r_tbl = p.res("tbl")
    r_tblg = p.res("tblg")
    r_x = [p.res("x0"), p.res("x1")]
    r_trig = [p.res("tg0"), p.res("tg1")]
    r_qw = [p.res("qw0"), p.res("qw1")]
    r_kvld = [p.res("kv0"), p.res("kv1")]
    r_wt = [p.res("wt0"), p.res("wt1")]
    r_w2s = [p.res("w2s0")]
    r_wrs = p.res("wrsl")
    r_xnp = p.res("xnpl")
    r_xnfm = p.res("xnfm")
    r_kq = p.res("kq")
    r_wo = p.res("wol")
    r_stbl = p.res("stbl")
    r_fin = p.res("fin")
    w_xno = [p.res("wxno0"), p.res("wxno1")]
    w_xnp = [p.res("wxnp0"), p.res("wxnp1")]
    w_kv = [p.res("wkv0"), p.res("wkv1")]
    w_kk = [p.res("wkk0"), p.res("wkk1")]
    w_qw = [p.res("wqw0"), p.res("wqw1")]
    w_sg = p.res("wsg")
    w_out = p.res("wout")
    r_pjd = p.res("pjd") if (pjp_d is not None or pjr_d is not None) else None
    r_kvs = p.res("kvs") if kvs_d is not None else None

    p.dma(GP, kdec[:, :], kdec_d, r_tblg)
    p.dma(SY, b2c[:, :], b2c_d, r_tbl)
    p.dma(SY, rmsc[:, :], rmsc_d, r_tbl)
    p.dma(SY, b1c[:, :], b1c_d, r_tbl)
    t_tbl = p.full(r_tbl)
    t_kdec = p.full(r_tblg)
    if ksb_d is not None:
        r_ksb = p.res("ksb")
        p.wait(SY, t_kdec)
        p.dma(SY, ksb_d, kdec[:, :], r_ksb)

    def init_fn(v):
        v.memset(KV32[:, :], 0.0)
        v.memset(ones128[:, :], 1.0)
        v.memset(ones128f[:, :], 1.0)
        v.memset(epsln[:, 0:1], LN_EPS)
        return v.memset(epsrms[:, 0:1], RMS_EPS)
    p.op(VE, init_fn)

    B_x = [Buf(), Buf()]
    B_xnbf = [Buf(), Buf()]
    B_trig = [Buf(), Buf()]
    B_pj = [Buf(), Buf()]
    B_kv = [Buf(), Buf()]
    B_qw = [Buf(), Buf()]
    B_qo = [Buf(), Buf()]
    B_wt = [Buf(), Buf()]
    B_stat = [Buf(), Buf()]

    t_sent = None
    if "sent" in DBG:
        r_sent = p.res("sent")
        tms = p.op(VE, lambda v: v.memset(kvstr[0][:, 0:E], 7.0))
        p.wait(SY, tms)
        for j in range(NPC):
            p.dma(SY, kdp_b[j * 128:(j + 1) * 128, :], kvstr[0][:, 0:E], r_sent)
        t_sent = p.full(r_sent)
        B_kv[0].read(t_sent)
        p.waits(GP, [t_sent])

    w1_tiles = [w1_d[i * 128:(i + 1) * 128, :] for i in range(64)]
    w2_tiles = [w2_d[(i // 4) * 128:(i // 4 + 1) * 128, (i % 4) * 2048:(i % 4 + 1) * 2048]
                for i in range(64)]
    w1q_tiles = [w1q_b[i * 128:(i + 1) * 128, :] for i in range(64)]
    w2q_tiles = [w2q_b[(i // 4) * 128:(i // 4 + 1) * 128, (i % 4) * 2048:(i % 4 + 1) * 2048]
                 for i in range(64)]
    qred_jobs = [(t, part1, i) for i, t in enumerate(w1_tiles)] + \
                [(t, part2, i) for i, t in enumerate(w2_tiles)]
    qred_pos = [0]

    def quant_reduce_step(nsteps):
        for _ in range(nsteps):
            i = qred_pos[0]
            if i >= len(qred_jobs):
                return
            qred_pos[0] = i + 1
            tile, part, col = qred_jobs[i]
            slot = i % 2
            bq = B_qw[slot]
            bq.before_write(p, SY)
            tl = p.dma(SY, qw32[slot][:, :], tile, r_qw[slot])
            bq.wrote(tl)
            if i % 2 == 0:
                p.wait(VE, tl)
                tr = p.op(VE, lambda v, slot=slot, part=part, col=col:
                          v.scalar_tensor_tensor(qw32[slot][:, :], qw32[slot][:, :], -1.0,
                                                 qw32[slot][:, :], ALU.mult, ALU.max,
                                                 accum_out=part[:, col:col + 1]))
            else:
                p.wait(SC, tl)
                tr = p.op(SC, lambda s, slot=slot, part=part, col=col:
                          s.activation(qw32[slot][:, :], qw32[slot][:, :], AF.Abs,
                                       accum_out=part[:, col:col + 1]))
            bq.read(tr)

    w2_order = [r * 4 + cb for cb in range(4) for r in range(16)]
    qwr_jobs = list(zip(w1_tiles, w1q_tiles, [0] * 64)) + \
               [(w2_tiles[i], w2q_tiles[i], 1) for i in w2_order]
    qwr_pos = [0]
    qwr_tok = [None] * len(qwr_jobs)
    tq0_box = [None]
    last_tqo = [None]

    def quant_write_step(nsteps):
        tq0 = tq0_box[0]
        if tq0 is None:
            return
        for _ in range(nsteps):
            i = qwr_pos[0]
            if i >= len(qwr_jobs):
                return
            qwr_pos[0] = i + 1
            tile, dst, which = qwr_jobs[i]
            inv_ap = inv12[:, which:which + 1]
            wres = w_qw[i % 2]
            slot = i % 2
            bq = B_qw[slot]
            bq.before_write(p, SY)
            tl = p.dma(SY, qw32[slot][:, :], tile, r_qw[slot])
            bq.wrote(tl)
            p.waits(VE, [tl, tq0])
            p.opq(VE, lambda v, slot=slot, inv_ap=inv_ap:
                  v.tensor_scalar(qmsk[:, 0:E], qw32[slot][:, :], inv_ap, 0.5,
                                  ALU.mult, ALU.is_ge))
            tm2 = p.op(VE, lambda v, slot=slot, inv_ap=inv_ap:
                       v.tensor_scalar(qmsk[:, E:2 * E], qw32[slot][:, :], inv_ap, -0.5,
                                       ALU.mult, ALU.is_le))
            bq.read(tm2)
            bo = B_qo[slot]
            bo.before_write(p, VE)
            tqo = p.op(VE, lambda v, slot=slot: v.tensor_tensor(qout[slot][:, :], qmsk[:, 0:E],
                                                                qmsk[:, E:2 * E], ALU.subtract))
            bo.wrote(tqo)
            last_tqo[0] = tqo
            p.wait(GP, tqo)
            tw = p.dma(GP, dst, qout[slot][:, :], wres)
            qwr_tok[i] = tw
            bo.read(tw)

    def emit_ln(src, dst_b, w_res, idx, slot):
        bx = B_x[slot]
        bx.before_write(p, SY)
        tl = p.dma(SY, xbuf[slot][:, :], src[idx * 128:(idx + 1) * 128, :], r_x[slot])
        bx.wrote(tl)
        x_ap = xbuf[slot][:, :]
        st8 = 8 * slot
        p.wait(VE, tl)
        B_stat[slot].before_write(p, VE)
        p.opq(VE, lambda v: v.tensor_scalar(junk[:, :], x_ap, 1.0, 0.0, ALU.mult, ALU.add,
                                            accum_out=stat[:, st8 + 0:st8 + 1]))
        p.opq(VE, lambda v: v.scalar_tensor_tensor(junk[:, :], x_ap, 1.0, x_ap,
                                                   ALU.mult, ALU.mult, accum_out=stat[:, st8 + 1:st8 + 2]))
        p.opq(VE, lambda v: v.tensor_scalar(stat[:, st8 + 2:st8 + 3], stat[:, st8 + 0:st8 + 1], 1.0 / E, 0.0, ALU.mult, ALU.add))
        p.opq(VE, lambda v: v.scalar_tensor_tensor(stat[:, st8 + 3:st8 + 4], stat[:, st8 + 2:st8 + 3], -1.0, stat[:, st8 + 2:st8 + 3],
                                                   ALU.mult, ALU.mult))
        tv = p.op(VE, lambda v: v.scalar_tensor_tensor(stat[:, st8 + 4:st8 + 5], stat[:, st8 + 1:st8 + 2], 1.0 / E,
                                                       stat[:, st8 + 3:st8 + 4], ALU.mult, ALU.add))
        p.wait(SC, tv)
        tsq = p.op(SC, lambda s: s.activation(stat[:, st8 + 5:st8 + 6], stat[:, st8 + 4:st8 + 5], AF.Sqrt,
                                              bias=epsln[:, 0:1], scale=1.0))
        p.wait(VE, tsq)
        p.opq(VE, lambda v: v.reciprocal(stat[:, st8 + 6:st8 + 7], stat[:, st8 + 5:st8 + 6]))
        tnm = p.op(VE, lambda v: v.scalar_tensor_tensor(stat[:, st8 + 7:st8 + 8], stat[:, st8 + 2:st8 + 3], -1.0,
                                                        stat[:, st8 + 6:st8 + 7], ALU.mult, ALU.mult))
        bn = B_xnbf[slot]
        bn.before_write(p, SC)
        p.wait(SC, tnm)
        txn = p.op(SC, lambda s: s.activation(xnbf[slot][:, :], x_ap, AF.Identity,
                                              bias=stat[:, st8 + 7:st8 + 8], scale=stat[:, st8 + 6:st8 + 7]))
        B_stat[slot].wrote(txn)
        bx.read(txn)
        bn.wrote(txn)
        p.wait(GP, txn)
        tw = p.dma(GP, dst_b[idx * 128:(idx + 1) * 128, :], xnbf[slot][:, :], w_res)
        bn.read(tw)
        return tw

    for i in range(NCH):
        emit_ln(xo, xn_b, w_xno[i % 2], i, i % 2)
        quant_reduce_step(1)
    for i in range(NPC):
        emit_ln(xp, xnp_b, w_xnp[i % 2], i, i % 2)
        quant_reduce_step(2)
    t_xno_all = [p.full(w_xno[0]), p.full(w_xno[1])]
    t_xnp_all = [p.full(w_xnp[0]), p.full(w_xnp[1])]

    p.waits(SC, t_xno_all)
    for f in range(NF):
        p.dmat(SC, xn_fm[:, f, :], xn_b[0:T_OWN, f * 128:(f + 1) * 128], r_xnfm)
    t_xnfm_all = p.full(r_xnfm)

    B_wrs = Buf()

    def load_W(w_dram):
        B_wrs.before_write(p, GP)
        for f in range(NF):
            p.dma(GP, wrs[:, f, :], w_dram[f * 128:(f + 1) * 128, :], r_wrs)
        tok = p.full(r_wrs)
        B_wrs.wrote(tok)
        return tok

    pbank = {}

    def proj_chunk(xn_t, banks, out_write, wait_toks):
        p.waits(TE, wait_toks)
        for bk_i in banks:
            if pbank.get(bk_i) is not None:
                p.wait(TE, pbank[bk_i])
        tmm = None
        for f in range(NF):
            A = xn_t(f)
            for n in range(4):
                bk = bank(banks[n])
                B = wrs[:, f, n * 512:(n + 1) * 512]
                st = (f == 0); sp = (f == NF - 1)
                if sp and n == 3:
                    tmm = p.op(TE, lambda t, A=A, B=B, bk=bk, st=st:
                               t.matmul(bk, A, B, start=st, stop=True))
                else:
                    p.opq(TE, lambda t, A=A, B=B, bk=bk, st=st, sp=sp:
                          t.matmul(bk, A, B, start=st, stop=sp))
        p.wait(SC, tmm)
        etoks = []
        for n in range(4):
            te = out_write(n, bank(banks[n]))
            pbank[banks[n]] = te
            etoks.append(te)
        return etoks

    def head3d(t, base_c, width=32):
        return t[:, :].rearrange("p (h d) -> p h d", d=128)[:, :, base_c:base_c + width]

    def rep3d(t, width=32):
        return t[:, :].rearrange("p (h d) -> p h d", d=32)[:, :, 0:width]

    def emit_rotary(dst32, cosb, sinb):
        x1 = head3d(dst32, 0); x2 = head3d(dst32, 64)
        tm1 = rotmp[:, 0:512].rearrange("p (h d) -> p h d", d=32)
        tm2 = rotmp[:, 512:1024].rearrange("p (h d) -> p h d", d=32)
        cs = rep3d(cosb); sn = rep3d(sinb)
        p.opq(VE, lambda v: v.tensor_tensor(tm1, x2, sn, ALU.mult))
        p.opq(VE, lambda v: v.tensor_tensor(tm2, x1, sn, ALU.mult))
        p.opq(VE, lambda v: v.tensor_tensor(x2, x2, cs, ALU.mult))
        p.opq(VE, lambda v: v.tensor_tensor(x2, x2, tm2, ALU.subtract))
        p.opq(VE, lambda v: v.tensor_tensor(x1, x1, cs, ALU.mult))
        return p.op(VE, lambda v: v.tensor_tensor(x1, x1, tm1, ALU.add))

    def load_trig(slot, cos_src, sin_src, j):
        bt = B_trig[slot]
        bt.before_write(p, SY)
        p.dma(SY, trigc[slot][:, :], cos_src[j * 128:(j + 1) * 128, :], r_trig[slot])
        p.dma(SY, trigs[slot][:, :], sin_src[j * 128:(j + 1) * 128, :], r_trig[slot])
        t2 = p.full(r_trig[slot])
        bt.wrote(t2)
        return t2

    def evac_pj(slot):
        def w(n, bk):
            if n == 0:
                B_pj[slot].before_write(p, SC)
            return p.op(SC, lambda s, bk=bk, n=n: s.copy(pj32[slot][:, n * 512:(n + 1) * 512], bk))
        return w

    p.waits(GP, t_xno_all + t_xnp_all)
    t_wk = load_W(wk_d)
    wrs_readers = []

    def kproj(xn_t, slot, wait_toks, trig_tok, bounce_kd, bounce_k, dbg_j=None):
        et = proj_chunk(xn_t, (0, 1, 2, 3), evac_pj(slot), wait_toks)
        B_pj[slot].wrote(et[-1])
        wrs_readers.append(et[-1])
        if dbg_j is not None and pjp_d is not None:
            p.waits(GP, et)
            tdp = p.dma(GP, pjp_d[dbg_j * 128:(dbg_j + 1) * 128, :], pj32[slot][:, :], r_pjd)
            B_pj[slot].read(tdp)
            p.waits(VE, [tdp])
        p.waits(VE, et + [trig_tok, t_kdec])
        trot = emit_rotary(pj32[slot], trigc[slot], trigs[slot])
        B_trig[slot].read(trot)
        if dbg_j is not None and pjr_d is not None:
            p.wait(GP, trot)
            tdr = p.dma(GP, pjr_d[dbg_j * 128:(dbg_j + 1) * 128, :], pj32[slot][:, :], r_pjd)
            B_pj[slot].read(tdr)
        bkv = B_kv[slot]
        bkv.before_write(p, VE)
        tkr = p.op(VE, lambda v, slot=slot: v.tensor_copy(kvstr[slot][:, E:2 * E], pj32[slot][:, :]))
        tkd = p.op(VE, lambda v, slot=slot: v.tensor_tensor(kvstr[slot][:, 0:E],
                                                            kvstr[slot][:, E:2 * E],
                                                            kdec[:, :], ALU.mult))
        toks = [tkr, tkd]
        B_pj[slot].read(toks[-1])
        bkv.wrote(toks[-1])
        if kvs_d is not None and dbg_j == 22:
            p.waits(SY, toks)
            tks = p.dma(SY, kvs_d, kvstr[slot][:, :], r_kvs)
            bkv.read(tks)
        p.waits(SY, toks)
        p.dma(SY, bounce_kd, kvstr[slot][:, 0:E], w_kk[slot])
        if bounce_k is not None:
            p.dma(SY, bounce_k, kvstr[slot][:, E:2 * E], w_kk[slot])
        bkv.read(p.full(w_kk[slot]))

    for j in range(NCH):
        slot = j % 2
        tt = load_trig(slot, cos_o, sin_o, j)
        kproj(lambda f, j=j: xn_fm[:, f, j * 128:(j + 1) * 128], slot,
              [t_wk, t_xnfm_all], tt, kd_b[j * 128:(j + 1) * 128, :],
              k_b[j * 128:(j + 1) * 128, :])
        quant_reduce_step(3)

    B_xnp = Buf()
    for g in range(T_PRE // 256):
        B_xnp.before_write(p, SC)
        p.waits(SC, t_xnp_all)
        for f in range(NF):
            p.dmat(SC, xnp_g[:, f, :], xnp_b[g * 256:(g + 1) * 256, f * 128:(f + 1) * 128], r_xnp)
        tg = p.full(r_xnp)
        B_xnp.wrote(tg)
        for jj in range(2):
            j = g * 2 + jj
            slot = jj % 2
            tt = load_trig(slot, cos_p, sin_p, j)
            kproj(lambda f, jj=jj: xnp_g[:, f, jj * 128:(jj + 1) * 128], slot,
                  [t_wk, tg], tt, kdp_b[j * 128:(j + 1) * 128, :], None, dbg_j=j)
            B_xnp.read(wrs_readers[-1])
            quant_reduce_step(2)


    quant_reduce_step(len(qred_jobs))
    p.waits(VE, [p.ctok(SC)])
    p.opq(VE, lambda v: v.tensor_reduce(pred[:, 0:1], part1[:, :], AX.X, ALU.add))
    tpr = p.op(VE, lambda v: v.tensor_reduce(pred[:, 1:2], part2[:, :], AX.X, ALU.add))
    p.waits(TE, [tpr])
    if pbank.get(2) is not None:
        p.wait(TE, pbank[2])
    tsum = p.op(TE, lambda t: t.matmul(ps[:, 1024:1026], ones128f[:, :], pred[:, 0:2],
                                       start=True, stop=True))
    p.wait(VE, tsum)
    NW = float(DFF * E)
    p.opq(VE, lambda v: v.tensor_scalar(sc12[:, 0:2], ps[:, 1024:1026], 1.0 / NW, 1e-8,
                                        ALU.mult, ALU.max))
    tq0_box[0] = p.op(VE, lambda v: v.reciprocal(inv12[:, 0:2], sc12[:, 0:2]))
    pbank[2] = tq0_box[0]

    p.waits(GP, wrs_readers)
    for t in wrs_readers:
        B_wrs.read(t)
    t_wv = load_W(wv_d)
    wrs_readers = []

    def evac_kv(slot):
        def w(n, bk):
            if n == 0:
                B_kv[slot].before_write(p, SC)
            return p.op(SC, lambda s, bk=bk, n=n, slot=slot:
                        s.copy(kvstr[slot][:, n * 512:(n + 1) * 512], bk))
        return w

    def vproj(xn_t, banks, slot, wait_toks, dst):
        et = proj_chunk(xn_t, banks, evac_kv(slot), wait_toks)
        wrs_readers.append(et[-1])
        B_kv[slot].wrote(et[-1])
        p.waits(GP, et)
        p.dma(GP, dst, kvstr[slot][:, 0:E], w_kv[slot])
        tw = p.full(w_kv[slot])
        B_kv[slot].read(tw)
        return tw

    for j in range(NCH):
        vproj(lambda f, j=j: xn_fm[:, f, j * 128:(j + 1) * 128],
              (0, 1, 2, 3), j % 2, [t_wv], v_b[j * 128:(j + 1) * 128, :])
        quant_write_step(2)
    for g in range(T_PRE // 256):
        B_xnp.before_write(p, SC)
        for f in range(NF):
            p.dmat(SC, xnp_g[:, f, :], xnp_b[g * 256:(g + 1) * 256, f * 128:(f + 1) * 128], r_xnp)
        tg = p.full(r_xnp)
        B_xnp.wrote(tg)
        for jj in range(2):
            j = g * 2 + jj
            tw = vproj(lambda f, jj=jj: xnp_g[:, f, jj * 128:(jj + 1) * 128],
                       (0, 1, 2, 3), jj % 2, [t_wv, tg], vp_b[j * 128:(j + 1) * 128, :])
            B_xnp.read(tw)
            quant_write_step(2)

    p.waits(GP, wrs_readers)
    for t in wrs_readers:
        B_wrs.read(t)
    t_wq = load_W(wq_d)
    wrs_readers = []
    for j in range(NCH):
        slot = j % 2
        tt = load_trig(slot, cos_o, sin_o, j)
        et = proj_chunk(lambda f, j=j: xn_fm[:, f, j * 128:(j + 1) * 128],
                        (0, 1, 2, 3), evac_pj(slot), [t_wq])
        B_pj[slot].wrote(et[-1])
        wrs_readers.append(et[-1])
        p.waits(VE, et + [tt])
        trot = emit_rotary(pj32[slot], trigc[slot], trigs[slot])
        B_trig[slot].read(trot)
        bkv = B_kv[slot]
        bkv.before_write(p, VE)
        tqr = p.op(VE, lambda v, slot=slot: v.tensor_copy(kvstr[slot][:, 0:E], pj32[slot][:, :]))
        B_pj[slot].read(tqr)
        bkv.wrote(tqr)
        p.wait(GP, tqr)
        p.dma(GP, q_b[j * 128:(j + 1) * 128, :], kvstr[slot][:, 0:E], w_kv[slot])
        bkv.read(p.full(w_kv[slot]))
        quant_write_step(2)

    t_bounce_all = [p.full(w_kv[0]), p.full(w_kv[1]), p.full(w_kk[0]), p.full(w_kk[1])]
    t_kv_upd = None
    t_kv_par = [None, None]
    for b_ in range(4):
        if pbank.get(b_) is not None:
            p.wait(TE, pbank[b_])
            pbank[b_] = None
    for l in range(NPC):
        slot = l % 2
        base = 2048 * (l % 2)
        bkv = B_kv[slot]
        bkv.before_write(p, SY)
        p.waits(SY, t_bounce_all)
        p.dma(SY, kvstr[slot][:, 0:E], kdp_b[l * 128:(l + 1) * 128, :], r_kvld[slot])
        p.dma(SY, kvstr[slot][:, E:2 * E], vp_b[l * 128:(l + 1) * 128, :], r_kvld[slot])
        tv = p.full(r_kvld[slot])
        bkv.wrote(tv)
        p.waits(TE, [tv])
        if t_kv_par[l % 2] is not None:
            p.wait(TE, t_kv_par[l % 2])
        tdel = None
        for h in range(H):
            A = kvstr[slot][:, h * 128:(h + 1) * 128]
            B = kvstr[slot][:, E + h * 128:E + (h + 1) * 128]
            bk = ps[:, base + h * 128:base + (h + 1) * 128]
            if h == H - 1:
                tdel = p.op(TE, lambda t, A=A, B=B, bk=bk: t.matmul(bk, A, B, start=True, stop=True))
            else:
                p.opq(TE, lambda t, A=A, B=B, bk=bk: t.matmul(bk, A, B, start=True, stop=True))
        bkv.read(tdel)
        p.waits(VE, [tdel, t_tbl])
        for hx in range(H):
            p.opq(VE, lambda v, hx=hx: v.tensor_scalar(
                KV32[:, hx * 128:(hx + 1) * 128], KV32[:, hx * 128:(hx + 1) * 128],
                _R128[hx], 0.0, ALU.mult, ALU.add))
        for bq_ in range(4):
            t_kv_upd = p.op(VE, lambda v, b=bq_, base=base: v.tensor_tensor(
                KV32[:, b * 512:(b + 1) * 512], KV32[:, b * 512:(b + 1) * 512],
                ps[:, base + b * 512:base + (b + 1) * 512], ALU.add))
        t_kv_par[l % 2] = t_kv_upd
        quant_write_step(1)

    gb = [None] * 8
    p.wait(TE, t_kv_upd)
    p.waits(SC, [p.ctok(VE)])
    B_wts = Buf()
    B_w2s = Buf()
    t_gadd = None
    g_last = None
    for sp_i in range(16):
        B_wts.before_write(p, SC)
        p.waits(SC, [qwr_tok[4 * sp_i + 2], qwr_tok[4 * sp_i + 3]])
        for f in range(NF):
            p.dmat(SC, wtile_sp[:, f, :],
                   w1q_b[sp_i * 512:(sp_i + 1) * 512, f * 128:(f + 1) * 128], r_wt[0])
        t_wts = p.full(r_wt[0])
        B_wts.wrote(t_wts)
        B_w2s.before_write(p, SC)
        jlast = 64 + 16 * (sp_i // 4) + 15
        p.waits(SC, [qwr_tok[jlast - 1], qwr_tok[jlast]])
        for fo4 in range(4):
            for dp in range(2):
                p.dmat(SC, w2T[:, fo4, dp * 1024:(dp + 1) * 1024],
                       w2q_b[dp * 1024:(dp + 1) * 1024,
                             (sp_i * 4 + fo4) * 128:(sp_i * 4 + fo4 + 1) * 128], r_w2s[0])
        t_w2 = p.full(r_w2s[0])
        B_w2s.wrote(t_w2)
        p.waits(TE, [t_wts, t_xnfm_all])
        for b in range(8):
            if gb[b] is not None:
                p.wait(TE, gb[b])
            if sp_i == 0 and pbank.get(b) is not None:
                p.wait(TE, pbank[b])
                pbank[b] = None
        htok = None
        for f in range(NF):
            for fo4 in range(4):
                A = wtile_sp[:, f, fo4 * 128:(fo4 + 1) * 128]
                for th in range(2):
                    bk = bank(fo4 * 2 + th)
                    B = xn_fm[:, f, th * 512:(th + 1) * 512]
                    st = (f == 0); spx = (f == NF - 1)
                    if spx and fo4 == 3 and th == 1:
                        htok = p.op(TE, lambda t, A=A, B=B, bk=bk, st=st, spx=spx:
                                    t.matmul(bk, A, B, start=st, stop=spx))
                    else:
                        p.opq(TE, lambda t, A=A, B=B, bk=bk, st=st, spx=spx:
                              t.matmul(bk, A, B, start=st, stop=spx))
        B_wts.read(htok)
        p.waits(SC, [htok, t_tbl])
        if g_last is not None:
            p.wait(SC, g_last)
        ev_last = None
        for fo4 in range(4):
            for th in range(2):
                ev_last = p.op(SC, lambda s, fo4=fo4, th=th, sp_i=sp_i:
                               s.activation(h_sp[:, fo4, th * 512:(th + 1) * 512],
                                            bank(fo4 * 2 + th), AF.Gelu,
                                            bias=b1c[:, sp_i * 4 + fo4:sp_i * 4 + fo4 + 1],
                                            scale=1.0))
        for m in range(32):
            db, th, b = m // 2, m % 2, m % 8
            bk = bank(b)
            p.waits(TE, [ev_last, t_w2])
            if gb[b] is not None:
                p.wait(TE, gb[b])
            tmm = None
            for fo4 in range(4):
                A = w2T[:, fo4, db * 128:(db + 1) * 128]
                B = h_sp[:, fo4, th * 512:(th + 1) * 512]
                st = (fo4 == 0)
                if fo4 == 3:
                    tmm = p.op(TE, lambda t, A=A, B=B, bk=bk, st=st:
                               t.matmul(bk, A, B, start=st, stop=True))
                else:
                    p.opq(TE, lambda t, A=A, B=B, bk=bk, st=st:
                          t.matmul(bk, A, B, start=st, stop=False))
            p.wait(VE, tmm)
            if sp_i == 0:
                t_gadd = p.op(VE, lambda v, db=db, th=th, bk=bk:
                              v.tensor_copy(g32[:, db, th * 512:(th + 1) * 512], bk))
            else:
                t_gadd = p.op(VE, lambda v, db=db, th=th, bk=bk:
                              v.tensor_tensor(g32[:, db, th * 512:(th + 1) * 512],
                                              g32[:, db, th * 512:(th + 1) * 512], bk, ALU.add))
            gb[b] = t_gadd
        g_last = tmm
        B_w2s.read(g_last)
        quant_write_step(2)
    t_q_done = [p.full(w_qw[0]), p.full(w_qw[1]), p.full(r_qw[0]), p.full(r_qw[1]), last_tqo[0]]

    silu_tok = [None] * 32
    p.waits(SC, [t_gadd, t_tbl] + t_q_done)
    for d in range(NF):
        for half in range(2):
            silu_tok[d * 2 + half] = p.op(SC, lambda s, d=d, half=half:
                                          s.activation(g_fm[:, d, half * 512:(half + 1) * 512],
                                                       g32[:, d, half * 512:(half + 1) * 512],
                                                       AF.Silu, bias=b2c[:, d:d + 1], scale=1.0))
    last_mm_half = silu_tok[31]
    gbank = gb

    ffn_gate = [t for t in silu_tok if t is not None]
    p.waits(GP, ffn_gate)
    p.waits(TE, [t for t in gbank if t is not None])
    p.dma(GP, dmaskT[:, :], dmaskT_d, r_stbl)
    p.dma(GP, qdecF[:, :], qdecF_d, r_stbl)
    t_stbl = p.full(r_stbl)
    p.wait(VE, t_kv_upd)
    t_kvbf = p.op(VE, lambda v: v.tensor_copy(KVbf[:, :], KV32[:, :]))
    if kvpre_d is not None:
        r_dbg = p.res("dbg")
        p.wait(GP, t_kvbf)
        p.dma(GP, kvpre_d, KV32[:, :], r_dbg)

    B_kq = Buf()
    last_yev = None
    last_tsr = None
    last_yn = None
    t_scoreP = None
    for grp in range(2):
        B_kq.before_write(p, SY)
        B_kq.before_write(p, SC)
        p.waits(SY, ffn_gate + t_bounce_all)
        p.waits(SC, ffn_gate + t_bounce_all)
        for f in range(NF):
            p.dmat(SC, k4[:, f, :], k_b[grp * 512:(grp + 1) * 512, f * 128:(f + 1) * 128], r_kq)
            p.dmat(SC, q4[:, f, :], q_b[grp * 512:(grp + 1) * 512, f * 128:(f + 1) * 128], r_kq)
        t_kq_all = p.full(r_kq)
        B_kq.wrote(t_kq_all)
        if kq_d is not None and grp == 0:
            if r_dbg is None:
                r_dbg = p.res("dbg")
            p.wait(GP, t_kq_all)
            p.dma(GP, kq_d[0], k4[:, :, :].rearrange("p a b -> p (a b)"), r_dbg)
            p.dma(GP, kq_d[1], q4[:, :, :].rearrange("p a b -> p (a b)"), r_dbg)
        for jj in range(4):
            j = grp * 4 + jj
            slot = j % 2
            bkv = B_kv[slot]
            bkv.before_write(p, SY)
            p.dma(SY, kvstr[slot][:, 0:E], kd_b[j * 128:(j + 1) * 128, :], r_kvld[slot])
            p.dma(SY, kvstr[slot][:, E:2 * E], v_b[j * 128:(j + 1) * 128, :], r_kvld[slot])
            tv = p.full(r_kvld[slot])
            bkv.wrote(tv)
            t_yev = []
            tdel = None
            for h in range(H):
                bkS = ps[:, 0:128]
                p.waits(TE, [t_kq_all, tv, t_kvbf])
                if t_scoreP is not None:
                    p.wait(TE, t_scoreP)
                tS = p.op(TE, lambda t, h=h, jj=jj, bkS=bkS:
                          t.matmul(bkS, k4[:, h, jj * 128:(jj + 1) * 128],
                                   q4[:, h, jj * 128:(jj + 1) * 128], start=True, stop=True))
                p.waits(VE, [tS, t_stbl])
                t_scoreP = p.op(VE, lambda v, h=h, bkS=bkS:
                                v.tensor_tensor(P_bf[:, :], bkS, dmaskT[:, h * 128:(h + 1) * 128],
                                                ALU.mult))
                tqd = p.op(VE, lambda v, h=h, jj=jj:
                           v.tensor_tensor(qtile[:, :], q4[:, h, jj * 128:(jj + 1) * 128],
                                           qdecF[:, h * 128:(h + 1) * 128], ALU.mult))
                bkY = ps[:, 512:640]
                p.waits(TE, [t_scoreP, tqd])
                if last_yev is not None:
                    p.wait(TE, last_yev)
                p.opq(TE, lambda t, h=h, slot=slot, bkY=bkY:
                      t.matmul(bkY, kvstr[slot][:, E + h * 128:E + (h + 1) * 128], P_bf[:, :],
                               start=True, stop=False))
                tY = p.op(TE, lambda t, h=h, bkY=bkY:
                          t.matmul(bkY, KVbf[:, h * 128:(h + 1) * 128], qtile[:, :],
                                   start=False, stop=True))
                if last_yn is not None:
                    p.wait(SC, last_yn)
                p.wait(SC, tY)
                last_yev = p.op(SC, lambda s, h=h, bkY=bkY: s.copy(Y_sb[:, h, :], bkY))
                t_yev.append(last_yev)
                bkD = ps[:, 2048 + h * 128:2048 + (h + 1) * 128]
                if t_kv_upd is not None:
                    p.wait(TE, t_kv_upd)
                if h == H - 1:
                    tdel = p.op(TE, lambda t, h=h, slot=slot, bkD=bkD:
                                t.matmul(bkD, kvstr[slot][:, h * 128:(h + 1) * 128],
                                         kvstr[slot][:, E + h * 128:E + (h + 1) * 128],
                                         start=True, stop=True))
                else:
                    p.opq(TE, lambda t, h=h, slot=slot, bkD=bkD:
                          t.matmul(bkD, kvstr[slot][:, h * 128:(h + 1) * 128],
                                   kvstr[slot][:, E + h * 128:E + (h + 1) * 128],
                                   start=True, stop=True))
            bkv.read(tdel)
            B_kq.read(tdel)
            p.waits(VE, [tdel])
            for hx in range(H):
                p.opq(VE, lambda v, hx=hx: v.tensor_scalar(
                    KV32[:, hx * 128:(hx + 1) * 128], KV32[:, hx * 128:(hx + 1) * 128],
                    _R128[hx], 0.0, ALU.mult, ALU.add))
            for bq_ in range(4):
                t_kv_upd = p.op(VE, lambda v, b=bq_: v.tensor_tensor(
                    KV32[:, b * 512:(b + 1) * 512], KV32[:, b * 512:(b + 1) * 512],
                    ps[:, 2048 + b * 512:2048 + (b + 1) * 512], ALU.add))
            t_kvbf = p.op(VE, lambda v: v.tensor_copy(KVbf[:, :], KV32[:, :]))
            p.waits(SC, t_yev)
            if ysb_d is not None:
                if r_dbg is None:
                    r_dbg = p.res("dbg")
                p.waits(GP, t_yev)
                tyd = p.dma(GP, ysb_d[j * 128:(j + 1) * 128, :],
                            Y_sb[:, :, :].rearrange("p a b -> p (a b)"), r_dbg)
                p.wait(SC, tyd)
            tsq = p.op(SC, lambda s: s.activation(Ysq[:, :, :].rearrange("p a b -> p (a b)"),
                                                  Y_sb[:, :, :].rearrange("p a b -> p (a b)"),
                                                  AF.Square))
            bkR = ps[:, 1024:1152]
            p.waits(TE, [tsq])
            if last_tsr is not None:
                p.wait(TE, last_tsr)
            tss = None
            for e in range(NF):
                st = (e == 0); sp = (e == NF - 1)
                if sp:
                    tss = p.op(TE, lambda t, e=e, bkR=bkR, st=st:
                               t.matmul(bkR, ones128[:, :], Ysq[:, e, :], start=st, stop=True))
                else:
                    p.opq(TE, lambda t, e=e, bkR=bkR, st=st:
                          t.matmul(bkR, ones128[:, :], Ysq[:, e, :], start=st, stop=False))
            p.wait(SC, tss)
            last_tsr = p.op(SC, lambda s, bkR=bkR: s.activation(rtmp[:, :], bkR, AF.Sqrt,
                                                                bias=epsrms[:, 0:1], scale=1.0 / E))
            p.waits(VE, [last_tsr, t_tbl])
            trc = p.op(VE, lambda v: v.reciprocal(invb[:, :], rtmp[:, :]))
            p.op(VE, lambda v: v.tensor_copy(invb_bf[:, :], invb[:, :]))
            for e in range(NF):
                last_yn = p.op(VE, lambda v, e=e, j=j:
                               v.scalar_tensor_tensor(Yn_all[:, e, j * 128:(j + 1) * 128],
                                                      Y_sb[:, e, :], rmsc[:, e:e + 1],
                                                      invb_bf[:, :], ALU.mult, ALU.mult))

    if yn_d is not None:
        if r_dbg is None:
            r_dbg = p.res("dbg")
        p.waits(GP, [last_yn, last_yev])
        p.dma(GP, yn_d, Yn_all[:, :, :].rearrange("p a b -> p (a b)"), r_dbg)
    if rct_d is not None:
        r_rct = p.res("rct")
        p.wait(SY, last_yn)
        p.dma(SY, rct_d, rCt[:, :], r_rct)

    p.waits(GP, [last_yn, t_kv_upd])
    B_wo = Buf()
    B_wo.before_write(p, GP)
    for f in range(NF):
        p.dma(GP, wo_sb[:, f, :], wo_d[f * 128:(f + 1) * 128, :], r_wo)
    t_wo = p.full(r_wo)
    B_wo.wrote(t_wo)
    t_iota = p.op(GP, lambda g2: (g2.iota(iota_p[:, :], [[0, 128]], channel_multiplier=1,
                                          allow_small_or_imprecise_dtypes=True),
                                  g2.iota(iota_f[:, :], [[1, 128]], channel_multiplier=0,
                                          allow_small_or_imprecise_dtypes=True))[-1])
    p.waits(VE, [t_iota, last_yn])
    t_ident = p.op(VE, lambda v: v.tensor_tensor(ident[:, :], iota_p[:, :], iota_f[:, :], ALU.is_equal))

    B_sg = Buf()
    B_sgo = Buf()
    tT_last = {}
    for half in range(2):
        for foc in range(NF):
            bk_i = foc % 2
            bk = bank(bk_i)
            p.waits(TE, [t_wo, last_yn])
            if pbank.get(bk_i) is not None:
                p.wait(TE, pbank[bk_i])
                pbank[bk_i] = None
            tS = None
            for f in range(NF):
                A = wo_sb[:, f, foc * 128:(foc + 1) * 128]
                B = Yn_all[:, f, half * 512:(half + 1) * 512]
                st = (f == 0); sp = (f == NF - 1)
                if sp:
                    tS = p.op(TE, lambda t, A=A, B=B, bk=bk, st=st: t.matmul(bk, A, B, start=st, stop=True))
                else:
                    p.opq(TE, lambda t, A=A, B=B, bk=bk, st=st: t.matmul(bk, A, B, start=st, stop=False))
            B_sg.before_write(p, VE)
            p.wait(VE, tS)
            tg = p.op(VE, lambda v, bk=bk, foc=foc, half=half:
                      v.tensor_tensor(sgld[:, 0:512], bk, g_fm[:, foc, half * 512:(half + 1) * 512],
                                      ALU.mult))
            pbank[bk_i] = tg
            B_sg.wrote(tg)
            for cs in range(4):
                bkT_i = 4 + (cs % 2)
                bkT = bank(bkT_i, 128)
                p.waits(TE, [tg, t_ident])
                if tT_last.get(bkT_i) is not None:
                    p.wait(TE, tT_last[bkT_i])
                tT = p.op(TE, lambda t, cs=cs, bkT=bkT:
                          t.transpose(bkT, sgld[:, cs * 128:(cs + 1) * 128], ident[:, :]))
                B_sg.read(tT)
                B_sgo.before_write(p, SC)
                p.wait(SC, tT)
                tcp = p.op(SC, lambda s, bkT=bkT: s.copy(sgout[:, :], bkT))
                B_sgo.wrote(tcp)
                p.wait(GP, tcp)
                cch = half * 4 + cs
                p.dma(GP, sg_b[cch * 128:(cch + 1) * 128, foc * 128:(foc + 1) * 128],
                      sgout[:, :], w_sg)
                tws = p.full(w_sg)
                B_sgo.read(tws)
                tT_last[bkT_i] = tws
    t_sg_all = p.full(w_sg)

    B_sgld = Buf()
    for cch in range(NCH):
        p.waits(SY, [t_sg_all])
        B_sgld.before_write(p, SY)
        p.dma(SY, sgld[:, :], sg_b[cch * 128:(cch + 1) * 128, :], r_fin)
        p.dma(SY, xb2[:, :], xo[cch * 128:(cch + 1) * 128, :], r_fin)
        t_both = p.full(r_fin)
        B_sgld.wrote(t_both)
        p.waits(VE, [t_both])
        ta = p.op(VE, lambda v: v.tensor_tensor(sgld[:, :], sgld[:, :], xb2[:, :], ALU.add))
        B_sgld.read(ta)
        p.wait(GP, ta)
        tst = p.dma(GP, out_d[cch * 128:(cch + 1) * 128, :], sgld[:, :], w_out)
        B_sgld.read(tst)

    fin = w_out.cnt
    p.ops[GP].append(lambda g2, sem=w_out.sem, v=fin: g2.wait_ge(sem, v))

    block = ctx.enter_context(nc.Block())
    p.finalize(block)
    return nc


def kernel(L0_attn_x, L0_attn_Wq, L0_attn_Wk, L0_attn_Wv, L0_attn_Wo, L0_attn_rms_w,
           L0_ffn_w1, L0_ffn_b1, L0_ffn_w2, L0_ffn_b2):
    x = np.ascontiguousarray(np.asarray(L0_attn_x, dtype=np.float32))
    wq = np.ascontiguousarray(np.asarray(L0_attn_Wq, dtype=np.float32))
    wk = np.ascontiguousarray(np.asarray(L0_attn_Wk, dtype=np.float32))
    wv = np.ascontiguousarray(np.asarray(L0_attn_Wv, dtype=np.float32))
    wo = np.ascontiguousarray(np.asarray(L0_attn_Wo, dtype=np.float32))
    rms_w = np.asarray(L0_attn_rms_w, dtype=np.float32)
    w1 = np.ascontiguousarray(np.asarray(L0_ffn_w1, dtype=np.float32))
    b1 = np.asarray(L0_ffn_b1, dtype=np.float32)
    w2 = np.ascontiguousarray(np.asarray(L0_ffn_w2, dtype=np.float32))
    b2 = np.asarray(L0_ffn_b2, dtype=np.float32)

    qdecF, kdec, dmaskT, rCt, freq = _host_tables()
    b1c = np.ascontiguousarray(b1.reshape(DFF // 128, 128).T)
    b2c = np.ascontiguousarray(b2.reshape(NF, 128).T)
    rmsc = np.ascontiguousarray(rms_w.reshape(NF, 128).T)

    in_maps = []
    for core in range(8):
        b, s = core // 4, core % 4
        xo_h = np.ascontiguousarray(x[b, s * T_OWN:(s + 1) * T_OWN])
        pre_rows = s * T_OWN
        xp_h = np.zeros((T_PRE, E), dtype=np.float32)
        if pre_rows > 0:
            take = min(pre_rows, T_PRE)
            xp_h[T_PRE - take:] = x[b, pre_rows - take:pre_rows]
        t_own = s * T_OWN + np.arange(T_OWN)
        t_pre = s * T_OWN - T_PRE + np.arange(T_PRE)
        cos_o, sin_o = _trig(t_own, freq)
        cos_p, sin_p = _trig(t_pre, freq)
        in_maps.append({
            "xo": xo_h, "xp": xp_h,
            "wq": wq, "wk": wk, "wv": wv, "wo": wo,
            "w1": w1, "w2": w2,
            "b1c": b1c, "b2c": b2c, "rmsc": rmsc,
            "cos_o": cos_o, "sin_o": sin_o, "cos_p": cos_p, "sin_p": sin_p,
            "qdecF": qdecF, "kdec": kdec, "dmaskT": dmaskT, "rCt": rCt,
        })

    with ExitStack() as ctx:
        nc = build_program(ctx)
        res = run_bass_kernel_spmd(nc, in_maps, core_ids=list(range(8)))
    global LAST_RES
    LAST_RES = res

    out = np.empty((2, 4096, E), dtype=np.float32)
    for core in range(8):
        b, s = core // 4, core % 4
        out[b, s * T_OWN:(s + 1) * T_OWN] = res.results[core]["out"]
    return out


# version 14
# speedup vs baseline: 1.0190x; 1.0190x over previous
import math
import numpy as np
from contextlib import ExitStack

import concourse.bass as bass
import concourse.mybir as mybir
from concourse.bass_utils import run_bass_kernel_spmd

F32 = mybir.dt.float32
BF16 = mybir.dt.bfloat16
AF = mybir.ActivationFunctionType
ALU = mybir.AluOpType
AX = mybir.AxisListType

E = 2048
H = 16
D = 128
C = 128
T_OWN = 1024
T_PRE = 3072
NCH = T_OWN // C
NPC = T_PRE // C
DFF = 8192
NF = E // 128
RMS_EPS = 1e-5
LN_EPS = 1e-5
SY, SC, VE, TE, GP = "sync", "scalar", "vector", "tensor", "gpsimd"
KB = 1024
DBG = []
LAST_RES = None


_R128 = None


def _decay_r128():
    start = 2.0 ** (-(2.0 ** -(math.log2(H) - 3)))
    slopes = np.array([start * start ** i for i in range(H)], dtype=np.float64)
    r = np.exp(-slopes)
    return [float(np.float32(rv ** 128.0)) for rv in r]


def _host_tables():
    start = 2.0 ** (-(2.0 ** -(math.log2(H) - 3)))
    slopes = np.array([start * start ** i for i in range(H)], dtype=np.float64)
    r = np.exp(-slopes)
    c_idx = np.arange(C, dtype=np.float64)
    qdecF = np.zeros((128, E), dtype=np.float64)
    kdec = np.zeros((128, E), dtype=np.float64)
    dmaskT = np.zeros((128, E), dtype=np.float64)
    rCt = np.zeros((128, E), dtype=np.float64)
    for h in range(H):
        qdecF[:, h * 128:(h + 1) * 128] = (r[h] ** (c_idx + 1.0))[None, :]
        kdec[:, h * 128:(h + 1) * 128] = (r[h] ** (C - 1.0 - c_idx))[:, None]
        diff = c_idx[None, :] - c_idx[:, None]
        dm = np.where(diff >= 0, r[h] ** np.maximum(diff, 0.0), 0.0)
        dmaskT[:, h * 128:(h + 1) * 128] = dm
        rCt[:, h * 128:(h + 1) * 128] = r[h] ** float(C)
    freq = (1.0 / 1024.0) ** np.linspace(0.0, 1.0, D // 4, dtype=np.float64)
    return (qdecF.astype(np.float32), kdec.astype(np.float32),
            dmaskT.astype(np.float32), rCt.astype(np.float32), freq)


def _trig(t_abs, freq):
    theta = np.asarray(t_abs, dtype=np.float64)[:, None] * freq[None, :]
    cos = np.tile(np.cos(theta), (1, H)).astype(np.float32)
    sin = np.tile(np.sin(theta), (1, H)).astype(np.float32)
    return np.ascontiguousarray(cos), np.ascontiguousarray(sin)


class Res:
    __slots__ = ("name", "sem", "cnt")

    def __init__(self, name, sem):
        self.name = name
        self.sem = sem
        self.cnt = 0


class Prog:
    CENGS = (SC, VE, TE, GP)

    def __init__(self, nc, ctx):
        self.nc = nc
        self.ctx = ctx
        self.ops = {e: [] for e in (SY, SC, VE, TE, GP)}
        self.csems = {e: ctx.enter_context(nc.semaphore("c_" + e)) for e in self.CENGS}
        self.ccnt = {e: 0 for e in self.CENGS}
        self.waited = {e: {} for e in (SY, SC, VE, TE, GP)}

    def res(self, name):
        return Res(name, self.ctx.enter_context(self.nc.semaphore("r_" + name)))

    def dma(self, eng, out, in_, r):
        r.cnt += 16
        val = r.cnt
        self.ops[eng].append(lambda e, out=out, in_=in_, s=r.sem:
                             e.dma_start(out=out, in_=in_).then_inc(s, 16))
        return ("r_" + r.name, r.sem, val)

    def dmat(self, eng, out, in_, r):
        r.cnt += 16
        val = r.cnt
        self.ops[eng].append(lambda e, out=out, in_=in_, s=r.sem:
                             e.dma_start_transpose(out, in_).then_inc(s, 16))
        return ("r_" + r.name, r.sem, val)

    SELFSYNC = (VE, SC, GP)

    def op(self, eng, fn):
        sem = self.csems[eng]
        if eng in self.SELFSYNC and self.ccnt[eng] > 0:
            prev = self.ccnt[eng]
            self.ops[eng].append(lambda e, sem=sem, v=prev: e.wait_ge(sem, v))
        self.ccnt[eng] += 1
        val = self.ccnt[eng]
        self.ops[eng].append(lambda e, fn=fn, sem=sem: fn(e).then_inc(sem, 1))
        return ("c_" + eng, sem, val)

    def opq(self, eng, fn):
        if eng in self.SELFSYNC:
            self.op(eng, fn)
        else:
            self.ops[eng].append(fn)

    def ctok(self, eng):
        return ("c_" + eng, self.csems[eng], self.ccnt[eng])

    def wait(self, eng, tok):
        if tok is None:
            return
        key, sem, v = tok
        if key == "c_" + eng:
            return
        if self.waited[eng].get(key, 0) >= v:
            return
        self.waited[eng][key] = v
        self.ops[eng].append(lambda e, sem=sem, v=v: e.wait_ge(sem, v))

    def waits(self, eng, toks):
        for t in toks:
            self.wait(eng, t)

    def full(self, r):
        return ("r_" + r.name, r.sem, r.cnt)

    def finalize(self, block):
        for name in (SY, SC, VE, TE, GP):
            ops = self.ops[name]
            getattr(block, name)(lambda e, ops=ops: [f(e) for f in ops])


class Buf:
    def __init__(self):
        self.rd = []
        self.wr = None

    def before_write(self, p, eng):
        p.waits(eng, self.rd)
        if self.wr is not None:
            p.wait(eng, self.wr)

    def wrote(self, tok):
        self.wr = tok
        self.rd = []

    def read(self, tok):
        if tok is not None:
            self.rd.append(tok)


def build_program(ctx):
    global _R128
    _R128 = _decay_r128()
    nc = bass.Bass("TRN2", target_bir_lowering=False, dynamic_dma_scratch_size=16384)

    def din(name, shape, dtype=F32):
        return nc.dram_tensor(name, list(shape), dtype, kind="ExternalInput").ap()

    def dint(name, shape, dtype=BF16):
        kind = "ExternalOutput" if name in DBG else "Internal"
        return nc.dram_tensor(name, list(shape), dtype, kind=kind).ap()

    def sb(name, shape, dtype=F32):
        return ctx.enter_context(nc.sbuf_tensor(name, list(shape), dtype))

    xo = din("xo", (T_OWN, E))
    xp = din("xp", (T_PRE, E))
    wq_d = din("wq", (E, E)); wk_d = din("wk", (E, E))
    wv_d = din("wv", (E, E)); wo_d = din("wo", (E, E))
    w1_d = din("w1", (DFF, E)); w2_d = din("w2", (E, DFF))
    b1c_d = din("b1c", (128, DFF // 128)); b2c_d = din("b2c", (128, NF))
    rmsc_d = din("rmsc", (128, NF))
    cos_o = din("cos_o", (T_OWN, 512)); sin_o = din("sin_o", (T_OWN, 512))
    cos_p = din("cos_p", (T_PRE, 512)); sin_p = din("sin_p", (T_PRE, 512))
    qdecF_d = din("qdecF", (128, E)); kdec_d = din("kdec", (128, E))
    dmaskT_d = din("dmaskT", (128, E))
    out_d = nc.dram_tensor("out", [T_OWN, E], F32, kind="ExternalOutput").ap()

    xn_b = dint("xn_b", (T_OWN, E))
    xnp_b = dint("xnp_b", (T_PRE, E))
    q_b = dint("q_b", (T_OWN, E)); k_b = dint("k_b", (T_OWN, E))
    kd_b = dint("kd_b", (T_OWN, E)); v_b = dint("v_b", (T_OWN, E))
    kdp_b = dint("kdp_b", (T_PRE, E)); vp_b = dint("vp_b", (T_PRE, E))
    w1q_b = dint("w1q_b", (DFF, E)); w2q_b = dint("w2q_b", (E, DFF))
    sg_b = dint("sg_b", (T_OWN, E), F32)
    kvpre_d = dint("kvpre_d", (128, E), F32) if "kvpre_d" in DBG else None
    yn_d = dint("yn_d", (128, NF * 1024)) if "yn_d" in DBG else None
    ysb_d = dint("ysb_d", (T_OWN, E)) if "ysb_d" in DBG else None
    kq_d = dint("kq_d", (2, 128, NF * 512)) if "kq_d" in DBG else None
    pjp_d = dint("pjp_d", (T_PRE, E), F32) if "pjp_d" in DBG else None
    pjr_d = dint("pjr_d", (T_PRE, E), F32) if "pjr_d" in DBG else None
    ksb_d = dint("ksb_d", (128, E)) if "ksb_d" in DBG else None
    kvs_d = dint("kvs_d", (128, 2 * E)) if "kvs_d" in DBG else None
    rct_d = dint("rct_d", (128, E), F32) if "rct_d" in DBG else None
    r_dbg = None

    kdec = sb("kdec_s", (128, E), BF16)
    KV32 = sb("KV32", (128, E))
    KVbf = sb("KVbf", (128, E), BF16)
    kvstr = [sb(f"kvstr{i}", (128, 2 * E), BF16) for i in range(2)]
    g_fm = sb("g_fm", (128, NF, 1024), BF16)
    b2c = sb("b2c_s", (128, NF)); rmsc = sb("rmsc_s", (128, NF))
    b1c = sb("b1c_s", (128, DFF // 128))
    ones128 = sb("ones128", (128, 128), BF16)
    ones128f = sb("ones128f", (128, 128))
    epsln = sb("epsln", (128, 1)); epsrms = sb("epsrms", (128, 1))
    stat = sb("stat", (128, 16))
    part1 = sb("part1", (128, 64)); part2 = sb("part2", (128, 64))
    pred = sb("pred", (128, 2)); sc12 = sb("sc12", (128, 2)); inv12 = sb("inv12", (128, 2))

    base = (nc.sbuf_base + 31) & ~31
    SB_TOP = 229344
    R1, R2, R3 = base, base + 32 * KB, base + 96 * KB
    need = R3 + 40 * KB
    assert need <= SB_TOP, f"scratch overflow: need {need}, top {SB_TOP}"

    def at(name, off, shape, dtype=F32):
        return nc.alloc_sbuf_tensor_at(name, list(shape), dtype, offset=off)

    xn_fm = at("xn_fm", R1, (128, NF, 1024), BF16)
    Yn_all = at("Yn_all", R1, (128, NF, 1024), BF16)
    xbuf = [at(f"xbuf{i}", R2 + i * 8 * KB, (128, E)) for i in range(2)]
    xnbf = [at(f"xnbf{i}", R2 + 16 * KB + i * 4 * KB, (128, E), BF16) for i in range(2)]
    junk = at("junk", R2 + 24 * KB, (128, E))
    wrs = at("wrs", R2, (128, NF, E), BF16)
    g_addr = nc.lookup_mloc("g_fm").addr
    qw32 = [at(f"qw32{i}", g_addr + i * 8 * KB, (128, E)) for i in range(2)]
    junkq = at("junkq", g_addr + 16 * KB, (128, E))
    qout = [at(f"qout{i}", g_addr + 24 * KB + i * 4 * KB, (128, E), BF16) for i in range(2)]
    qmsk = at("qmsk", g_addr + 16 * KB, (128, 2 * E), BF16)
    g32 = at("g32", R2, (128, NF, 1024))
    k4 = at("k4", R2, (128, NF, 512), BF16)
    q4 = at("q4", R2 + 16 * KB, (128, NF, 512), BF16)
    wo_sb = at("wo_sb", R2, (128, NF, E), BF16)
    xnp_g = at("xnp_g", R3, (128, NF, 256), BF16)
    pj32 = [at(f"pj32{i}", R3 + 8 * KB + i * 8 * KB, (128, E)) for i in range(2)]
    trigc = [at(f"trigc{i}", R3 + 24 * KB + i * 2 * KB, (128, 512)) for i in range(2)]
    trigs = [at(f"trigs{i}", R3 + 28 * KB + i * 2 * KB, (128, 512)) for i in range(2)]
    rotmp = at("rotmp", R3 + 32 * KB, (128, 1024))
    h_sp = at("h_sp", R3, (128, 4, 1024), BF16)
    wtile_sp = at("wtile_sp", R3 + 8 * KB, (128, 16, 512), BF16)
    w2T = at("w2T", R3 + 24 * KB, (128, 4, 2048), BF16)
    Y_sb = at("Y_sb", R3, (128, NF, 128), BF16)
    Ysq = at("Ysq", R3 + 4 * KB, (128, NF, 128), BF16)
    qtile = at("qtile", R3 + 8 * KB, (128, 128), BF16)
    P_bf = at("P_bf", R3 + 8 * KB + 256, (128, 128), BF16)
    invb_bf = at("invb_bf", R3 + 8 * KB + 512, (128, 128), BF16)
    invb = at("invb", R3 + 9 * KB, (128, 128))
    rtmp = at("rtmp", R3 + 9 * KB + 512, (128, 128))
    dmaskT = at("dmaskT_s", R3 + 10 * KB, (128, E), BF16)
    qdecF = at("qdecF_s", R3 + 14 * KB, (128, E), BF16)
    sgout = at("sgout", R3, (128, 128))
    ident = at("ident", R3 + KB, (128, 128))
    iota_p = at("iota_p", R3 + 2 * KB, (128, 128))
    iota_f = at("iota_f", R3 + 3 * KB, (128, 128))
    sgld = at("sgld", R3 + 4 * KB, (128, E))
    xb2 = at("xb2", R3 + 12 * KB, (128, E))

    ps = ctx.enter_context(nc.psum_tensor("ps", [128, 4096], F32))

    def bank(i, w=512):
        return ps[:, i * 512:i * 512 + w]

    p = Prog(nc, ctx)

    r_tbl = p.res("tbl")
    r_tblg = p.res("tblg")
    r_x = [p.res("x0"), p.res("x1")]
    r_trig = [p.res("tg0"), p.res("tg1")]
    r_qw = [p.res("qw0"), p.res("qw1")]
    r_kvld = [p.res("kv0"), p.res("kv1")]
    r_wt = [p.res("wt0"), p.res("wt1")]
    r_w2s = [p.res("w2s0")]
    r_wrs = p.res("wrsl")
    r_xnp = p.res("xnpl")
    r_xnfm = p.res("xnfm")
    r_kq = p.res("kq")
    r_wo = p.res("wol")
    r_stbl = p.res("stbl")
    r_fin = p.res("fin")
    w_xno = [p.res("wxno0"), p.res("wxno1")]
    w_xnp = [p.res("wxnp0"), p.res("wxnp1")]
    w_kv = [p.res("wkv0"), p.res("wkv1")]
    w_kk = [p.res("wkk0"), p.res("wkk1")]
    w_qw = [p.res("wqw0"), p.res("wqw1")]
    w_sg = p.res("wsg")
    w_out = p.res("wout")
    r_pjd = p.res("pjd") if (pjp_d is not None or pjr_d is not None) else None
    r_kvs = p.res("kvs") if kvs_d is not None else None

    p.dma(GP, kdec[:, :], kdec_d, r_tblg)
    p.dma(SY, b2c[:, :], b2c_d, r_tbl)
    p.dma(SY, rmsc[:, :], rmsc_d, r_tbl)
    p.dma(SY, b1c[:, :], b1c_d, r_tbl)
    t_tbl = p.full(r_tbl)
    t_kdec = p.full(r_tblg)
    if ksb_d is not None:
        r_ksb = p.res("ksb")
        p.wait(SY, t_kdec)
        p.dma(SY, ksb_d, kdec[:, :], r_ksb)

    def init_fn(v):
        v.memset(KV32[:, :], 0.0)
        v.memset(ones128[:, :], 1.0)
        v.memset(ones128f[:, :], 1.0)
        v.memset(epsln[:, 0:1], LN_EPS)
        return v.memset(epsrms[:, 0:1], RMS_EPS)
    p.op(VE, init_fn)

    B_x = [Buf(), Buf()]
    B_xnbf = [Buf(), Buf()]
    B_trig = [Buf(), Buf()]
    B_pj = [Buf(), Buf()]
    B_kv = [Buf(), Buf()]
    B_qw = [Buf(), Buf()]
    B_qo = [Buf(), Buf()]
    B_wt = [Buf(), Buf()]
    B_stat = [Buf(), Buf()]

    t_sent = None
    if "sent" in DBG:
        r_sent = p.res("sent")
        tms = p.op(VE, lambda v: v.memset(kvstr[0][:, 0:E], 7.0))
        p.wait(SY, tms)
        for j in range(NPC):
            p.dma(SY, kdp_b[j * 128:(j + 1) * 128, :], kvstr[0][:, 0:E], r_sent)
        t_sent = p.full(r_sent)
        B_kv[0].read(t_sent)
        p.waits(GP, [t_sent])

    w1_tiles = [w1_d[i * 128:(i + 1) * 128, :] for i in range(64)]
    w2_tiles = [w2_d[(i // 4) * 128:(i // 4 + 1) * 128, (i % 4) * 2048:(i % 4 + 1) * 2048]
                for i in range(64)]
    w1q_tiles = [w1q_b[i * 128:(i + 1) * 128, :] for i in range(64)]
    w2q_tiles = [w2q_b[(i // 4) * 128:(i // 4 + 1) * 128, (i % 4) * 2048:(i % 4 + 1) * 2048]
                 for i in range(64)]
    qred_jobs = [(t, part1, i) for i, t in enumerate(w1_tiles)] + \
                [(t, part2, i) for i, t in enumerate(w2_tiles)]
    qred_pos = [0]

    def quant_reduce_step(nsteps):
        for _ in range(nsteps):
            i = qred_pos[0]
            if i >= len(qred_jobs):
                return
            qred_pos[0] = i + 1
            tile, part, col = qred_jobs[i]
            slot = i % 2
            bq = B_qw[slot]
            bq.before_write(p, SY)
            tl = p.dma(SY, qw32[slot][:, :], tile, r_qw[slot])
            bq.wrote(tl)
            if i % 2 == 0:
                p.wait(VE, tl)
                tr = p.op(VE, lambda v, slot=slot, part=part, col=col:
                          v.scalar_tensor_tensor(qw32[slot][:, :], qw32[slot][:, :], -1.0,
                                                 qw32[slot][:, :], ALU.mult, ALU.max,
                                                 accum_out=part[:, col:col + 1]))
            else:
                p.wait(SC, tl)
                tr = p.op(SC, lambda s, slot=slot, part=part, col=col:
                          s.activation(qw32[slot][:, :], qw32[slot][:, :], AF.Abs,
                                       accum_out=part[:, col:col + 1]))
            bq.read(tr)

    w2_order = [r * 4 + cb for cb in range(4) for r in range(16)]
    qwr_jobs = list(zip(w1_tiles, w1q_tiles, [0] * 64)) + \
               [(w2_tiles[i], w2q_tiles[i], 1) for i in w2_order]
    qwr_pos = [0]
    qwr_tok = [None] * len(qwr_jobs)
    tq0_box = [None]
    last_tqo = [None]

    def quant_write_step(nsteps):
        tq0 = tq0_box[0]
        if tq0 is None:
            return
        for _ in range(nsteps):
            i = qwr_pos[0]
            if i >= len(qwr_jobs):
                return
            qwr_pos[0] = i + 1
            tile, dst, which = qwr_jobs[i]
            inv_ap = inv12[:, which:which + 1]
            wres = w_qw[i % 2]
            slot = i % 2
            bq = B_qw[slot]
            bq.before_write(p, SY)
            tl = p.dma(SY, qw32[slot][:, :], tile, r_qw[slot])
            bq.wrote(tl)
            p.waits(VE, [tl, tq0])
            p.opq(VE, lambda v, slot=slot, inv_ap=inv_ap:
                  v.tensor_scalar(qmsk[:, 0:E], qw32[slot][:, :], inv_ap, 0.5,
                                  ALU.mult, ALU.is_ge))
            tm2 = p.op(VE, lambda v, slot=slot, inv_ap=inv_ap:
                       v.tensor_scalar(qmsk[:, E:2 * E], qw32[slot][:, :], inv_ap, -0.5,
                                       ALU.mult, ALU.is_le))
            bq.read(tm2)
            bo = B_qo[slot]
            bo.before_write(p, VE)
            tqo = p.op(VE, lambda v, slot=slot: v.tensor_tensor(qout[slot][:, :], qmsk[:, 0:E],
                                                                qmsk[:, E:2 * E], ALU.subtract))
            bo.wrote(tqo)
            last_tqo[0] = tqo
            p.wait(GP, tqo)
            tw = p.dma(GP, dst, qout[slot][:, :], wres)
            qwr_tok[i] = tw
            bo.read(tw)

    def emit_ln(src, dst_b, w_res, idx, slot):
        bx = B_x[slot]
        bx.before_write(p, SY)
        tl = p.dma(SY, xbuf[slot][:, :], src[idx * 128:(idx + 1) * 128, :], r_x[slot])
        bx.wrote(tl)
        x_ap = xbuf[slot][:, :]
        st8 = 8 * slot
        p.wait(VE, tl)
        B_stat[slot].before_write(p, VE)
        p.opq(VE, lambda v: v.tensor_scalar(junk[:, :], x_ap, 1.0, 0.0, ALU.mult, ALU.add,
                                            accum_out=stat[:, st8 + 0:st8 + 1]))
        p.opq(VE, lambda v: v.scalar_tensor_tensor(junk[:, :], x_ap, 1.0, x_ap,
                                                   ALU.mult, ALU.mult, accum_out=stat[:, st8 + 1:st8 + 2]))
        p.opq(VE, lambda v: v.tensor_scalar(stat[:, st8 + 2:st8 + 3], stat[:, st8 + 0:st8 + 1], 1.0 / E, 0.0, ALU.mult, ALU.add))
        p.opq(VE, lambda v: v.scalar_tensor_tensor(stat[:, st8 + 3:st8 + 4], stat[:, st8 + 2:st8 + 3], -1.0, stat[:, st8 + 2:st8 + 3],
                                                   ALU.mult, ALU.mult))
        tv = p.op(VE, lambda v: v.scalar_tensor_tensor(stat[:, st8 + 4:st8 + 5], stat[:, st8 + 1:st8 + 2], 1.0 / E,
                                                       stat[:, st8 + 3:st8 + 4], ALU.mult, ALU.add))
        p.wait(SC, tv)
        tsq = p.op(SC, lambda s: s.activation(stat[:, st8 + 5:st8 + 6], stat[:, st8 + 4:st8 + 5], AF.Sqrt,
                                              bias=epsln[:, 0:1], scale=1.0))
        p.wait(VE, tsq)
        p.opq(VE, lambda v: v.reciprocal(stat[:, st8 + 6:st8 + 7], stat[:, st8 + 5:st8 + 6]))
        tnm = p.op(VE, lambda v: v.scalar_tensor_tensor(stat[:, st8 + 7:st8 + 8], stat[:, st8 + 2:st8 + 3], -1.0,
                                                        stat[:, st8 + 6:st8 + 7], ALU.mult, ALU.mult))
        bn = B_xnbf[slot]
        bn.before_write(p, SC)
        p.wait(SC, tnm)
        txn = p.op(SC, lambda s: s.activation(xnbf[slot][:, :], x_ap, AF.Identity,
                                              bias=stat[:, st8 + 7:st8 + 8], scale=stat[:, st8 + 6:st8 + 7]))
        B_stat[slot].wrote(txn)
        bx.read(txn)
        bn.wrote(txn)
        p.wait(GP, txn)
        tw = p.dma(GP, dst_b[idx * 128:(idx + 1) * 128, :], xnbf[slot][:, :], w_res)
        bn.read(tw)
        return tw

    for i in range(NCH):
        emit_ln(xo, xn_b, w_xno[i % 2], i, i % 2)
        quant_reduce_step(1)
    for i in range(NPC):
        emit_ln(xp, xnp_b, w_xnp[i % 2], i, i % 2)
        quant_reduce_step(2)
    t_xno_all = [p.full(w_xno[0]), p.full(w_xno[1])]
    t_xnp_all = [p.full(w_xnp[0]), p.full(w_xnp[1])]

    p.waits(SC, t_xno_all)
    for f in range(NF):
        p.dmat(SC, xn_fm[:, f, :], xn_b[0:T_OWN, f * 128:(f + 1) * 128], r_xnfm)
    t_xnfm_all = p.full(r_xnfm)

    B_wrs = Buf()

    def load_W(w_dram):
        B_wrs.before_write(p, GP)
        for f in range(NF):
            p.dma(GP, wrs[:, f, :], w_dram[f * 128:(f + 1) * 128, :], r_wrs)
        tok = p.full(r_wrs)
        B_wrs.wrote(tok)
        return tok

    pbank = {}

    def proj_chunk(xn_t, banks, out_write, wait_toks):
        etoks = []
        for n in range(4):
            bk_i = banks[n % len(banks)]
            bk = bank(bk_i)
            p.waits(TE, wait_toks)
            if pbank.get(bk_i) is not None:
                p.wait(TE, pbank[bk_i])
            tmm = None
            for f in range(NF):
                A = xn_t(f)
                B = wrs[:, f, n * 512:(n + 1) * 512]
                st = (f == 0); sp = (f == NF - 1)
                if sp:
                    tmm = p.op(TE, lambda t, A=A, B=B, bk=bk, st=st: t.matmul(bk, A, B, start=st, stop=True))
                else:
                    p.opq(TE, lambda t, A=A, B=B, bk=bk, st=st: t.matmul(bk, A, B, start=st, stop=False))
            p.wait(SC, tmm)
            te = out_write(n, bk)
            pbank[bk_i] = te
            etoks.append(te)
        return etoks

    def head3d(t, base_c, width=32):
        return t[:, :].rearrange("p (h d) -> p h d", d=128)[:, :, base_c:base_c + width]

    def rep3d(t, width=32):
        return t[:, :].rearrange("p (h d) -> p h d", d=32)[:, :, 0:width]

    def emit_rotary(dst32, cosb, sinb):
        x1 = head3d(dst32, 0); x2 = head3d(dst32, 64)
        tm1 = rotmp[:, 0:512].rearrange("p (h d) -> p h d", d=32)
        tm2 = rotmp[:, 512:1024].rearrange("p (h d) -> p h d", d=32)
        cs = rep3d(cosb); sn = rep3d(sinb)
        p.opq(VE, lambda v: v.tensor_tensor(tm1, x2, sn, ALU.mult))
        p.opq(VE, lambda v: v.tensor_tensor(tm2, x1, sn, ALU.mult))
        p.opq(VE, lambda v: v.tensor_tensor(x2, x2, cs, ALU.mult))
        p.opq(VE, lambda v: v.tensor_tensor(x2, x2, tm2, ALU.subtract))
        p.opq(VE, lambda v: v.tensor_tensor(x1, x1, cs, ALU.mult))
        return p.op(VE, lambda v: v.tensor_tensor(x1, x1, tm1, ALU.add))

    def load_trig(slot, cos_src, sin_src, j):
        bt = B_trig[slot]
        bt.before_write(p, SY)
        p.dma(SY, trigc[slot][:, :], cos_src[j * 128:(j + 1) * 128, :], r_trig[slot])
        p.dma(SY, trigs[slot][:, :], sin_src[j * 128:(j + 1) * 128, :], r_trig[slot])
        t2 = p.full(r_trig[slot])
        bt.wrote(t2)
        return t2

    def evac_pj(slot):
        def w(n, bk):
            if n == 0:
                B_pj[slot].before_write(p, SC)
            return p.op(SC, lambda s, bk=bk, n=n: s.copy(pj32[slot][:, n * 512:(n + 1) * 512], bk))
        return w

    p.waits(GP, t_xno_all + t_xnp_all)
    t_wk = load_W(wk_d)
    wrs_readers = []

    def kproj(xn_t, slot, wait_toks, trig_tok, bounce_kd, bounce_k, dbg_j=None):
        et = proj_chunk(xn_t, (0, 1, 2, 3), evac_pj(slot), wait_toks)
        B_pj[slot].wrote(et[-1])
        wrs_readers.append(et[-1])
        if dbg_j is not None and pjp_d is not None:
            p.waits(GP, et)
            tdp = p.dma(GP, pjp_d[dbg_j * 128:(dbg_j + 1) * 128, :], pj32[slot][:, :], r_pjd)
            B_pj[slot].read(tdp)
            p.waits(VE, [tdp])
        p.waits(VE, et + [trig_tok, t_kdec])
        trot = emit_rotary(pj32[slot], trigc[slot], trigs[slot])
        B_trig[slot].read(trot)
        if dbg_j is not None and pjr_d is not None:
            p.wait(GP, trot)
            tdr = p.dma(GP, pjr_d[dbg_j * 128:(dbg_j + 1) * 128, :], pj32[slot][:, :], r_pjd)
            B_pj[slot].read(tdr)
        bkv = B_kv[slot]
        bkv.before_write(p, VE)
        tkr = p.op(VE, lambda v, slot=slot: v.tensor_copy(kvstr[slot][:, E:2 * E], pj32[slot][:, :]))
        tkd = p.op(VE, lambda v, slot=slot: v.tensor_tensor(kvstr[slot][:, 0:E],
                                                            kvstr[slot][:, E:2 * E],
                                                            kdec[:, :], ALU.mult))
        toks = [tkr, tkd]
        B_pj[slot].read(toks[-1])
        bkv.wrote(toks[-1])
        if kvs_d is not None and dbg_j == 22:
            p.waits(SY, toks)
            tks = p.dma(SY, kvs_d, kvstr[slot][:, :], r_kvs)
            bkv.read(tks)
        p.waits(SY, toks)
        p.dma(SY, bounce_kd, kvstr[slot][:, 0:E], w_kk[slot])
        if bounce_k is not None:
            p.dma(SY, bounce_k, kvstr[slot][:, E:2 * E], w_kk[slot])
        bkv.read(p.full(w_kk[slot]))

    for j in range(NCH):
        slot = j % 2
        tt = load_trig(slot, cos_o, sin_o, j)
        kproj(lambda f, j=j: xn_fm[:, f, j * 128:(j + 1) * 128], slot,
              [t_wk, t_xnfm_all], tt, kd_b[j * 128:(j + 1) * 128, :],
              k_b[j * 128:(j + 1) * 128, :])
        quant_reduce_step(3)

    B_xnp = Buf()
    for g in range(T_PRE // 256):
        B_xnp.before_write(p, SC)
        p.waits(SC, t_xnp_all)
        for f in range(NF):
            p.dmat(SC, xnp_g[:, f, :], xnp_b[g * 256:(g + 1) * 256, f * 128:(f + 1) * 128], r_xnp)
        tg = p.full(r_xnp)
        B_xnp.wrote(tg)
        for jj in range(2):
            j = g * 2 + jj
            slot = jj % 2
            tt = load_trig(slot, cos_p, sin_p, j)
            kproj(lambda f, jj=jj: xnp_g[:, f, jj * 128:(jj + 1) * 128], slot,
                  [t_wk, tg], tt, kdp_b[j * 128:(j + 1) * 128, :], None, dbg_j=j)
            B_xnp.read(wrs_readers[-1])
            quant_reduce_step(2)


    quant_reduce_step(len(qred_jobs))
    p.waits(VE, [p.ctok(SC)])
    p.opq(VE, lambda v: v.tensor_reduce(pred[:, 0:1], part1[:, :], AX.X, ALU.add))
    tpr = p.op(VE, lambda v: v.tensor_reduce(pred[:, 1:2], part2[:, :], AX.X, ALU.add))
    p.waits(TE, [tpr])
    if pbank.get(2) is not None:
        p.wait(TE, pbank[2])
    tsum = p.op(TE, lambda t: t.matmul(ps[:, 1024:1026], ones128f[:, :], pred[:, 0:2],
                                       start=True, stop=True))
    p.wait(VE, tsum)
    NW = float(DFF * E)
    p.opq(VE, lambda v: v.tensor_scalar(sc12[:, 0:2], ps[:, 1024:1026], 1.0 / NW, 1e-8,
                                        ALU.mult, ALU.max))
    tq0_box[0] = p.op(VE, lambda v: v.reciprocal(inv12[:, 0:2], sc12[:, 0:2]))
    pbank[2] = tq0_box[0]

    p.waits(GP, wrs_readers)
    for t in wrs_readers:
        B_wrs.read(t)
    t_wv = load_W(wv_d)
    wrs_readers = []

    def evac_kv(slot):
        def w(n, bk):
            if n == 0:
                B_kv[slot].before_write(p, SC)
            return p.op(SC, lambda s, bk=bk, n=n, slot=slot:
                        s.copy(kvstr[slot][:, n * 512:(n + 1) * 512], bk))
        return w

    def vproj(xn_t, banks, slot, wait_toks, dst):
        et = proj_chunk(xn_t, banks, evac_kv(slot), wait_toks)
        wrs_readers.append(et[-1])
        B_kv[slot].wrote(et[-1])
        p.waits(GP, et)
        p.dma(GP, dst, kvstr[slot][:, 0:E], w_kv[slot])
        tw = p.full(w_kv[slot])
        B_kv[slot].read(tw)
        return tw

    for j in range(NCH):
        vproj(lambda f, j=j: xn_fm[:, f, j * 128:(j + 1) * 128],
              (0, 1, 2, 3), j % 2, [t_wv], v_b[j * 128:(j + 1) * 128, :])
        quant_write_step(2)
    for g in range(T_PRE // 256):
        B_xnp.before_write(p, SC)
        for f in range(NF):
            p.dmat(SC, xnp_g[:, f, :], xnp_b[g * 256:(g + 1) * 256, f * 128:(f + 1) * 128], r_xnp)
        tg = p.full(r_xnp)
        B_xnp.wrote(tg)
        for jj in range(2):
            j = g * 2 + jj
            tw = vproj(lambda f, jj=jj: xnp_g[:, f, jj * 128:(jj + 1) * 128],
                       (0, 1, 2, 3), jj % 2, [t_wv, tg], vp_b[j * 128:(j + 1) * 128, :])
            B_xnp.read(tw)
            quant_write_step(2)

    p.waits(GP, wrs_readers)
    for t in wrs_readers:
        B_wrs.read(t)
    t_wq = load_W(wq_d)
    wrs_readers = []
    for j in range(NCH):
        slot = j % 2
        tt = load_trig(slot, cos_o, sin_o, j)
        et = proj_chunk(lambda f, j=j: xn_fm[:, f, j * 128:(j + 1) * 128],
                        (0, 1, 2, 3), evac_pj(slot), [t_wq])
        B_pj[slot].wrote(et[-1])
        wrs_readers.append(et[-1])
        p.waits(VE, et + [tt])
        trot = emit_rotary(pj32[slot], trigc[slot], trigs[slot])
        B_trig[slot].read(trot)
        bkv = B_kv[slot]
        bkv.before_write(p, VE)
        tqr = p.op(VE, lambda v, slot=slot: v.tensor_copy(kvstr[slot][:, 0:E], pj32[slot][:, :]))
        B_pj[slot].read(tqr)
        bkv.wrote(tqr)
        p.wait(GP, tqr)
        p.dma(GP, q_b[j * 128:(j + 1) * 128, :], kvstr[slot][:, 0:E], w_kv[slot])
        bkv.read(p.full(w_kv[slot]))
        quant_write_step(2)

    t_bounce_all = [p.full(w_kv[0]), p.full(w_kv[1]), p.full(w_kk[0]), p.full(w_kk[1])]
    t_kv_upd = None
    t_kv_par = [None, None]
    for b_ in range(4):
        if pbank.get(b_) is not None:
            p.wait(TE, pbank[b_])
            pbank[b_] = None
    for l in range(NPC):
        slot = l % 2
        base = 2048 * (l % 2)
        bkv = B_kv[slot]
        bkv.before_write(p, SY)
        p.waits(SY, t_bounce_all)
        p.dma(SY, kvstr[slot][:, 0:E], kdp_b[l * 128:(l + 1) * 128, :], r_kvld[slot])
        p.dma(SY, kvstr[slot][:, E:2 * E], vp_b[l * 128:(l + 1) * 128, :], r_kvld[slot])
        tv = p.full(r_kvld[slot])
        bkv.wrote(tv)
        p.waits(TE, [tv])
        if t_kv_par[l % 2] is not None:
            p.wait(TE, t_kv_par[l % 2])
        tdel = None
        for h in range(H):
            A = kvstr[slot][:, h * 128:(h + 1) * 128]
            B = kvstr[slot][:, E + h * 128:E + (h + 1) * 128]
            bk = ps[:, base + h * 128:base + (h + 1) * 128]
            if h == H - 1:
                tdel = p.op(TE, lambda t, A=A, B=B, bk=bk: t.matmul(bk, A, B, start=True, stop=True))
            else:
                p.opq(TE, lambda t, A=A, B=B, bk=bk: t.matmul(bk, A, B, start=True, stop=True))
        bkv.read(tdel)
        p.waits(VE, [tdel, t_tbl])
        for hx in range(H):
            p.opq(VE, lambda v, hx=hx: v.tensor_scalar(
                KV32[:, hx * 128:(hx + 1) * 128], KV32[:, hx * 128:(hx + 1) * 128],
                _R128[hx], 0.0, ALU.mult, ALU.add))
        for bq_ in range(4):
            t_kv_upd = p.op(VE, lambda v, b=bq_, base=base: v.tensor_tensor(
                KV32[:, b * 512:(b + 1) * 512], KV32[:, b * 512:(b + 1) * 512],
                ps[:, base + b * 512:base + (b + 1) * 512], ALU.add))
        t_kv_par[l % 2] = t_kv_upd
        quant_write_step(1)

    gb = [None] * 8
    p.wait(TE, t_kv_upd)
    p.waits(SC, [p.ctok(VE)])
    B_wts = Buf()
    B_w2s = Buf()
    t_gadd = None
    g_last = None
    for sp_i in range(16):
        B_wts.before_write(p, SC)
        p.waits(SC, [qwr_tok[4 * sp_i + 2], qwr_tok[4 * sp_i + 3]])
        for f in range(NF):
            p.dmat(SC, wtile_sp[:, f, :],
                   w1q_b[sp_i * 512:(sp_i + 1) * 512, f * 128:(f + 1) * 128], r_wt[0])
        t_wts = p.full(r_wt[0])
        B_wts.wrote(t_wts)
        B_w2s.before_write(p, SC)
        jlast = 64 + 16 * (sp_i // 4) + 15
        p.waits(SC, [qwr_tok[jlast - 1], qwr_tok[jlast]])
        for fo4 in range(4):
            for dp in range(2):
                p.dmat(SC, w2T[:, fo4, dp * 1024:(dp + 1) * 1024],
                       w2q_b[dp * 1024:(dp + 1) * 1024,
                             (sp_i * 4 + fo4) * 128:(sp_i * 4 + fo4 + 1) * 128], r_w2s[0])
        t_w2 = p.full(r_w2s[0])
        B_w2s.wrote(t_w2)
        p.waits(TE, [t_wts, t_xnfm_all])
        for b in range(8):
            if gb[b] is not None:
                p.wait(TE, gb[b])
            if sp_i == 0 and pbank.get(b) is not None:
                p.wait(TE, pbank[b])
                pbank[b] = None
        htok = None
        for f in range(NF):
            for fo4 in range(4):
                A = wtile_sp[:, f, fo4 * 128:(fo4 + 1) * 128]
                for th in range(2):
                    bk = bank(fo4 * 2 + th)
                    B = xn_fm[:, f, th * 512:(th + 1) * 512]
                    st = (f == 0); spx = (f == NF - 1)
                    if spx and fo4 == 3 and th == 1:
                        htok = p.op(TE, lambda t, A=A, B=B, bk=bk, st=st, spx=spx:
                                    t.matmul(bk, A, B, start=st, stop=spx))
                    else:
                        p.opq(TE, lambda t, A=A, B=B, bk=bk, st=st, spx=spx:
                              t.matmul(bk, A, B, start=st, stop=spx))
        B_wts.read(htok)
        p.waits(SC, [htok, t_tbl])
        if g_last is not None:
            p.wait(SC, g_last)
        ev_last = None
        for fo4 in range(4):
            for th in range(2):
                ev_last = p.op(SC, lambda s, fo4=fo4, th=th, sp_i=sp_i:
                               s.activation(h_sp[:, fo4, th * 512:(th + 1) * 512],
                                            bank(fo4 * 2 + th), AF.Gelu,
                                            bias=b1c[:, sp_i * 4 + fo4:sp_i * 4 + fo4 + 1],
                                            scale=1.0))
        for m in range(32):
            db, th, b = m // 2, m % 2, m % 8
            bk = bank(b)
            p.waits(TE, [ev_last, t_w2])
            if gb[b] is not None:
                p.wait(TE, gb[b])
            tmm = None
            for fo4 in range(4):
                A = w2T[:, fo4, db * 128:(db + 1) * 128]
                B = h_sp[:, fo4, th * 512:(th + 1) * 512]
                st = (fo4 == 0)
                if fo4 == 3:
                    tmm = p.op(TE, lambda t, A=A, B=B, bk=bk, st=st:
                               t.matmul(bk, A, B, start=st, stop=True))
                else:
                    p.opq(TE, lambda t, A=A, B=B, bk=bk, st=st:
                          t.matmul(bk, A, B, start=st, stop=False))
            p.wait(VE, tmm)
            if sp_i == 0:
                t_gadd = p.op(VE, lambda v, db=db, th=th, bk=bk:
                              v.tensor_copy(g32[:, db, th * 512:(th + 1) * 512], bk))
            else:
                t_gadd = p.op(VE, lambda v, db=db, th=th, bk=bk:
                              v.tensor_tensor(g32[:, db, th * 512:(th + 1) * 512],
                                              g32[:, db, th * 512:(th + 1) * 512], bk, ALU.add))
            gb[b] = t_gadd
        g_last = tmm
        B_w2s.read(g_last)
        quant_write_step(2)
    t_q_done = [p.full(w_qw[0]), p.full(w_qw[1]), p.full(r_qw[0]), p.full(r_qw[1]), last_tqo[0]]

    silu_tok = [None] * 32
    p.waits(SC, [t_gadd, t_tbl] + t_q_done)
    for d in range(NF):
        for half in range(2):
            silu_tok[d * 2 + half] = p.op(SC, lambda s, d=d, half=half:
                                          s.activation(g_fm[:, d, half * 512:(half + 1) * 512],
                                                       g32[:, d, half * 512:(half + 1) * 512],
                                                       AF.Silu, bias=b2c[:, d:d + 1], scale=1.0))
    last_mm_half = silu_tok[31]
    gbank = gb

    ffn_gate = [t for t in silu_tok if t is not None]
    p.waits(GP, ffn_gate)
    p.waits(TE, [t for t in gbank if t is not None])
    p.dma(GP, dmaskT[:, :], dmaskT_d, r_stbl)
    p.dma(GP, qdecF[:, :], qdecF_d, r_stbl)
    t_stbl = p.full(r_stbl)
    p.wait(VE, t_kv_upd)
    t_kvbf = p.op(VE, lambda v: v.tensor_copy(KVbf[:, :], KV32[:, :]))
    if kvpre_d is not None:
        r_dbg = p.res("dbg")
        p.wait(GP, t_kvbf)
        p.dma(GP, kvpre_d, KV32[:, :], r_dbg)

    B_kq = Buf()
    last_yev = None
    last_tsr = None
    last_yn = None
    t_scoreP = None
    for grp in range(2):
        B_kq.before_write(p, SY)
        B_kq.before_write(p, SC)
        p.waits(SY, ffn_gate + t_bounce_all)
        p.waits(SC, ffn_gate + t_bounce_all)
        for f in range(NF):
            p.dmat(SC, k4[:, f, :], k_b[grp * 512:(grp + 1) * 512, f * 128:(f + 1) * 128], r_kq)
            p.dmat(SC, q4[:, f, :], q_b[grp * 512:(grp + 1) * 512, f * 128:(f + 1) * 128], r_kq)
        t_kq_all = p.full(r_kq)
        B_kq.wrote(t_kq_all)
        if kq_d is not None and grp == 0:
            if r_dbg is None:
                r_dbg = p.res("dbg")
            p.wait(GP, t_kq_all)
            p.dma(GP, kq_d[0], k4[:, :, :].rearrange("p a b -> p (a b)"), r_dbg)
            p.dma(GP, kq_d[1], q4[:, :, :].rearrange("p a b -> p (a b)"), r_dbg)
        for jj in range(4):
            j = grp * 4 + jj
            slot = j % 2
            bkv = B_kv[slot]
            bkv.before_write(p, SY)
            p.dma(SY, kvstr[slot][:, 0:E], kd_b[j * 128:(j + 1) * 128, :], r_kvld[slot])
            p.dma(SY, kvstr[slot][:, E:2 * E], v_b[j * 128:(j + 1) * 128, :], r_kvld[slot])
            tv = p.full(r_kvld[slot])
            bkv.wrote(tv)
            t_yev = []
            tdel = None
            for h in range(H):
                bkS = ps[:, 0:128]
                p.waits(TE, [t_kq_all, tv, t_kvbf])
                if t_scoreP is not None:
                    p.wait(TE, t_scoreP)
                tS = p.op(TE, lambda t, h=h, jj=jj, bkS=bkS:
                          t.matmul(bkS, k4[:, h, jj * 128:(jj + 1) * 128],
                                   q4[:, h, jj * 128:(jj + 1) * 128], start=True, stop=True))
                p.waits(VE, [tS, t_stbl])
                t_scoreP = p.op(VE, lambda v, h=h, bkS=bkS:
                                v.tensor_tensor(P_bf[:, :], bkS, dmaskT[:, h * 128:(h + 1) * 128],
                                                ALU.mult))
                tqd = p.op(VE, lambda v, h=h, jj=jj:
                           v.tensor_tensor(qtile[:, :], q4[:, h, jj * 128:(jj + 1) * 128],
                                           qdecF[:, h * 128:(h + 1) * 128], ALU.mult))
                bkY = ps[:, 512:640]
                p.waits(TE, [t_scoreP, tqd])
                if last_yev is not None:
                    p.wait(TE, last_yev)
                p.opq(TE, lambda t, h=h, slot=slot, bkY=bkY:
                      t.matmul(bkY, kvstr[slot][:, E + h * 128:E + (h + 1) * 128], P_bf[:, :],
                               start=True, stop=False))
                tY = p.op(TE, lambda t, h=h, bkY=bkY:
                          t.matmul(bkY, KVbf[:, h * 128:(h + 1) * 128], qtile[:, :],
                                   start=False, stop=True))
                if last_yn is not None:
                    p.wait(SC, last_yn)
                p.wait(SC, tY)
                last_yev = p.op(SC, lambda s, h=h, bkY=bkY: s.copy(Y_sb[:, h, :], bkY))
                t_yev.append(last_yev)
                bkD = ps[:, 2048 + h * 128:2048 + (h + 1) * 128]
                if t_kv_upd is not None:
                    p.wait(TE, t_kv_upd)
                if h == H - 1:
                    tdel = p.op(TE, lambda t, h=h, slot=slot, bkD=bkD:
                                t.matmul(bkD, kvstr[slot][:, h * 128:(h + 1) * 128],
                                         kvstr[slot][:, E + h * 128:E + (h + 1) * 128],
                                         start=True, stop=True))
                else:
                    p.opq(TE, lambda t, h=h, slot=slot, bkD=bkD:
                          t.matmul(bkD, kvstr[slot][:, h * 128:(h + 1) * 128],
                                   kvstr[slot][:, E + h * 128:E + (h + 1) * 128],
                                   start=True, stop=True))
            bkv.read(tdel)
            B_kq.read(tdel)
            p.waits(VE, [tdel])
            for hx in range(H):
                p.opq(VE, lambda v, hx=hx: v.tensor_scalar(
                    KV32[:, hx * 128:(hx + 1) * 128], KV32[:, hx * 128:(hx + 1) * 128],
                    _R128[hx], 0.0, ALU.mult, ALU.add))
            for bq_ in range(4):
                t_kv_upd = p.op(VE, lambda v, b=bq_: v.tensor_tensor(
                    KV32[:, b * 512:(b + 1) * 512], KV32[:, b * 512:(b + 1) * 512],
                    ps[:, 2048 + b * 512:2048 + (b + 1) * 512], ALU.add))
            t_kvbf = p.op(VE, lambda v: v.tensor_copy(KVbf[:, :], KV32[:, :]))
            p.waits(SC, t_yev)
            if ysb_d is not None:
                if r_dbg is None:
                    r_dbg = p.res("dbg")
                p.waits(GP, t_yev)
                tyd = p.dma(GP, ysb_d[j * 128:(j + 1) * 128, :],
                            Y_sb[:, :, :].rearrange("p a b -> p (a b)"), r_dbg)
                p.wait(SC, tyd)
            tsq = p.op(SC, lambda s: s.activation(Ysq[:, :, :].rearrange("p a b -> p (a b)"),
                                                  Y_sb[:, :, :].rearrange("p a b -> p (a b)"),
                                                  AF.Square))
            bkR = ps[:, 1024:1152]
            p.waits(TE, [tsq])
            if last_tsr is not None:
                p.wait(TE, last_tsr)
            tss = None
            for e in range(NF):
                st = (e == 0); sp = (e == NF - 1)
                if sp:
                    tss = p.op(TE, lambda t, e=e, bkR=bkR, st=st:
                               t.matmul(bkR, ones128[:, :], Ysq[:, e, :], start=st, stop=True))
                else:
                    p.opq(TE, lambda t, e=e, bkR=bkR, st=st:
                          t.matmul(bkR, ones128[:, :], Ysq[:, e, :], start=st, stop=False))
            p.wait(SC, tss)
            last_tsr = p.op(SC, lambda s, bkR=bkR: s.activation(rtmp[:, :], bkR, AF.Sqrt,
                                                                bias=epsrms[:, 0:1], scale=1.0 / E))
            p.waits(VE, [last_tsr, t_tbl])
            trc = p.op(VE, lambda v: v.reciprocal(invb[:, :], rtmp[:, :]))
            p.op(VE, lambda v: v.tensor_copy(invb_bf[:, :], invb[:, :]))
            for e in range(NF):
                last_yn = p.op(VE, lambda v, e=e, j=j:
                               v.scalar_tensor_tensor(Yn_all[:, e, j * 128:(j + 1) * 128],
                                                      Y_sb[:, e, :], rmsc[:, e:e + 1],
                                                      invb_bf[:, :], ALU.mult, ALU.mult))

    if yn_d is not None:
        if r_dbg is None:
            r_dbg = p.res("dbg")
        p.waits(GP, [last_yn, last_yev])
        p.dma(GP, yn_d, Yn_all[:, :, :].rearrange("p a b -> p (a b)"), r_dbg)
    if rct_d is not None:
        r_rct = p.res("rct")
        p.wait(SY, last_yn)
        p.dma(SY, rct_d, rCt[:, :], r_rct)

    p.waits(GP, [last_yn, t_kv_upd])
    B_wo = Buf()
    B_wo.before_write(p, GP)
    for f in range(NF):
        p.dma(GP, wo_sb[:, f, :], wo_d[f * 128:(f + 1) * 128, :], r_wo)
    t_wo = p.full(r_wo)
    B_wo.wrote(t_wo)
    t_iota = p.op(GP, lambda g2: (g2.iota(iota_p[:, :], [[0, 128]], channel_multiplier=1,
                                          allow_small_or_imprecise_dtypes=True),
                                  g2.iota(iota_f[:, :], [[1, 128]], channel_multiplier=0,
                                          allow_small_or_imprecise_dtypes=True))[-1])
    p.waits(VE, [t_iota, last_yn])
    t_ident = p.op(VE, lambda v: v.tensor_tensor(ident[:, :], iota_p[:, :], iota_f[:, :], ALU.is_equal))

    B_sg = Buf()
    B_sgo = Buf()
    tT_last = {}
    for half in range(2):
        for foc in range(NF):
            bk_i = foc % 2
            bk = bank(bk_i)
            p.waits(TE, [t_wo, last_yn])
            if pbank.get(bk_i) is not None:
                p.wait(TE, pbank[bk_i])
                pbank[bk_i] = None
            tS = None
            for f in range(NF):
                A = wo_sb[:, f, foc * 128:(foc + 1) * 128]
                B = Yn_all[:, f, half * 512:(half + 1) * 512]
                st = (f == 0); sp = (f == NF - 1)
                if sp:
                    tS = p.op(TE, lambda t, A=A, B=B, bk=bk, st=st: t.matmul(bk, A, B, start=st, stop=True))
                else:
                    p.opq(TE, lambda t, A=A, B=B, bk=bk, st=st: t.matmul(bk, A, B, start=st, stop=False))
            B_sg.before_write(p, VE)
            p.wait(VE, tS)
            tg = p.op(VE, lambda v, bk=bk, foc=foc, half=half:
                      v.tensor_tensor(sgld[:, 0:512], bk, g_fm[:, foc, half * 512:(half + 1) * 512],
                                      ALU.mult))
            pbank[bk_i] = tg
            B_sg.wrote(tg)
            for cs in range(4):
                bkT_i = 4 + (cs % 2)
                bkT = bank(bkT_i, 128)
                p.waits(TE, [tg, t_ident])
                if tT_last.get(bkT_i) is not None:
                    p.wait(TE, tT_last[bkT_i])
                tT = p.op(TE, lambda t, cs=cs, bkT=bkT:
                          t.transpose(bkT, sgld[:, cs * 128:(cs + 1) * 128], ident[:, :]))
                B_sg.read(tT)
                B_sgo.before_write(p, SC)
                p.wait(SC, tT)
                tcp = p.op(SC, lambda s, bkT=bkT: s.copy(sgout[:, :], bkT))
                B_sgo.wrote(tcp)
                p.wait(GP, tcp)
                cch = half * 4 + cs
                p.dma(GP, sg_b[cch * 128:(cch + 1) * 128, foc * 128:(foc + 1) * 128],
                      sgout[:, :], w_sg)
                tws = p.full(w_sg)
                B_sgo.read(tws)
                tT_last[bkT_i] = tws
    t_sg_all = p.full(w_sg)

    B_sgld = Buf()
    for cch in range(NCH):
        p.waits(SY, [t_sg_all])
        B_sgld.before_write(p, SY)
        p.dma(SY, sgld[:, :], sg_b[cch * 128:(cch + 1) * 128, :], r_fin)
        p.dma(SY, xb2[:, :], xo[cch * 128:(cch + 1) * 128, :], r_fin)
        t_both = p.full(r_fin)
        B_sgld.wrote(t_both)
        p.waits(VE, [t_both])
        ta = p.op(VE, lambda v: v.tensor_tensor(sgld[:, :], sgld[:, :], xb2[:, :], ALU.add))
        B_sgld.read(ta)
        p.wait(GP, ta)
        tst = p.dma(GP, out_d[cch * 128:(cch + 1) * 128, :], sgld[:, :], w_out)
        B_sgld.read(tst)

    fin = w_out.cnt
    p.ops[GP].append(lambda g2, sem=w_out.sem, v=fin: g2.wait_ge(sem, v))

    block = ctx.enter_context(nc.Block())
    p.finalize(block)
    return nc


def kernel(L0_attn_x, L0_attn_Wq, L0_attn_Wk, L0_attn_Wv, L0_attn_Wo, L0_attn_rms_w,
           L0_ffn_w1, L0_ffn_b1, L0_ffn_w2, L0_ffn_b2):
    x = np.ascontiguousarray(np.asarray(L0_attn_x, dtype=np.float32))
    wq = np.ascontiguousarray(np.asarray(L0_attn_Wq, dtype=np.float32))
    wk = np.ascontiguousarray(np.asarray(L0_attn_Wk, dtype=np.float32))
    wv = np.ascontiguousarray(np.asarray(L0_attn_Wv, dtype=np.float32))
    wo = np.ascontiguousarray(np.asarray(L0_attn_Wo, dtype=np.float32))
    rms_w = np.asarray(L0_attn_rms_w, dtype=np.float32)
    w1 = np.ascontiguousarray(np.asarray(L0_ffn_w1, dtype=np.float32))
    b1 = np.asarray(L0_ffn_b1, dtype=np.float32)
    w2 = np.ascontiguousarray(np.asarray(L0_ffn_w2, dtype=np.float32))
    b2 = np.asarray(L0_ffn_b2, dtype=np.float32)

    qdecF, kdec, dmaskT, rCt, freq = _host_tables()
    b1c = np.ascontiguousarray(b1.reshape(DFF // 128, 128).T)
    b2c = np.ascontiguousarray(b2.reshape(NF, 128).T)
    rmsc = np.ascontiguousarray(rms_w.reshape(NF, 128).T)

    in_maps = []
    for core in range(8):
        b, s = core // 4, core % 4
        xo_h = np.ascontiguousarray(x[b, s * T_OWN:(s + 1) * T_OWN])
        pre_rows = s * T_OWN
        xp_h = np.zeros((T_PRE, E), dtype=np.float32)
        if pre_rows > 0:
            take = min(pre_rows, T_PRE)
            xp_h[T_PRE - take:] = x[b, pre_rows - take:pre_rows]
        t_own = s * T_OWN + np.arange(T_OWN)
        t_pre = s * T_OWN - T_PRE + np.arange(T_PRE)
        cos_o, sin_o = _trig(t_own, freq)
        cos_p, sin_p = _trig(t_pre, freq)
        in_maps.append({
            "xo": xo_h, "xp": xp_h,
            "wq": wq, "wk": wk, "wv": wv, "wo": wo,
            "w1": w1, "w2": w2,
            "b1c": b1c, "b2c": b2c, "rmsc": rmsc,
            "cos_o": cos_o, "sin_o": sin_o, "cos_p": cos_p, "sin_p": sin_p,
            "qdecF": qdecF, "kdec": kdec, "dmaskT": dmaskT, "rCt": rCt,
        })

    with ExitStack() as ctx:
        nc = build_program(ctx)
        res = run_bass_kernel_spmd(nc, in_maps, core_ids=list(range(8)))
    global LAST_RES
    LAST_RES = res

    out = np.empty((2, 4096, E), dtype=np.float32)
    for core in range(8):
        b, s = core // 4, core % 4
        out[b, s * T_OWN:(s + 1) * T_OWN] = res.results[core]["out"]
    return out


# version 15
# speedup vs baseline: 1.0466x; 1.0271x over previous
import math
import numpy as np
from contextlib import ExitStack

import concourse.bass as bass
import concourse.mybir as mybir
from concourse.bass_utils import run_bass_kernel_spmd

F32 = mybir.dt.float32
BF16 = mybir.dt.bfloat16
AF = mybir.ActivationFunctionType
ALU = mybir.AluOpType
AX = mybir.AxisListType

E = 2048
H = 16
D = 128
C = 128
T_OWN = 1024
T_PRE = 3072
NCH = T_OWN // C
NPC = T_PRE // C
DFF = 8192
NF = E // 128
RMS_EPS = 1e-5
LN_EPS = 1e-5
SY, SC, VE, TE, GP = "sync", "scalar", "vector", "tensor", "gpsimd"
KB = 1024
DBG = []
LAST_RES = None


_R128 = None


def _decay_r128():
    start = 2.0 ** (-(2.0 ** -(math.log2(H) - 3)))
    slopes = np.array([start * start ** i for i in range(H)], dtype=np.float64)
    r = np.exp(-slopes)
    return [float(np.float32(rv ** 128.0)) for rv in r]


def _host_tables():
    start = 2.0 ** (-(2.0 ** -(math.log2(H) - 3)))
    slopes = np.array([start * start ** i for i in range(H)], dtype=np.float64)
    r = np.exp(-slopes)
    c_idx = np.arange(C, dtype=np.float64)
    qdecF = np.zeros((128, E), dtype=np.float64)
    kdec = np.zeros((128, E), dtype=np.float64)
    dmaskT = np.zeros((128, E), dtype=np.float64)
    rCt = np.zeros((128, E), dtype=np.float64)
    for h in range(H):
        qdecF[:, h * 128:(h + 1) * 128] = (r[h] ** (c_idx + 1.0))[None, :]
        kdec[:, h * 128:(h + 1) * 128] = (r[h] ** (C - 1.0 - c_idx))[:, None]
        diff = c_idx[None, :] - c_idx[:, None]
        dm = np.where(diff >= 0, r[h] ** np.maximum(diff, 0.0), 0.0)
        dmaskT[:, h * 128:(h + 1) * 128] = dm
        rCt[:, h * 128:(h + 1) * 128] = r[h] ** float(C)
    freq = (1.0 / 1024.0) ** np.linspace(0.0, 1.0, D // 4, dtype=np.float64)
    return (qdecF.astype(np.float32), kdec.astype(np.float32),
            dmaskT.astype(np.float32), rCt.astype(np.float32), freq)


def _trig(t_abs, freq):
    theta = np.asarray(t_abs, dtype=np.float64)[:, None] * freq[None, :]
    cos = np.tile(np.cos(theta), (1, H)).astype(np.float32)
    sin = np.tile(np.sin(theta), (1, H)).astype(np.float32)
    return np.ascontiguousarray(cos), np.ascontiguousarray(sin)


class Res:
    __slots__ = ("name", "sem", "cnt")

    def __init__(self, name, sem):
        self.name = name
        self.sem = sem
        self.cnt = 0


class Prog:
    CENGS = (SC, VE, TE, GP)

    def __init__(self, nc, ctx):
        self.nc = nc
        self.ctx = ctx
        self.ops = {e: [] for e in (SY, SC, VE, TE, GP)}
        self.csems = {e: ctx.enter_context(nc.semaphore("c_" + e)) for e in self.CENGS}
        self.ccnt = {e: 0 for e in self.CENGS}
        self.waited = {e: {} for e in (SY, SC, VE, TE, GP)}

    def res(self, name):
        return Res(name, self.ctx.enter_context(self.nc.semaphore("r_" + name)))

    def dma(self, eng, out, in_, r):
        r.cnt += 16
        val = r.cnt
        self.ops[eng].append(lambda e, out=out, in_=in_, s=r.sem:
                             e.dma_start(out=out, in_=in_).then_inc(s, 16))
        return ("r_" + r.name, r.sem, val)

    def dmat(self, eng, out, in_, r):
        r.cnt += 16
        val = r.cnt
        self.ops[eng].append(lambda e, out=out, in_=in_, s=r.sem:
                             e.dma_start_transpose(out, in_).then_inc(s, 16))
        return ("r_" + r.name, r.sem, val)

    SELFSYNC = (VE, SC, GP)

    def op(self, eng, fn):
        sem = self.csems[eng]
        if eng in self.SELFSYNC and self.ccnt[eng] > 0:
            prev = self.ccnt[eng]
            self.ops[eng].append(lambda e, sem=sem, v=prev: e.wait_ge(sem, v))
        self.ccnt[eng] += 1
        val = self.ccnt[eng]
        self.ops[eng].append(lambda e, fn=fn, sem=sem: fn(e).then_inc(sem, 1))
        return ("c_" + eng, sem, val)

    def opq(self, eng, fn):
        if eng in self.SELFSYNC:
            self.op(eng, fn)
        else:
            self.ops[eng].append(fn)

    def ctok(self, eng):
        return ("c_" + eng, self.csems[eng], self.ccnt[eng])

    def wait(self, eng, tok):
        if tok is None:
            return
        key, sem, v = tok
        if key == "c_" + eng:
            return
        if self.waited[eng].get(key, 0) >= v:
            return
        self.waited[eng][key] = v
        self.ops[eng].append(lambda e, sem=sem, v=v: e.wait_ge(sem, v))

    def waits(self, eng, toks):
        for t in toks:
            self.wait(eng, t)

    def full(self, r):
        return ("r_" + r.name, r.sem, r.cnt)

    def finalize(self, block):
        for name in (SY, SC, VE, TE, GP):
            ops = self.ops[name]
            getattr(block, name)(lambda e, ops=ops: [f(e) for f in ops])


class Buf:
    def __init__(self):
        self.rd = []
        self.wr = None

    def before_write(self, p, eng):
        p.waits(eng, self.rd)
        if self.wr is not None:
            p.wait(eng, self.wr)

    def wrote(self, tok):
        self.wr = tok
        self.rd = []

    def read(self, tok):
        if tok is not None:
            self.rd.append(tok)


def build_program(ctx):
    global _R128
    _R128 = _decay_r128()
    nc = bass.Bass("TRN2", target_bir_lowering=False, dynamic_dma_scratch_size=16384)

    def din(name, shape, dtype=F32):
        return nc.dram_tensor(name, list(shape), dtype, kind="ExternalInput").ap()

    def dint(name, shape, dtype=BF16):
        kind = "ExternalOutput" if name in DBG else "Internal"
        return nc.dram_tensor(name, list(shape), dtype, kind=kind).ap()

    def sb(name, shape, dtype=F32):
        return ctx.enter_context(nc.sbuf_tensor(name, list(shape), dtype))

    xo = din("xo", (T_OWN, E))
    xp = din("xp", (T_PRE, E))
    wq_d = din("wq", (E, E)); wk_d = din("wk", (E, E))
    wv_d = din("wv", (E, E)); wo_d = din("wo", (E, E))
    w1_d = din("w1", (DFF, E)); w2_d = din("w2", (E, DFF))
    b1c_d = din("b1c", (128, DFF // 128)); b2c_d = din("b2c", (128, NF))
    rmsc_d = din("rmsc", (128, NF))
    cos_o = din("cos_o", (T_OWN, 512)); sin_o = din("sin_o", (T_OWN, 512))
    cos_p = din("cos_p", (T_PRE, 512)); sin_p = din("sin_p", (T_PRE, 512))
    qdecF_d = din("qdecF", (128, E)); kdec_d = din("kdec", (128, E))
    dmaskT_d = din("dmaskT", (128, E))
    out_d = nc.dram_tensor("out", [T_OWN, E], F32, kind="ExternalOutput").ap()

    xn_b = dint("xn_b", (T_OWN, E))
    xnp_b = dint("xnp_b", (T_PRE, E))
    q_b = dint("q_b", (T_OWN, E)); k_b = dint("k_b", (T_OWN, E))
    kd_b = dint("kd_b", (T_OWN, E)); v_b = dint("v_b", (T_OWN, E))
    kdp_b = dint("kdp_b", (T_PRE, E)); vp_b = dint("vp_b", (T_PRE, E))
    w1q_b = dint("w1q_b", (DFF, E)); w2q_b = dint("w2q_b", (E, DFF))
    sg_b = dint("sg_b", (T_OWN, E), F32)
    kvpre_d = dint("kvpre_d", (128, E), F32) if "kvpre_d" in DBG else None
    yn_d = dint("yn_d", (128, NF * 1024)) if "yn_d" in DBG else None
    ysb_d = dint("ysb_d", (T_OWN, E)) if "ysb_d" in DBG else None
    kq_d = dint("kq_d", (2, 128, NF * 512)) if "kq_d" in DBG else None
    pjp_d = dint("pjp_d", (T_PRE, E), F32) if "pjp_d" in DBG else None
    pjr_d = dint("pjr_d", (T_PRE, E), F32) if "pjr_d" in DBG else None
    ksb_d = dint("ksb_d", (128, E)) if "ksb_d" in DBG else None
    kvs_d = dint("kvs_d", (128, 2 * E)) if "kvs_d" in DBG else None
    rct_d = dint("rct_d", (128, E), F32) if "rct_d" in DBG else None
    r_dbg = None

    kdec = sb("kdec_s", (128, E), BF16)
    KV32 = sb("KV32", (128, E))
    KVbf = sb("KVbf", (128, E), BF16)
    kvstr = [sb(f"kvstr{i}", (128, 2 * E), BF16) for i in range(2)]
    g_fm = sb("g_fm", (128, NF, 1024), BF16)
    b2c = sb("b2c_s", (128, NF)); rmsc = sb("rmsc_s", (128, NF))
    b1c = sb("b1c_s", (128, DFF // 128))
    ones128 = sb("ones128", (128, 128), BF16)
    ones128f = sb("ones128f", (128, 128))
    epsln = sb("epsln", (128, 1)); epsrms = sb("epsrms", (128, 1))
    stat = sb("stat", (128, 16))
    part1 = sb("part1", (128, 64)); part2 = sb("part2", (128, 64))
    pred = sb("pred", (128, 2)); sc12 = sb("sc12", (128, 2)); inv12 = sb("inv12", (128, 2))

    base = (nc.sbuf_base + 31) & ~31
    SB_TOP = 229344
    R1, R2, R3 = base, base + 32 * KB, base + 96 * KB
    need = R3 + 40 * KB
    assert need <= SB_TOP, f"scratch overflow: need {need}, top {SB_TOP}"

    def at(name, off, shape, dtype=F32):
        return nc.alloc_sbuf_tensor_at(name, list(shape), dtype, offset=off)

    xn_fm = at("xn_fm", R1, (128, NF, 1024), BF16)
    Yn_all = at("Yn_all", R1, (128, NF, 1024), BF16)
    xbuf = [at(f"xbuf{i}", R2 + i * 8 * KB, (128, E)) for i in range(2)]
    xnbf = [at(f"xnbf{i}", R2 + 16 * KB + i * 4 * KB, (128, E), BF16) for i in range(2)]
    junk = at("junk", R2 + 24 * KB, (128, E))
    wrs = at("wrs", R2, (128, NF, E), BF16)
    g_addr = nc.lookup_mloc("g_fm").addr
    qw32 = [at(f"qw32{i}", g_addr + i * 8 * KB, (128, E)) for i in range(2)]
    junkq = at("junkq", g_addr + 16 * KB, (128, E))
    qout = [at(f"qout{i}", g_addr + 24 * KB + i * 4 * KB, (128, E), BF16) for i in range(2)]
    qmsk = at("qmsk", g_addr + 16 * KB, (128, 2 * E), BF16)
    g32 = at("g32", R2, (128, NF, 1024))
    k4 = at("k4", R2, (128, NF, 512), BF16)
    q4 = at("q4", R2 + 16 * KB, (128, NF, 512), BF16)
    wo_sb = at("wo_sb", R2, (128, NF, E), BF16)
    xnp_g = at("xnp_g", R3, (128, NF, 256), BF16)
    pj32 = [at(f"pj32{i}", R3 + 8 * KB + i * 8 * KB, (128, E)) for i in range(2)]
    trigc = [at(f"trigc{i}", R3 + 24 * KB + i * 2 * KB, (128, 512)) for i in range(2)]
    trigs = [at(f"trigs{i}", R3 + 28 * KB + i * 2 * KB, (128, 512)) for i in range(2)]
    rotmp = at("rotmp", R3 + 32 * KB, (128, 1024))
    h_sp = at("h_sp", R3, (128, 4, 1024), BF16)
    wtile_sp = at("wtile_sp", R3 + 8 * KB, (128, 16, 512), BF16)
    w2T = at("w2T", R3 + 24 * KB, (128, 4, 2048), BF16)
    Y_sb = at("Y_sb", R3, (128, NF, 128), BF16)
    Ysq = at("Ysq", R3 + 4 * KB, (128, NF, 128), BF16)
    qtile = at("qtile", R3 + 8 * KB, (128, 128), BF16)
    P_bf = at("P_bf", R3 + 8 * KB + 256, (128, 128), BF16)
    invb_bf = at("invb_bf", R3 + 8 * KB + 512, (128, 128), BF16)
    P_bfB = at("P_bfB", R3 + 8 * KB + 768, (128, 128), BF16)
    qtileB = at("qtileB", R3 + 18 * KB, (128, 128), BF16)
    invb = at("invb", R3 + 9 * KB, (128, 128))
    rtmp = at("rtmp", R3 + 9 * KB + 512, (128, 128))
    dmaskT = at("dmaskT_s", R3 + 10 * KB, (128, E), BF16)
    qdecF = at("qdecF_s", R3 + 14 * KB, (128, E), BF16)
    sgout = at("sgout", R3, (128, 128))
    ident = at("ident", R3 + KB, (128, 128))
    iota_p = at("iota_p", R3 + 2 * KB, (128, 128))
    iota_f = at("iota_f", R3 + 3 * KB, (128, 128))
    sgld = at("sgld", R3 + 4 * KB, (128, E))
    xb2 = at("xb2", R3 + 12 * KB, (128, E))

    ps = ctx.enter_context(nc.psum_tensor("ps", [128, 4096], F32))

    def bank(i, w=512):
        return ps[:, i * 512:i * 512 + w]

    p = Prog(nc, ctx)

    r_tbl = p.res("tbl")
    r_tblg = p.res("tblg")
    r_x = [p.res("x0"), p.res("x1")]
    r_trig = [p.res("tg0"), p.res("tg1")]
    r_qw = [p.res("qw0"), p.res("qw1")]
    r_kvld = [p.res("kv0"), p.res("kv1")]
    r_wt = [p.res("wt0"), p.res("wt1")]
    r_w2s = [p.res("w2s0")]
    r_wrs = p.res("wrsl")
    r_xnp = p.res("xnpl")
    r_xnfm = p.res("xnfm")
    r_kq = p.res("kq")
    r_wo = p.res("wol")
    r_stbl = p.res("stbl")
    r_fin = p.res("fin")
    w_xno = [p.res("wxno0"), p.res("wxno1")]
    w_xnp = [p.res("wxnp0"), p.res("wxnp1")]
    w_kv = [p.res("wkv0"), p.res("wkv1")]
    w_kk = [p.res("wkk0"), p.res("wkk1")]
    w_qw = [p.res("wqw0"), p.res("wqw1")]
    w_sg = p.res("wsg")
    w_out = p.res("wout")
    r_pjd = p.res("pjd") if (pjp_d is not None or pjr_d is not None) else None
    r_kvs = p.res("kvs") if kvs_d is not None else None

    p.dma(GP, kdec[:, :], kdec_d, r_tblg)
    p.dma(SY, b2c[:, :], b2c_d, r_tbl)
    p.dma(SY, rmsc[:, :], rmsc_d, r_tbl)
    p.dma(SY, b1c[:, :], b1c_d, r_tbl)
    t_tbl = p.full(r_tbl)
    t_kdec = p.full(r_tblg)
    if ksb_d is not None:
        r_ksb = p.res("ksb")
        p.wait(SY, t_kdec)
        p.dma(SY, ksb_d, kdec[:, :], r_ksb)

    def init_fn(v):
        v.memset(KV32[:, :], 0.0)
        v.memset(ones128[:, :], 1.0)
        v.memset(ones128f[:, :], 1.0)
        v.memset(epsln[:, 0:1], LN_EPS)
        return v.memset(epsrms[:, 0:1], RMS_EPS)
    p.op(VE, init_fn)

    B_x = [Buf(), Buf()]
    B_xnbf = [Buf(), Buf()]
    B_trig = [Buf(), Buf()]
    B_pj = [Buf(), Buf()]
    B_kv = [Buf(), Buf()]
    B_qw = [Buf(), Buf()]
    B_qo = [Buf(), Buf()]
    B_wt = [Buf(), Buf()]
    B_stat = [Buf(), Buf()]

    t_sent = None
    if "sent" in DBG:
        r_sent = p.res("sent")
        tms = p.op(VE, lambda v: v.memset(kvstr[0][:, 0:E], 7.0))
        p.wait(SY, tms)
        for j in range(NPC):
            p.dma(SY, kdp_b[j * 128:(j + 1) * 128, :], kvstr[0][:, 0:E], r_sent)
        t_sent = p.full(r_sent)
        B_kv[0].read(t_sent)
        p.waits(GP, [t_sent])

    w1_tiles = [w1_d[i * 128:(i + 1) * 128, :] for i in range(64)]
    w2_tiles = [w2_d[(i // 4) * 128:(i // 4 + 1) * 128, (i % 4) * 2048:(i % 4 + 1) * 2048]
                for i in range(64)]
    w1q_tiles = [w1q_b[i * 128:(i + 1) * 128, :] for i in range(64)]
    w2q_tiles = [w2q_b[(i // 4) * 128:(i // 4 + 1) * 128, (i % 4) * 2048:(i % 4 + 1) * 2048]
                 for i in range(64)]
    qred_jobs = [(t, part1, i) for i, t in enumerate(w1_tiles)] + \
                [(t, part2, i) for i, t in enumerate(w2_tiles)]
    qred_pos = [0]

    def quant_reduce_step(nsteps):
        for _ in range(nsteps):
            i = qred_pos[0]
            if i >= len(qred_jobs):
                return
            qred_pos[0] = i + 1
            tile, part, col = qred_jobs[i]
            slot = i % 2
            bq = B_qw[slot]
            bq.before_write(p, SY)
            tl = p.dma(SY, qw32[slot][:, :], tile, r_qw[slot])
            bq.wrote(tl)
            if i % 2 == 0:
                p.wait(VE, tl)
                tr = p.op(VE, lambda v, slot=slot, part=part, col=col:
                          v.scalar_tensor_tensor(qw32[slot][:, :], qw32[slot][:, :], -1.0,
                                                 qw32[slot][:, :], ALU.mult, ALU.max,
                                                 accum_out=part[:, col:col + 1]))
            else:
                p.wait(SC, tl)
                tr = p.op(SC, lambda s, slot=slot, part=part, col=col:
                          s.activation(qw32[slot][:, :], qw32[slot][:, :], AF.Abs,
                                       accum_out=part[:, col:col + 1]))
            bq.read(tr)

    w2_order = [r * 4 + cb for cb in range(4) for r in range(16)]
    qwr_jobs = list(zip(w1_tiles, w1q_tiles, [0] * 64)) + \
               [(w2_tiles[i], w2q_tiles[i], 1) for i in w2_order]
    qwr_pos = [0]
    qwr_tok = [None] * len(qwr_jobs)
    tq0_box = [None]
    last_tqo = [None]

    def quant_write_step(nsteps):
        tq0 = tq0_box[0]
        if tq0 is None:
            return
        for _ in range(nsteps):
            i = qwr_pos[0]
            if i >= len(qwr_jobs):
                return
            qwr_pos[0] = i + 1
            tile, dst, which = qwr_jobs[i]
            inv_ap = inv12[:, which:which + 1]
            wres = w_qw[i % 2]
            slot = i % 2
            bq = B_qw[slot]
            bq.before_write(p, SY)
            tl = p.dma(SY, qw32[slot][:, :], tile, r_qw[slot])
            bq.wrote(tl)
            p.waits(VE, [tl, tq0])
            p.opq(VE, lambda v, slot=slot, inv_ap=inv_ap:
                  v.tensor_scalar(qmsk[:, 0:E], qw32[slot][:, :], inv_ap, 0.5,
                                  ALU.mult, ALU.is_ge))
            tm2 = p.op(VE, lambda v, slot=slot, inv_ap=inv_ap:
                       v.tensor_scalar(qmsk[:, E:2 * E], qw32[slot][:, :], inv_ap, -0.5,
                                       ALU.mult, ALU.is_le))
            bq.read(tm2)
            bo = B_qo[slot]
            bo.before_write(p, VE)
            tqo = p.op(VE, lambda v, slot=slot: v.tensor_tensor(qout[slot][:, :], qmsk[:, 0:E],
                                                                qmsk[:, E:2 * E], ALU.subtract))
            bo.wrote(tqo)
            last_tqo[0] = tqo
            p.wait(GP, tqo)
            tw = p.dma(GP, dst, qout[slot][:, :], wres)
            qwr_tok[i] = tw
            bo.read(tw)

    def emit_ln(src, dst_b, w_res, idx, slot):
        bx = B_x[slot]
        bx.before_write(p, SY)
        tl = p.dma(SY, xbuf[slot][:, :], src[idx * 128:(idx + 1) * 128, :], r_x[slot])
        bx.wrote(tl)
        x_ap = xbuf[slot][:, :]
        st8 = 8 * slot
        p.wait(VE, tl)
        B_stat[slot].before_write(p, VE)
        p.opq(VE, lambda v: v.tensor_scalar(junk[:, :], x_ap, 1.0, 0.0, ALU.mult, ALU.add,
                                            accum_out=stat[:, st8 + 0:st8 + 1]))
        p.opq(VE, lambda v: v.scalar_tensor_tensor(junk[:, :], x_ap, 1.0, x_ap,
                                                   ALU.mult, ALU.mult, accum_out=stat[:, st8 + 1:st8 + 2]))
        p.opq(VE, lambda v: v.tensor_scalar(stat[:, st8 + 2:st8 + 3], stat[:, st8 + 0:st8 + 1], 1.0 / E, 0.0, ALU.mult, ALU.add))
        p.opq(VE, lambda v: v.scalar_tensor_tensor(stat[:, st8 + 3:st8 + 4], stat[:, st8 + 2:st8 + 3], -1.0, stat[:, st8 + 2:st8 + 3],
                                                   ALU.mult, ALU.mult))
        tv = p.op(VE, lambda v: v.scalar_tensor_tensor(stat[:, st8 + 4:st8 + 5], stat[:, st8 + 1:st8 + 2], 1.0 / E,
                                                       stat[:, st8 + 3:st8 + 4], ALU.mult, ALU.add))
        p.wait(SC, tv)
        tsq = p.op(SC, lambda s: s.activation(stat[:, st8 + 5:st8 + 6], stat[:, st8 + 4:st8 + 5], AF.Sqrt,
                                              bias=epsln[:, 0:1], scale=1.0))
        p.wait(VE, tsq)
        p.opq(VE, lambda v: v.reciprocal(stat[:, st8 + 6:st8 + 7], stat[:, st8 + 5:st8 + 6]))
        tnm = p.op(VE, lambda v: v.scalar_tensor_tensor(stat[:, st8 + 7:st8 + 8], stat[:, st8 + 2:st8 + 3], -1.0,
                                                        stat[:, st8 + 6:st8 + 7], ALU.mult, ALU.mult))
        bn = B_xnbf[slot]
        bn.before_write(p, SC)
        p.wait(SC, tnm)
        txn = p.op(SC, lambda s: s.activation(xnbf[slot][:, :], x_ap, AF.Identity,
                                              bias=stat[:, st8 + 7:st8 + 8], scale=stat[:, st8 + 6:st8 + 7]))
        B_stat[slot].wrote(txn)
        bx.read(txn)
        bn.wrote(txn)
        p.wait(GP, txn)
        tw = p.dma(GP, dst_b[idx * 128:(idx + 1) * 128, :], xnbf[slot][:, :], w_res)
        bn.read(tw)
        return tw

    for i in range(NCH):
        emit_ln(xo, xn_b, w_xno[i % 2], i, i % 2)
        quant_reduce_step(1)
    for i in range(NPC):
        emit_ln(xp, xnp_b, w_xnp[i % 2], i, i % 2)
        quant_reduce_step(2)
    t_xno_all = [p.full(w_xno[0]), p.full(w_xno[1])]
    t_xnp_all = [p.full(w_xnp[0]), p.full(w_xnp[1])]

    p.waits(SC, t_xno_all)
    for f in range(NF):
        p.dmat(SC, xn_fm[:, f, :], xn_b[0:T_OWN, f * 128:(f + 1) * 128], r_xnfm)
    t_xnfm_all = p.full(r_xnfm)

    B_wrs = Buf()

    def load_W(w_dram):
        B_wrs.before_write(p, GP)
        for f in range(NF):
            p.dma(GP, wrs[:, f, :], w_dram[f * 128:(f + 1) * 128, :], r_wrs)
        tok = p.full(r_wrs)
        B_wrs.wrote(tok)
        return tok

    pbank = {}

    def proj_chunk(xn_t, banks, out_write, wait_toks):
        etoks = []
        for n in range(4):
            bk_i = banks[n % len(banks)]
            bk = bank(bk_i)
            p.waits(TE, wait_toks)
            if pbank.get(bk_i) is not None:
                p.wait(TE, pbank[bk_i])
            tmm = None
            for f in range(NF):
                A = xn_t(f)
                B = wrs[:, f, n * 512:(n + 1) * 512]
                st = (f == 0); sp = (f == NF - 1)
                if sp:
                    tmm = p.op(TE, lambda t, A=A, B=B, bk=bk, st=st: t.matmul(bk, A, B, start=st, stop=True))
                else:
                    p.opq(TE, lambda t, A=A, B=B, bk=bk, st=st: t.matmul(bk, A, B, start=st, stop=False))
            p.wait(SC, tmm)
            te = out_write(n, bk)
            pbank[bk_i] = te
            etoks.append(te)
        return etoks

    def head3d(t, base_c, width=32):
        return t[:, :].rearrange("p (h d) -> p h d", d=128)[:, :, base_c:base_c + width]

    def rep3d(t, width=32):
        return t[:, :].rearrange("p (h d) -> p h d", d=32)[:, :, 0:width]

    def emit_rotary(dst32, cosb, sinb):
        x1 = head3d(dst32, 0); x2 = head3d(dst32, 64)
        tm1 = rotmp[:, 0:512].rearrange("p (h d) -> p h d", d=32)
        tm2 = rotmp[:, 512:1024].rearrange("p (h d) -> p h d", d=32)
        cs = rep3d(cosb); sn = rep3d(sinb)
        p.opq(VE, lambda v: v.tensor_tensor(tm1, x2, sn, ALU.mult))
        p.opq(VE, lambda v: v.tensor_tensor(tm2, x1, sn, ALU.mult))
        p.opq(VE, lambda v: v.tensor_tensor(x2, x2, cs, ALU.mult))
        p.opq(VE, lambda v: v.tensor_tensor(x2, x2, tm2, ALU.subtract))
        p.opq(VE, lambda v: v.tensor_tensor(x1, x1, cs, ALU.mult))
        return p.op(VE, lambda v: v.tensor_tensor(x1, x1, tm1, ALU.add))

    def load_trig(slot, cos_src, sin_src, j):
        bt = B_trig[slot]
        bt.before_write(p, SY)
        p.dma(SY, trigc[slot][:, :], cos_src[j * 128:(j + 1) * 128, :], r_trig[slot])
        p.dma(SY, trigs[slot][:, :], sin_src[j * 128:(j + 1) * 128, :], r_trig[slot])
        t2 = p.full(r_trig[slot])
        bt.wrote(t2)
        return t2

    def evac_pj(slot):
        def w(n, bk):
            if n == 0:
                B_pj[slot].before_write(p, SC)
            return p.op(SC, lambda s, bk=bk, n=n: s.copy(pj32[slot][:, n * 512:(n + 1) * 512], bk))
        return w

    p.waits(GP, t_xno_all + t_xnp_all)
    t_wk = load_W(wk_d)
    wrs_readers = []

    def kproj(xn_t, slot, wait_toks, trig_tok, bounce_kd, bounce_k, dbg_j=None):
        et = proj_chunk(xn_t, (0, 1, 2, 3), evac_pj(slot), wait_toks)
        B_pj[slot].wrote(et[-1])
        wrs_readers.append(et[-1])
        if dbg_j is not None and pjp_d is not None:
            p.waits(GP, et)
            tdp = p.dma(GP, pjp_d[dbg_j * 128:(dbg_j + 1) * 128, :], pj32[slot][:, :], r_pjd)
            B_pj[slot].read(tdp)
            p.waits(VE, [tdp])
        p.waits(VE, et + [trig_tok, t_kdec])
        trot = emit_rotary(pj32[slot], trigc[slot], trigs[slot])
        B_trig[slot].read(trot)
        if dbg_j is not None and pjr_d is not None:
            p.wait(GP, trot)
            tdr = p.dma(GP, pjr_d[dbg_j * 128:(dbg_j + 1) * 128, :], pj32[slot][:, :], r_pjd)
            B_pj[slot].read(tdr)
        bkv = B_kv[slot]
        bkv.before_write(p, VE)
        tkr = p.op(VE, lambda v, slot=slot: v.tensor_copy(kvstr[slot][:, E:2 * E], pj32[slot][:, :]))
        tkd = p.op(VE, lambda v, slot=slot: v.tensor_tensor(kvstr[slot][:, 0:E],
                                                            kvstr[slot][:, E:2 * E],
                                                            kdec[:, :], ALU.mult))
        toks = [tkr, tkd]
        B_pj[slot].read(toks[-1])
        bkv.wrote(toks[-1])
        if kvs_d is not None and dbg_j == 22:
            p.waits(SY, toks)
            tks = p.dma(SY, kvs_d, kvstr[slot][:, :], r_kvs)
            bkv.read(tks)
        p.waits(SY, toks)
        p.dma(SY, bounce_kd, kvstr[slot][:, 0:E], w_kk[slot])
        if bounce_k is not None:
            p.dma(SY, bounce_k, kvstr[slot][:, E:2 * E], w_kk[slot])
        bkv.read(p.full(w_kk[slot]))

    for j in range(NCH):
        slot = j % 2
        tt = load_trig(slot, cos_o, sin_o, j)
        kproj(lambda f, j=j: xn_fm[:, f, j * 128:(j + 1) * 128], slot,
              [t_wk, t_xnfm_all], tt, kd_b[j * 128:(j + 1) * 128, :],
              k_b[j * 128:(j + 1) * 128, :])
        quant_reduce_step(3)

    B_xnp = Buf()
    for g in range(T_PRE // 256):
        B_xnp.before_write(p, SC)
        p.waits(SC, t_xnp_all)
        for f in range(NF):
            p.dmat(SC, xnp_g[:, f, :], xnp_b[g * 256:(g + 1) * 256, f * 128:(f + 1) * 128], r_xnp)
        tg = p.full(r_xnp)
        B_xnp.wrote(tg)
        for jj in range(2):
            j = g * 2 + jj
            slot = jj % 2
            tt = load_trig(slot, cos_p, sin_p, j)
            kproj(lambda f, jj=jj: xnp_g[:, f, jj * 128:(jj + 1) * 128], slot,
                  [t_wk, tg], tt, kdp_b[j * 128:(j + 1) * 128, :], None, dbg_j=j)
            B_xnp.read(wrs_readers[-1])
            quant_reduce_step(2)


    quant_reduce_step(len(qred_jobs))
    p.waits(VE, [p.ctok(SC)])
    p.opq(VE, lambda v: v.tensor_reduce(pred[:, 0:1], part1[:, :], AX.X, ALU.add))
    tpr = p.op(VE, lambda v: v.tensor_reduce(pred[:, 1:2], part2[:, :], AX.X, ALU.add))
    p.waits(TE, [tpr])
    if pbank.get(2) is not None:
        p.wait(TE, pbank[2])
    tsum = p.op(TE, lambda t: t.matmul(ps[:, 1024:1026], ones128f[:, :], pred[:, 0:2],
                                       start=True, stop=True))
    p.wait(VE, tsum)
    NW = float(DFF * E)
    p.opq(VE, lambda v: v.tensor_scalar(sc12[:, 0:2], ps[:, 1024:1026], 1.0 / NW, 1e-8,
                                        ALU.mult, ALU.max))
    tq0_box[0] = p.op(VE, lambda v: v.reciprocal(inv12[:, 0:2], sc12[:, 0:2]))
    pbank[2] = tq0_box[0]

    p.waits(GP, wrs_readers)
    for t in wrs_readers:
        B_wrs.read(t)
    t_wv = load_W(wv_d)
    wrs_readers = []

    def evac_kv(slot):
        def w(n, bk):
            if n == 0:
                B_kv[slot].before_write(p, SC)
            return p.op(SC, lambda s, bk=bk, n=n, slot=slot:
                        s.copy(kvstr[slot][:, n * 512:(n + 1) * 512], bk))
        return w

    def vproj(xn_t, banks, slot, wait_toks, dst):
        et = proj_chunk(xn_t, banks, evac_kv(slot), wait_toks)
        wrs_readers.append(et[-1])
        B_kv[slot].wrote(et[-1])
        p.waits(GP, et)
        p.dma(GP, dst, kvstr[slot][:, 0:E], w_kv[slot])
        tw = p.full(w_kv[slot])
        B_kv[slot].read(tw)
        return tw

    for j in range(NCH):
        vproj(lambda f, j=j: xn_fm[:, f, j * 128:(j + 1) * 128],
              (0, 1, 2, 3), j % 2, [t_wv], v_b[j * 128:(j + 1) * 128, :])
        quant_write_step(2)
    for g in range(T_PRE // 256):
        B_xnp.before_write(p, SC)
        for f in range(NF):
            p.dmat(SC, xnp_g[:, f, :], xnp_b[g * 256:(g + 1) * 256, f * 128:(f + 1) * 128], r_xnp)
        tg = p.full(r_xnp)
        B_xnp.wrote(tg)
        for jj in range(2):
            j = g * 2 + jj
            tw = vproj(lambda f, jj=jj: xnp_g[:, f, jj * 128:(jj + 1) * 128],
                       (0, 1, 2, 3), jj % 2, [t_wv, tg], vp_b[j * 128:(j + 1) * 128, :])
            B_xnp.read(tw)
            quant_write_step(2)

    p.waits(GP, wrs_readers)
    for t in wrs_readers:
        B_wrs.read(t)
    t_wq = load_W(wq_d)
    wrs_readers = []
    for j in range(NCH):
        slot = j % 2
        tt = load_trig(slot, cos_o, sin_o, j)
        et = proj_chunk(lambda f, j=j: xn_fm[:, f, j * 128:(j + 1) * 128],
                        (0, 1, 2, 3), evac_pj(slot), [t_wq])
        B_pj[slot].wrote(et[-1])
        wrs_readers.append(et[-1])
        p.waits(VE, et + [tt])
        trot = emit_rotary(pj32[slot], trigc[slot], trigs[slot])
        B_trig[slot].read(trot)
        bkv = B_kv[slot]
        bkv.before_write(p, VE)
        tqr = p.op(VE, lambda v, slot=slot: v.tensor_copy(kvstr[slot][:, 0:E], pj32[slot][:, :]))
        B_pj[slot].read(tqr)
        bkv.wrote(tqr)
        p.wait(GP, tqr)
        p.dma(GP, q_b[j * 128:(j + 1) * 128, :], kvstr[slot][:, 0:E], w_kv[slot])
        bkv.read(p.full(w_kv[slot]))
        quant_write_step(2)

    t_bounce_all = [p.full(w_kv[0]), p.full(w_kv[1]), p.full(w_kk[0]), p.full(w_kk[1])]
    t_kv_upd = None
    t_kv_par = [None, None]
    for b_ in range(4):
        if pbank.get(b_) is not None:
            p.wait(TE, pbank[b_])
            pbank[b_] = None
    for l in range(NPC):
        slot = l % 2
        base = 2048 * (l % 2)
        bkv = B_kv[slot]
        bkv.before_write(p, SY)
        p.waits(SY, t_bounce_all)
        p.dma(SY, kvstr[slot][:, 0:E], kdp_b[l * 128:(l + 1) * 128, :], r_kvld[slot])
        p.dma(SY, kvstr[slot][:, E:2 * E], vp_b[l * 128:(l + 1) * 128, :], r_kvld[slot])
        tv = p.full(r_kvld[slot])
        bkv.wrote(tv)
        p.waits(TE, [tv])
        if t_kv_par[l % 2] is not None:
            p.wait(TE, t_kv_par[l % 2])
        tdel = None
        for h in range(H):
            A = kvstr[slot][:, h * 128:(h + 1) * 128]
            B = kvstr[slot][:, E + h * 128:E + (h + 1) * 128]
            bk = ps[:, base + h * 128:base + (h + 1) * 128]
            if h == H - 1:
                tdel = p.op(TE, lambda t, A=A, B=B, bk=bk: t.matmul(bk, A, B, start=True, stop=True))
            else:
                p.opq(TE, lambda t, A=A, B=B, bk=bk: t.matmul(bk, A, B, start=True, stop=True))
        bkv.read(tdel)
        p.waits(VE, [tdel, t_tbl])
        for hx in range(H):
            p.opq(VE, lambda v, hx=hx: v.tensor_scalar(
                KV32[:, hx * 128:(hx + 1) * 128], KV32[:, hx * 128:(hx + 1) * 128],
                _R128[hx], 0.0, ALU.mult, ALU.add))
        for bq_ in range(4):
            t_kv_upd = p.op(VE, lambda v, b=bq_, base=base: v.tensor_tensor(
                KV32[:, b * 512:(b + 1) * 512], KV32[:, b * 512:(b + 1) * 512],
                ps[:, base + b * 512:base + (b + 1) * 512], ALU.add))
        t_kv_par[l % 2] = t_kv_upd
        quant_write_step(1)

    gb = [None] * 8
    p.wait(TE, t_kv_upd)
    p.waits(SC, [p.ctok(VE)])
    B_wts = Buf()
    B_w2s = Buf()
    t_gadd = None
    g_last = None
    for sp_i in range(16):
        B_wts.before_write(p, SC)
        p.waits(SC, [qwr_tok[4 * sp_i + 2], qwr_tok[4 * sp_i + 3]])
        for f in range(NF):
            p.dmat(SC, wtile_sp[:, f, :],
                   w1q_b[sp_i * 512:(sp_i + 1) * 512, f * 128:(f + 1) * 128], r_wt[0])
        t_wts = p.full(r_wt[0])
        B_wts.wrote(t_wts)
        B_w2s.before_write(p, SC)
        jlast = 64 + 16 * (sp_i // 4) + 15
        p.waits(SC, [qwr_tok[jlast - 1], qwr_tok[jlast]])
        for fo4 in range(4):
            for dp in range(2):
                p.dmat(SC, w2T[:, fo4, dp * 1024:(dp + 1) * 1024],
                       w2q_b[dp * 1024:(dp + 1) * 1024,
                             (sp_i * 4 + fo4) * 128:(sp_i * 4 + fo4 + 1) * 128], r_w2s[0])
        t_w2 = p.full(r_w2s[0])
        B_w2s.wrote(t_w2)
        p.waits(TE, [t_wts, t_xnfm_all])
        for b in range(8):
            if gb[b] is not None:
                p.wait(TE, gb[b])
            if sp_i == 0 and pbank.get(b) is not None:
                p.wait(TE, pbank[b])
                pbank[b] = None
        htok = None
        for f in range(NF):
            for fo4 in range(4):
                A = wtile_sp[:, f, fo4 * 128:(fo4 + 1) * 128]
                for th in range(2):
                    bk = bank(fo4 * 2 + th)
                    B = xn_fm[:, f, th * 512:(th + 1) * 512]
                    st = (f == 0); spx = (f == NF - 1)
                    if spx and fo4 == 3 and th == 1:
                        htok = p.op(TE, lambda t, A=A, B=B, bk=bk, st=st, spx=spx:
                                    t.matmul(bk, A, B, start=st, stop=spx))
                    else:
                        p.opq(TE, lambda t, A=A, B=B, bk=bk, st=st, spx=spx:
                              t.matmul(bk, A, B, start=st, stop=spx))
        B_wts.read(htok)
        p.waits(SC, [htok, t_tbl])
        if g_last is not None:
            p.wait(SC, g_last)
        ev_last = None
        for fo4 in range(4):
            for th in range(2):
                ev_last = p.op(SC, lambda s, fo4=fo4, th=th, sp_i=sp_i:
                               s.activation(h_sp[:, fo4, th * 512:(th + 1) * 512],
                                            bank(fo4 * 2 + th), AF.Gelu,
                                            bias=b1c[:, sp_i * 4 + fo4:sp_i * 4 + fo4 + 1],
                                            scale=1.0))
        for m in range(32):
            db, th, b = m // 2, m % 2, m % 8
            bk = bank(b)
            p.waits(TE, [ev_last, t_w2])
            if gb[b] is not None:
                p.wait(TE, gb[b])
            tmm = None
            for fo4 in range(4):
                A = w2T[:, fo4, db * 128:(db + 1) * 128]
                B = h_sp[:, fo4, th * 512:(th + 1) * 512]
                st = (fo4 == 0)
                if fo4 == 3:
                    tmm = p.op(TE, lambda t, A=A, B=B, bk=bk, st=st:
                               t.matmul(bk, A, B, start=st, stop=True))
                else:
                    p.opq(TE, lambda t, A=A, B=B, bk=bk, st=st:
                          t.matmul(bk, A, B, start=st, stop=False))
            p.wait(VE, tmm)
            if sp_i == 0:
                t_gadd = p.op(VE, lambda v, db=db, th=th, bk=bk:
                              v.tensor_copy(g32[:, db, th * 512:(th + 1) * 512], bk))
            else:
                t_gadd = p.op(VE, lambda v, db=db, th=th, bk=bk:
                              v.tensor_tensor(g32[:, db, th * 512:(th + 1) * 512],
                                              g32[:, db, th * 512:(th + 1) * 512], bk, ALU.add))
            gb[b] = t_gadd
        g_last = tmm
        B_w2s.read(g_last)
        quant_write_step(2)
    t_q_done = [p.full(w_qw[0]), p.full(w_qw[1]), p.full(r_qw[0]), p.full(r_qw[1]), last_tqo[0]]

    silu_tok = [None] * 32
    p.waits(SC, [t_gadd, t_tbl] + t_q_done)
    for d in range(NF):
        for half in range(2):
            silu_tok[d * 2 + half] = p.op(SC, lambda s, d=d, half=half:
                                          s.activation(g_fm[:, d, half * 512:(half + 1) * 512],
                                                       g32[:, d, half * 512:(half + 1) * 512],
                                                       AF.Silu, bias=b2c[:, d:d + 1], scale=1.0))
    last_mm_half = silu_tok[31]
    gbank = gb

    ffn_gate = [t for t in silu_tok if t is not None]
    p.waits(GP, ffn_gate)
    p.waits(TE, [t for t in gbank if t is not None])
    p.dma(GP, dmaskT[:, :], dmaskT_d, r_stbl)
    p.dma(GP, qdecF[:, :], qdecF_d, r_stbl)
    t_stbl = p.full(r_stbl)
    p.wait(VE, t_kv_upd)
    t_kvbf = p.op(VE, lambda v: v.tensor_copy(KVbf[:, :], KV32[:, :]))
    if kvpre_d is not None:
        r_dbg = p.res("dbg")
        p.wait(GP, t_kvbf)
        p.dma(GP, kvpre_d, KV32[:, :], r_dbg)

    B_kq = Buf()
    last_tsr = None
    last_yn = None
    scoreP_par = [None, None]
    yev_par = [None, None]
    tYp = [None, None]
    P_pair = [P_bf, P_bfB]
    qt_pair = [qtile, qtileB]
    for grp in range(2):
        B_kq.before_write(p, SY)
        B_kq.before_write(p, SC)
        p.waits(SY, ffn_gate + t_bounce_all)
        p.waits(SC, ffn_gate + t_bounce_all)
        for f in range(NF):
            p.dmat(SC, k4[:, f, :], k_b[grp * 512:(grp + 1) * 512, f * 128:(f + 1) * 128], r_kq)
            p.dmat(SC, q4[:, f, :], q_b[grp * 512:(grp + 1) * 512, f * 128:(f + 1) * 128], r_kq)
        t_kq_all = p.full(r_kq)
        B_kq.wrote(t_kq_all)
        if kq_d is not None and grp == 0:
            if r_dbg is None:
                r_dbg = p.res("dbg")
            p.wait(GP, t_kq_all)
            p.dma(GP, kq_d[0], k4[:, :, :].rearrange("p a b -> p (a b)"), r_dbg)
            p.dma(GP, kq_d[1], q4[:, :, :].rearrange("p a b -> p (a b)"), r_dbg)
        for jj in range(4):
            j = grp * 4 + jj
            slot = j % 2
            bkv = B_kv[slot]
            bkv.before_write(p, SY)
            p.dma(SY, kvstr[slot][:, 0:E], kd_b[j * 128:(j + 1) * 128, :], r_kvld[slot])
            p.dma(SY, kvstr[slot][:, E:2 * E], v_b[j * 128:(j + 1) * 128, :], r_kvld[slot])
            tv = p.full(r_kvld[slot])
            bkv.wrote(tv)
            t_yev = []
            tdel = None
            tS_tok = [None] * H

            def emit_S(h, jj=jj):
                par = h % 2
                bkS = ps[:, 0:128] if par == 0 else ps[:, 1024:1152]
                p.waits(TE, [t_kq_all])
                if scoreP_par[par] is not None:
                    p.wait(TE, scoreP_par[par])
                if par == 1 and last_tsr is not None:
                    p.wait(TE, last_tsr)
                tS_tok[h] = p.op(TE, lambda t, h=h, jj=jj, bkS=bkS:
                                 t.matmul(bkS, k4[:, h, jj * 128:(jj + 1) * 128],
                                          q4[:, h, jj * 128:(jj + 1) * 128],
                                          start=True, stop=True))

            emit_S(0)
            for h in range(H):
                par = h % 2
                bkS = ps[:, 0:128] if par == 0 else ps[:, 1024:1152]
                if h + 1 < H:
                    emit_S(h + 1)
                Pb = P_pair[par]
                qt = qt_pair[par]
                p.waits(VE, [tS_tok[h], t_stbl])
                if tYp[par] is not None:
                    p.wait(VE, tYp[par])
                scoreP_par[par] = p.op(VE, lambda v, h=h, bkS=bkS, Pb=Pb:
                                       v.tensor_tensor(Pb[:, :], bkS,
                                                       dmaskT[:, h * 128:(h + 1) * 128],
                                                       ALU.mult))
                tqd = p.op(VE, lambda v, h=h, jj=jj, qt=qt:
                           v.tensor_tensor(qt[:, :], q4[:, h, jj * 128:(jj + 1) * 128],
                                           qdecF[:, h * 128:(h + 1) * 128], ALU.mult))
                bkY = ps[:, 512:640] if par == 0 else ps[:, 1536:1664]
                p.waits(TE, [scoreP_par[par], tqd, tv, t_kvbf])
                if yev_par[par] is not None:
                    p.wait(TE, yev_par[par])
                p.opq(TE, lambda t, h=h, slot=slot, bkY=bkY, Pb=Pb:
                      t.matmul(bkY, kvstr[slot][:, E + h * 128:E + (h + 1) * 128], Pb[:, :],
                               start=True, stop=False))
                tY = p.op(TE, lambda t, h=h, bkY=bkY, qt=qt:
                          t.matmul(bkY, KVbf[:, h * 128:(h + 1) * 128], qt[:, :],
                                   start=False, stop=True))
                tYp[par] = tY
                if last_yn is not None:
                    p.wait(SC, last_yn)
                p.wait(SC, tY)
                yev_par[par] = p.op(SC, lambda s, h=h, bkY=bkY: s.copy(Y_sb[:, h, :], bkY))
                t_yev.append(yev_par[par])
                bkD = ps[:, 2048 + h * 128:2048 + (h + 1) * 128]
                if t_kv_upd is not None:
                    p.wait(TE, t_kv_upd)
                if h == H - 1:
                    tdel = p.op(TE, lambda t, h=h, slot=slot, bkD=bkD:
                                t.matmul(bkD, kvstr[slot][:, h * 128:(h + 1) * 128],
                                         kvstr[slot][:, E + h * 128:E + (h + 1) * 128],
                                         start=True, stop=True))
                else:
                    p.opq(TE, lambda t, h=h, slot=slot, bkD=bkD:
                          t.matmul(bkD, kvstr[slot][:, h * 128:(h + 1) * 128],
                                   kvstr[slot][:, E + h * 128:E + (h + 1) * 128],
                                   start=True, stop=True))
            bkv.read(tdel)
            B_kq.read(tdel)
            p.waits(VE, [tdel])
            for hx in range(H):
                p.opq(VE, lambda v, hx=hx: v.tensor_scalar(
                    KV32[:, hx * 128:(hx + 1) * 128], KV32[:, hx * 128:(hx + 1) * 128],
                    _R128[hx], 0.0, ALU.mult, ALU.add))
            for bq_ in range(4):
                t_kv_upd = p.op(VE, lambda v, b=bq_: v.tensor_tensor(
                    KV32[:, b * 512:(b + 1) * 512], KV32[:, b * 512:(b + 1) * 512],
                    ps[:, 2048 + b * 512:2048 + (b + 1) * 512], ALU.add))
            t_kvbf = p.op(VE, lambda v: v.tensor_copy(KVbf[:, :], KV32[:, :]))
            p.waits(SC, t_yev)
            if ysb_d is not None:
                if r_dbg is None:
                    r_dbg = p.res("dbg")
                p.waits(GP, t_yev)
                tyd = p.dma(GP, ysb_d[j * 128:(j + 1) * 128, :],
                            Y_sb[:, :, :].rearrange("p a b -> p (a b)"), r_dbg)
                p.wait(SC, tyd)
            tsq = p.op(SC, lambda s: s.activation(Ysq[:, :, :].rearrange("p a b -> p (a b)"),
                                                  Y_sb[:, :, :].rearrange("p a b -> p (a b)"),
                                                  AF.Square))
            bkR = ps[:, 1024:1152]
            p.waits(TE, [tsq])
            if scoreP_par[1] is not None:
                p.wait(TE, scoreP_par[1])
            if last_tsr is not None:
                p.wait(TE, last_tsr)
            tss = None
            for e in range(NF):
                st = (e == 0); sp = (e == NF - 1)
                if sp:
                    tss = p.op(TE, lambda t, e=e, bkR=bkR, st=st:
                               t.matmul(bkR, ones128[:, :], Ysq[:, e, :], start=st, stop=True))
                else:
                    p.opq(TE, lambda t, e=e, bkR=bkR, st=st:
                          t.matmul(bkR, ones128[:, :], Ysq[:, e, :], start=st, stop=False))
            p.wait(SC, tss)
            last_tsr = p.op(SC, lambda s, bkR=bkR: s.activation(rtmp[:, :], bkR, AF.Sqrt,
                                                                bias=epsrms[:, 0:1], scale=1.0 / E))
            p.waits(VE, [last_tsr, t_tbl])
            trc = p.op(VE, lambda v: v.reciprocal(invb[:, :], rtmp[:, :]))
            p.op(VE, lambda v: v.tensor_copy(invb_bf[:, :], invb[:, :]))
            for e in range(NF):
                last_yn = p.op(VE, lambda v, e=e, j=j:
                               v.scalar_tensor_tensor(Yn_all[:, e, j * 128:(j + 1) * 128],
                                                      Y_sb[:, e, :], rmsc[:, e:e + 1],
                                                      invb_bf[:, :], ALU.mult, ALU.mult))

    if yn_d is not None:
        if r_dbg is None:
            r_dbg = p.res("dbg")
        p.waits(GP, [last_yn, last_yev])
        p.dma(GP, yn_d, Yn_all[:, :, :].rearrange("p a b -> p (a b)"), r_dbg)
    if rct_d is not None:
        r_rct = p.res("rct")
        p.wait(SY, last_yn)
        p.dma(SY, rct_d, rCt[:, :], r_rct)

    p.waits(GP, [last_yn, t_kv_upd])
    B_wo = Buf()
    B_wo.before_write(p, GP)
    for f in range(NF):
        p.dma(GP, wo_sb[:, f, :], wo_d[f * 128:(f + 1) * 128, :], r_wo)
    t_wo = p.full(r_wo)
    B_wo.wrote(t_wo)
    t_iota = p.op(GP, lambda g2: (g2.iota(iota_p[:, :], [[0, 128]], channel_multiplier=1,
                                          allow_small_or_imprecise_dtypes=True),
                                  g2.iota(iota_f[:, :], [[1, 128]], channel_multiplier=0,
                                          allow_small_or_imprecise_dtypes=True))[-1])
    p.waits(VE, [t_iota, last_yn])
    t_ident = p.op(VE, lambda v: v.tensor_tensor(ident[:, :], iota_p[:, :], iota_f[:, :], ALU.is_equal))

    B_sg = Buf()
    B_sgo = Buf()
    tT_last = {}
    for half in range(2):
        for foc in range(NF):
            bk_i = foc % 2
            bk = bank(bk_i)
            p.waits(TE, [t_wo, last_yn])
            if pbank.get(bk_i) is not None:
                p.wait(TE, pbank[bk_i])
                pbank[bk_i] = None
            tS = None
            for f in range(NF):
                A = wo_sb[:, f, foc * 128:(foc + 1) * 128]
                B = Yn_all[:, f, half * 512:(half + 1) * 512]
                st = (f == 0); sp = (f == NF - 1)
                if sp:
                    tS = p.op(TE, lambda t, A=A, B=B, bk=bk, st=st: t.matmul(bk, A, B, start=st, stop=True))
                else:
                    p.opq(TE, lambda t, A=A, B=B, bk=bk, st=st: t.matmul(bk, A, B, start=st, stop=False))
            B_sg.before_write(p, VE)
            p.wait(VE, tS)
            tg = p.op(VE, lambda v, bk=bk, foc=foc, half=half:
                      v.tensor_tensor(sgld[:, 0:512], bk, g_fm[:, foc, half * 512:(half + 1) * 512],
                                      ALU.mult))
            pbank[bk_i] = tg
            B_sg.wrote(tg)
            for cs in range(4):
                bkT_i = 4 + (cs % 2)
                bkT = bank(bkT_i, 128)
                p.waits(TE, [tg, t_ident])
                if tT_last.get(bkT_i) is not None:
                    p.wait(TE, tT_last[bkT_i])
                tT = p.op(TE, lambda t, cs=cs, bkT=bkT:
                          t.transpose(bkT, sgld[:, cs * 128:(cs + 1) * 128], ident[:, :]))
                B_sg.read(tT)
                B_sgo.before_write(p, SC)
                p.wait(SC, tT)
                tcp = p.op(SC, lambda s, bkT=bkT: s.copy(sgout[:, :], bkT))
                B_sgo.wrote(tcp)
                p.wait(GP, tcp)
                cch = half * 4 + cs
                p.dma(GP, sg_b[cch * 128:(cch + 1) * 128, foc * 128:(foc + 1) * 128],
                      sgout[:, :], w_sg)
                tws = p.full(w_sg)
                B_sgo.read(tws)
                tT_last[bkT_i] = tws
    t_sg_all = p.full(w_sg)

    B_sgld = Buf()
    for cch in range(NCH):
        p.waits(SY, [t_sg_all])
        B_sgld.before_write(p, SY)
        p.dma(SY, sgld[:, :], sg_b[cch * 128:(cch + 1) * 128, :], r_fin)
        p.dma(SY, xb2[:, :], xo[cch * 128:(cch + 1) * 128, :], r_fin)
        t_both = p.full(r_fin)
        B_sgld.wrote(t_both)
        p.waits(VE, [t_both])
        ta = p.op(VE, lambda v: v.tensor_tensor(sgld[:, :], sgld[:, :], xb2[:, :], ALU.add))
        B_sgld.read(ta)
        p.wait(GP, ta)
        tst = p.dma(GP, out_d[cch * 128:(cch + 1) * 128, :], sgld[:, :], w_out)
        B_sgld.read(tst)

    fin = w_out.cnt
    p.ops[GP].append(lambda g2, sem=w_out.sem, v=fin: g2.wait_ge(sem, v))

    block = ctx.enter_context(nc.Block())
    p.finalize(block)
    return nc


def kernel(L0_attn_x, L0_attn_Wq, L0_attn_Wk, L0_attn_Wv, L0_attn_Wo, L0_attn_rms_w,
           L0_ffn_w1, L0_ffn_b1, L0_ffn_w2, L0_ffn_b2):
    x = np.ascontiguousarray(np.asarray(L0_attn_x, dtype=np.float32))
    wq = np.ascontiguousarray(np.asarray(L0_attn_Wq, dtype=np.float32))
    wk = np.ascontiguousarray(np.asarray(L0_attn_Wk, dtype=np.float32))
    wv = np.ascontiguousarray(np.asarray(L0_attn_Wv, dtype=np.float32))
    wo = np.ascontiguousarray(np.asarray(L0_attn_Wo, dtype=np.float32))
    rms_w = np.asarray(L0_attn_rms_w, dtype=np.float32)
    w1 = np.ascontiguousarray(np.asarray(L0_ffn_w1, dtype=np.float32))
    b1 = np.asarray(L0_ffn_b1, dtype=np.float32)
    w2 = np.ascontiguousarray(np.asarray(L0_ffn_w2, dtype=np.float32))
    b2 = np.asarray(L0_ffn_b2, dtype=np.float32)

    qdecF, kdec, dmaskT, rCt, freq = _host_tables()
    b1c = np.ascontiguousarray(b1.reshape(DFF // 128, 128).T)
    b2c = np.ascontiguousarray(b2.reshape(NF, 128).T)
    rmsc = np.ascontiguousarray(rms_w.reshape(NF, 128).T)

    in_maps = []
    for core in range(8):
        b, s = core // 4, core % 4
        xo_h = np.ascontiguousarray(x[b, s * T_OWN:(s + 1) * T_OWN])
        pre_rows = s * T_OWN
        xp_h = np.zeros((T_PRE, E), dtype=np.float32)
        if pre_rows > 0:
            take = min(pre_rows, T_PRE)
            xp_h[T_PRE - take:] = x[b, pre_rows - take:pre_rows]
        t_own = s * T_OWN + np.arange(T_OWN)
        t_pre = s * T_OWN - T_PRE + np.arange(T_PRE)
        cos_o, sin_o = _trig(t_own, freq)
        cos_p, sin_p = _trig(t_pre, freq)
        in_maps.append({
            "xo": xo_h, "xp": xp_h,
            "wq": wq, "wk": wk, "wv": wv, "wo": wo,
            "w1": w1, "w2": w2,
            "b1c": b1c, "b2c": b2c, "rmsc": rmsc,
            "cos_o": cos_o, "sin_o": sin_o, "cos_p": cos_p, "sin_p": sin_p,
            "qdecF": qdecF, "kdec": kdec, "dmaskT": dmaskT, "rCt": rCt,
        })

    with ExitStack() as ctx:
        nc = build_program(ctx)
        res = run_bass_kernel_spmd(nc, in_maps, core_ids=list(range(8)))
    global LAST_RES
    LAST_RES = res

    out = np.empty((2, 4096, E), dtype=np.float32)
    for core in range(8):
        b, s = core // 4, core % 4
        out[b, s * T_OWN:(s + 1) * T_OWN] = res.results[core]["out"]
    return out


# version 16
# speedup vs baseline: 1.1221x; 1.0722x over previous
import math
import numpy as np
from contextlib import ExitStack

import concourse.bass as bass
import concourse.mybir as mybir
from concourse.bass_utils import run_bass_kernel_spmd

F32 = mybir.dt.float32
BF16 = mybir.dt.bfloat16
AF = mybir.ActivationFunctionType
ALU = mybir.AluOpType
AX = mybir.AxisListType

E = 2048
H = 16
D = 128
C = 128
T_OWN = 1024
T_PRE = 3072
NCH = T_OWN // C
NPC = T_PRE // C
DFF = 8192
NF = E // 128
RMS_EPS = 1e-5
LN_EPS = 1e-5
SY, SC, VE, TE, GP = "sync", "scalar", "vector", "tensor", "gpsimd"
KB = 1024
DBG = []
LAST_RES = None


_R128 = None


def _decay_r128():
    start = 2.0 ** (-(2.0 ** -(math.log2(H) - 3)))
    slopes = np.array([start * start ** i for i in range(H)], dtype=np.float64)
    r = np.exp(-slopes)
    return [float(np.float32(rv ** 128.0)) for rv in r]


def _host_tables():
    start = 2.0 ** (-(2.0 ** -(math.log2(H) - 3)))
    slopes = np.array([start * start ** i for i in range(H)], dtype=np.float64)
    r = np.exp(-slopes)
    c_idx = np.arange(C, dtype=np.float64)
    qdecF = np.zeros((128, E), dtype=np.float64)
    kdec = np.zeros((128, E), dtype=np.float64)
    dmaskT = np.zeros((128, E), dtype=np.float64)
    rCt = np.zeros((128, E), dtype=np.float64)
    for h in range(H):
        qdecF[:, h * 128:(h + 1) * 128] = (r[h] ** (c_idx + 1.0))[None, :]
        kdec[:, h * 128:(h + 1) * 128] = (r[h] ** (C - 1.0 - c_idx))[:, None]
        diff = c_idx[None, :] - c_idx[:, None]
        dm = np.where(diff >= 0, r[h] ** np.maximum(diff, 0.0), 0.0)
        dmaskT[:, h * 128:(h + 1) * 128] = dm
        rCt[:, h * 128:(h + 1) * 128] = r[h] ** float(C)
    freq = (1.0 / 1024.0) ** np.linspace(0.0, 1.0, D // 4, dtype=np.float64)
    return (qdecF.astype(np.float32), kdec.astype(np.float32),
            dmaskT.astype(np.float32), rCt.astype(np.float32), freq)


def _trig(t_abs, freq):
    theta = np.asarray(t_abs, dtype=np.float64)[:, None] * freq[None, :]
    cos = np.tile(np.cos(theta), (1, H)).astype(np.float32)
    sin = np.tile(np.sin(theta), (1, H)).astype(np.float32)
    return np.ascontiguousarray(cos), np.ascontiguousarray(sin)


class Res:
    __slots__ = ("name", "sem", "cnt")

    def __init__(self, name, sem):
        self.name = name
        self.sem = sem
        self.cnt = 0


class Prog:
    CENGS = (SC, VE, TE, GP)

    def __init__(self, nc, ctx):
        self.nc = nc
        self.ctx = ctx
        self.ops = {e: [] for e in (SY, SC, VE, TE, GP)}
        self.csems = {e: ctx.enter_context(nc.semaphore("c_" + e)) for e in self.CENGS}
        self.ccnt = {e: 0 for e in self.CENGS}
        self.waited = {e: {} for e in (SY, SC, VE, TE, GP)}

    def res(self, name):
        return Res(name, self.ctx.enter_context(self.nc.semaphore("r_" + name)))

    def dma(self, eng, out, in_, r):
        r.cnt += 16
        val = r.cnt
        self.ops[eng].append(lambda e, out=out, in_=in_, s=r.sem:
                             e.dma_start(out=out, in_=in_).then_inc(s, 16))
        return ("r_" + r.name, r.sem, val)

    def dmat(self, eng, out, in_, r):
        r.cnt += 16
        val = r.cnt
        self.ops[eng].append(lambda e, out=out, in_=in_, s=r.sem:
                             e.dma_start_transpose(out, in_).then_inc(s, 16))
        return ("r_" + r.name, r.sem, val)

    SELFSYNC = (VE, SC, GP)

    def op(self, eng, fn):
        sem = self.csems[eng]
        if eng in self.SELFSYNC and self.ccnt[eng] > 0:
            prev = self.ccnt[eng]
            self.ops[eng].append(lambda e, sem=sem, v=prev: e.wait_ge(sem, v))
        self.ccnt[eng] += 1
        val = self.ccnt[eng]
        self.ops[eng].append(lambda e, fn=fn, sem=sem: fn(e).then_inc(sem, 1))
        return ("c_" + eng, sem, val)

    def opq(self, eng, fn):
        if eng in self.SELFSYNC:
            self.op(eng, fn)
        else:
            self.ops[eng].append(fn)

    def ctok(self, eng):
        return ("c_" + eng, self.csems[eng], self.ccnt[eng])

    def wait(self, eng, tok):
        if tok is None:
            return
        key, sem, v = tok
        if key == "c_" + eng:
            return
        if self.waited[eng].get(key, 0) >= v:
            return
        self.waited[eng][key] = v
        self.ops[eng].append(lambda e, sem=sem, v=v: e.wait_ge(sem, v))

    def waits(self, eng, toks):
        for t in toks:
            self.wait(eng, t)

    def full(self, r):
        return ("r_" + r.name, r.sem, r.cnt)

    def finalize(self, block):
        for name in (SY, SC, VE, TE, GP):
            ops = self.ops[name]
            getattr(block, name)(lambda e, ops=ops: [f(e) for f in ops])


class Buf:
    def __init__(self):
        self.rd = []
        self.wr = None

    def before_write(self, p, eng):
        p.waits(eng, self.rd)
        if self.wr is not None:
            p.wait(eng, self.wr)

    def wrote(self, tok):
        self.wr = tok
        self.rd = []

    def read(self, tok):
        if tok is not None:
            self.rd.append(tok)


def build_program(ctx):
    global _R128
    _R128 = _decay_r128()
    nc = bass.Bass("TRN2", target_bir_lowering=False, dynamic_dma_scratch_size=16384)

    def din(name, shape, dtype=F32):
        return nc.dram_tensor(name, list(shape), dtype, kind="ExternalInput").ap()

    def dint(name, shape, dtype=BF16):
        kind = "ExternalOutput" if name in DBG else "Internal"
        return nc.dram_tensor(name, list(shape), dtype, kind=kind).ap()

    def sb(name, shape, dtype=F32):
        return ctx.enter_context(nc.sbuf_tensor(name, list(shape), dtype))

    xo = din("xo", (T_OWN, E))
    xp = din("xp", (T_PRE, E))
    wq_d = din("wq", (E, E)); wk_d = din("wk", (E, E))
    wv_d = din("wv", (E, E)); wo_d = din("wo", (E, E))
    w1_d = din("w1", (DFF, E)); w2_d = din("w2", (E, DFF))
    b1c_d = din("b1c", (128, DFF // 128)); b2c_d = din("b2c", (128, NF))
    rmsc_d = din("rmsc", (128, NF))
    cos_o = din("cos_o", (T_OWN, 512)); sin_o = din("sin_o", (T_OWN, 512))
    cos_p = din("cos_p", (T_PRE, 512)); sin_p = din("sin_p", (T_PRE, 512))
    qdecF_d = din("qdecF", (128, E)); kdec_d = din("kdec", (128, E))
    dmaskT_d = din("dmaskT", (128, E))
    out_d = nc.dram_tensor("out", [T_OWN, E], F32, kind="ExternalOutput").ap()

    xn_b = dint("xn_b", (T_OWN, E))
    xnp_b = dint("xnp_b", (T_PRE, E))
    q_b = dint("q_b", (T_OWN, E)); k_b = dint("k_b", (T_OWN, E))
    kd_b = dint("kd_b", (T_OWN, E)); v_b = dint("v_b", (T_OWN, E))
    kdp_b = dint("kdp_b", (T_PRE, E)); vp_b = dint("vp_b", (T_PRE, E))
    w1q_b = dint("w1q_b", (DFF, E)); w2q_b = dint("w2q_b", (E, DFF))
    sg_b = dint("sg_b", (T_OWN, E), F32)
    kvpre_d = dint("kvpre_d", (128, E), F32) if "kvpre_d" in DBG else None
    yn_d = dint("yn_d", (128, NF * 1024)) if "yn_d" in DBG else None
    ysb_d = dint("ysb_d", (T_OWN, E)) if "ysb_d" in DBG else None
    kq_d = dint("kq_d", (2, 128, NF * 512)) if "kq_d" in DBG else None
    pjp_d = dint("pjp_d", (T_PRE, E), F32) if "pjp_d" in DBG else None
    pjr_d = dint("pjr_d", (T_PRE, E), F32) if "pjr_d" in DBG else None
    ksb_d = dint("ksb_d", (128, E)) if "ksb_d" in DBG else None
    kvs_d = dint("kvs_d", (128, 2 * E)) if "kvs_d" in DBG else None
    rct_d = dint("rct_d", (128, E), F32) if "rct_d" in DBG else None
    r_dbg = None

    kdec = sb("kdec_s", (128, E), BF16)
    KV32 = sb("KV32", (128, E))
    KVbf = sb("KVbf", (128, E), BF16)
    kvstr = [sb(f"kvstr{i}", (128, 2 * E), BF16) for i in range(2)]
    g_fm = sb("g_fm", (128, NF, 1024), BF16)
    b2c = sb("b2c_s", (128, NF)); rmsc = sb("rmsc_s", (128, NF))
    b1c = sb("b1c_s", (128, DFF // 128))
    ones128 = sb("ones128", (128, 128), BF16)
    ones128f = sb("ones128f", (128, 128))
    epsln = sb("epsln", (128, 1)); epsrms = sb("epsrms", (128, 1))
    stat = sb("stat", (128, 16))
    part1 = sb("part1", (128, 64)); part2 = sb("part2", (128, 64))
    pred = sb("pred", (128, 2)); sc12 = sb("sc12", (128, 2)); inv12 = sb("inv12", (128, 2))

    base = (nc.sbuf_base + 31) & ~31
    SB_TOP = 229344
    R1, R2, R3 = base, base + 32 * KB, base + 96 * KB
    need = R3 + 40 * KB
    assert need <= SB_TOP, f"scratch overflow: need {need}, top {SB_TOP}"

    def at(name, off, shape, dtype=F32):
        return nc.alloc_sbuf_tensor_at(name, list(shape), dtype, offset=off)

    xn_fm = at("xn_fm", R1, (128, NF, 1024), BF16)
    Yn_all = at("Yn_all", R1, (128, NF, 1024), BF16)
    xbuf = [at(f"xbuf{i}", R2 + i * 8 * KB, (128, E)) for i in range(2)]
    xnbf = [at(f"xnbf{i}", R2 + 16 * KB + i * 4 * KB, (128, E), BF16) for i in range(2)]
    junk = at("junk", R2 + 24 * KB, (128, E))
    wrs = at("wrs", R2, (128, NF, E), BF16)
    g_addr = nc.lookup_mloc("g_fm").addr
    qw32 = [at(f"qw32{i}", g_addr + i * 8 * KB, (128, E)) for i in range(2)]
    junkq = at("junkq", g_addr + 16 * KB, (128, E))
    qout = [at(f"qout{i}", g_addr + 24 * KB + i * 4 * KB, (128, E), BF16) for i in range(2)]
    qmsk = at("qmsk", g_addr + 16 * KB, (128, 2 * E), BF16)
    g32 = at("g32", R2, (128, NF, 1024))
    k4 = at("k4", R2, (128, NF, 512), BF16)
    q4 = at("q4", R2 + 16 * KB, (128, NF, 512), BF16)
    wo_sb = at("wo_sb", R2, (128, NF, E), BF16)
    xnp_g = at("xnp_g", R3, (128, NF, 256), BF16)
    pj32 = [at(f"pj32{i}", R3 + 8 * KB + i * 8 * KB, (128, E)) for i in range(2)]
    trigc = [at(f"trigc{i}", R3 + 24 * KB + i * 2 * KB, (128, 512)) for i in range(2)]
    trigs = [at(f"trigs{i}", R3 + 28 * KB + i * 2 * KB, (128, 512)) for i in range(2)]
    rotmp = at("rotmp", R3 + 32 * KB, (128, 1024))
    h_sp = at("h_sp", R3, (128, 4, 1024), BF16)
    wtile_sp = at("wtile_sp", R3 + 8 * KB, (128, 16, 512), BF16)
    w2T = at("w2T", R3 + 24 * KB, (128, 4, 2048), BF16)
    Y_sb = at("Y_sb", R3, (128, NF, 128), BF16)
    Ysq = at("Ysq", R3 + 4 * KB, (128, NF, 128), BF16)
    qtile = at("qtile", R3 + 8 * KB, (128, 128), BF16)
    P_bf = at("P_bf", R3 + 8 * KB + 256, (128, 128), BF16)
    invb_bf = at("invb_bf", R3 + 8 * KB + 512, (128, 128), BF16)
    P_bfB = at("P_bfB", R3 + 8 * KB + 768, (128, 128), BF16)
    qtileB = at("qtileB", R3 + 18 * KB, (128, 128), BF16)
    invb = at("invb", R3 + 9 * KB, (128, 128))
    rtmp = at("rtmp", R3 + 9 * KB + 512, (128, 128))
    dmaskT = at("dmaskT_s", R3 + 10 * KB, (128, E), BF16)
    qdecF = at("qdecF_s", R3 + 14 * KB, (128, E), BF16)
    sgout2 = at("sgout2", R3 + 20 * KB, (128, 2, 512))
    sgf = at("sgf", R3 + 24 * KB, (128, E))
    xbf = at("xbf", R3 + 12 * KB, (128, E))
    ident = at("ident", R3 + KB, (128, 128))
    iota_p = at("iota_p", R3 + 2 * KB, (128, 128))
    iota_f = at("iota_f", R3 + 3 * KB, (128, 128))
    sgld = at("sgld", R3 + 4 * KB, (128, E))
    xb2 = at("xb2", R3 + 12 * KB, (128, E))

    ps = ctx.enter_context(nc.psum_tensor("ps", [128, 4096], F32))

    def bank(i, w=512):
        return ps[:, i * 512:i * 512 + w]

    p = Prog(nc, ctx)

    r_tbl = p.res("tbl")
    r_tblg = p.res("tblg")
    r_x = [p.res("x0"), p.res("x1")]
    r_trig = [p.res("tg0"), p.res("tg1")]
    r_qw = [p.res("qw0"), p.res("qw1")]
    r_kvld = [p.res("kv0"), p.res("kv1")]
    r_wt = [p.res("wt0"), p.res("wt1")]
    r_w2s = [p.res("w2s0")]
    r_wrs = p.res("wrsl")
    r_xnp = p.res("xnpl")
    r_xnfm = p.res("xnfm")
    r_kq = p.res("kq")
    r_wo = p.res("wol")
    r_stbl = p.res("stbl")
    r_fin = p.res("fin")
    w_xno = [p.res("wxno0"), p.res("wxno1")]
    w_xnp = [p.res("wxnp0"), p.res("wxnp1")]
    w_kv = [p.res("wkv0"), p.res("wkv1")]
    w_kk = [p.res("wkk0"), p.res("wkk1")]
    w_qw = [p.res("wqw0"), p.res("wqw1")]
    w_sg = [p.res("wsg0"), p.res("wsg1")]
    w_out = p.res("wout")
    r_pjd = p.res("pjd") if (pjp_d is not None or pjr_d is not None) else None
    r_kvs = p.res("kvs") if kvs_d is not None else None

    p.dma(GP, kdec[:, :], kdec_d, r_tblg)
    p.dma(SY, b2c[:, :], b2c_d, r_tbl)
    p.dma(SY, rmsc[:, :], rmsc_d, r_tbl)
    p.dma(SY, b1c[:, :], b1c_d, r_tbl)
    t_tbl = p.full(r_tbl)
    t_kdec = p.full(r_tblg)
    if ksb_d is not None:
        r_ksb = p.res("ksb")
        p.wait(SY, t_kdec)
        p.dma(SY, ksb_d, kdec[:, :], r_ksb)

    def init_fn(v):
        v.memset(KV32[:, :], 0.0)
        v.memset(ones128[:, :], 1.0)
        v.memset(ones128f[:, :], 1.0)
        v.memset(epsln[:, 0:1], LN_EPS)
        return v.memset(epsrms[:, 0:1], RMS_EPS)
    p.op(VE, init_fn)

    B_x = [Buf(), Buf()]
    B_xnbf = [Buf(), Buf()]
    B_trig = [Buf(), Buf()]
    B_pj = [Buf(), Buf()]
    B_kv = [Buf(), Buf()]
    B_qw = [Buf(), Buf()]
    B_qo = [Buf(), Buf()]
    B_wt = [Buf(), Buf()]
    B_stat = [Buf(), Buf()]

    t_sent = None
    if "sent" in DBG:
        r_sent = p.res("sent")
        tms = p.op(VE, lambda v: v.memset(kvstr[0][:, 0:E], 7.0))
        p.wait(SY, tms)
        for j in range(NPC):
            p.dma(SY, kdp_b[j * 128:(j + 1) * 128, :], kvstr[0][:, 0:E], r_sent)
        t_sent = p.full(r_sent)
        B_kv[0].read(t_sent)
        p.waits(GP, [t_sent])

    w1_tiles = [w1_d[i * 128:(i + 1) * 128, :] for i in range(64)]
    w2_tiles = [w2_d[(i // 4) * 128:(i // 4 + 1) * 128, (i % 4) * 2048:(i % 4 + 1) * 2048]
                for i in range(64)]
    w1q_tiles = [w1q_b[i * 128:(i + 1) * 128, :] for i in range(64)]
    w2q_tiles = [w2q_b[(i // 4) * 128:(i // 4 + 1) * 128, (i % 4) * 2048:(i % 4 + 1) * 2048]
                 for i in range(64)]
    qred_jobs = [(t, part1, i) for i, t in enumerate(w1_tiles)] + \
                [(t, part2, i) for i, t in enumerate(w2_tiles)]
    qred_pos = [0]

    def quant_reduce_step(nsteps):
        for _ in range(nsteps):
            i = qred_pos[0]
            if i >= len(qred_jobs):
                return
            qred_pos[0] = i + 1
            tile, part, col = qred_jobs[i]
            slot = i % 2
            bq = B_qw[slot]
            bq.before_write(p, SY)
            tl = p.dma(SY, qw32[slot][:, :], tile, r_qw[slot])
            bq.wrote(tl)
            if i % 2 == 0:
                p.wait(VE, tl)
                tr = p.op(VE, lambda v, slot=slot, part=part, col=col:
                          v.scalar_tensor_tensor(qw32[slot][:, :], qw32[slot][:, :], -1.0,
                                                 qw32[slot][:, :], ALU.mult, ALU.max,
                                                 accum_out=part[:, col:col + 1]))
            else:
                p.wait(SC, tl)
                tr = p.op(SC, lambda s, slot=slot, part=part, col=col:
                          s.activation(qw32[slot][:, :], qw32[slot][:, :], AF.Abs,
                                       accum_out=part[:, col:col + 1]))
            bq.read(tr)

    w2_order = [r * 4 + cb for cb in range(4) for r in range(16)]
    qwr_jobs = list(zip(w1_tiles, w1q_tiles, [0] * 64)) + \
               [(w2_tiles[i], w2q_tiles[i], 1) for i in w2_order]
    qwr_pos = [0]
    qwr_tok = [None] * len(qwr_jobs)
    tq0_box = [None]
    last_tqo = [None]

    def quant_write_step(nsteps):
        tq0 = tq0_box[0]
        if tq0 is None:
            return
        for _ in range(nsteps):
            i = qwr_pos[0]
            if i >= len(qwr_jobs):
                return
            qwr_pos[0] = i + 1
            tile, dst, which = qwr_jobs[i]
            inv_ap = inv12[:, which:which + 1]
            wres = w_qw[i % 2]
            slot = i % 2
            bq = B_qw[slot]
            bq.before_write(p, SY)
            tl = p.dma(SY, qw32[slot][:, :], tile, r_qw[slot])
            bq.wrote(tl)
            p.waits(VE, [tl, tq0])
            p.opq(VE, lambda v, slot=slot, inv_ap=inv_ap:
                  v.tensor_scalar(qmsk[:, 0:E], qw32[slot][:, :], inv_ap, 0.5,
                                  ALU.mult, ALU.is_ge))
            tm2 = p.op(VE, lambda v, slot=slot, inv_ap=inv_ap:
                       v.tensor_scalar(qmsk[:, E:2 * E], qw32[slot][:, :], inv_ap, -0.5,
                                       ALU.mult, ALU.is_le))
            bq.read(tm2)
            bo = B_qo[slot]
            bo.before_write(p, VE)
            tqo = p.op(VE, lambda v, slot=slot: v.tensor_tensor(qout[slot][:, :], qmsk[:, 0:E],
                                                                qmsk[:, E:2 * E], ALU.subtract))
            bo.wrote(tqo)
            last_tqo[0] = tqo
            p.wait(GP, tqo)
            tw = p.dma(GP, dst, qout[slot][:, :], wres)
            qwr_tok[i] = tw
            bo.read(tw)

    def emit_ln(src, dst_b, w_res, idx, slot):
        bx = B_x[slot]
        bx.before_write(p, SY)
        tl = p.dma(SY, xbuf[slot][:, :], src[idx * 128:(idx + 1) * 128, :], r_x[slot])
        bx.wrote(tl)
        x_ap = xbuf[slot][:, :]
        st8 = 8 * slot
        p.wait(VE, tl)
        B_stat[slot].before_write(p, VE)
        p.opq(VE, lambda v: v.tensor_scalar(junk[:, :], x_ap, 1.0, 0.0, ALU.mult, ALU.add,
                                            accum_out=stat[:, st8 + 0:st8 + 1]))
        p.opq(VE, lambda v: v.scalar_tensor_tensor(junk[:, :], x_ap, 1.0, x_ap,
                                                   ALU.mult, ALU.mult, accum_out=stat[:, st8 + 1:st8 + 2]))
        p.opq(VE, lambda v: v.tensor_scalar(stat[:, st8 + 2:st8 + 3], stat[:, st8 + 0:st8 + 1], 1.0 / E, 0.0, ALU.mult, ALU.add))
        p.opq(VE, lambda v: v.scalar_tensor_tensor(stat[:, st8 + 3:st8 + 4], stat[:, st8 + 2:st8 + 3], -1.0, stat[:, st8 + 2:st8 + 3],
                                                   ALU.mult, ALU.mult))
        tv = p.op(VE, lambda v: v.scalar_tensor_tensor(stat[:, st8 + 4:st8 + 5], stat[:, st8 + 1:st8 + 2], 1.0 / E,
                                                       stat[:, st8 + 3:st8 + 4], ALU.mult, ALU.add))
        p.wait(SC, tv)
        tsq = p.op(SC, lambda s: s.activation(stat[:, st8 + 5:st8 + 6], stat[:, st8 + 4:st8 + 5], AF.Sqrt,
                                              bias=epsln[:, 0:1], scale=1.0))
        p.wait(VE, tsq)
        p.opq(VE, lambda v: v.reciprocal(stat[:, st8 + 6:st8 + 7], stat[:, st8 + 5:st8 + 6]))
        tnm = p.op(VE, lambda v: v.scalar_tensor_tensor(stat[:, st8 + 7:st8 + 8], stat[:, st8 + 2:st8 + 3], -1.0,
                                                        stat[:, st8 + 6:st8 + 7], ALU.mult, ALU.mult))
        bn = B_xnbf[slot]
        bn.before_write(p, SC)
        p.wait(SC, tnm)
        txn = p.op(SC, lambda s: s.activation(xnbf[slot][:, :], x_ap, AF.Identity,
                                              bias=stat[:, st8 + 7:st8 + 8], scale=stat[:, st8 + 6:st8 + 7]))
        B_stat[slot].wrote(txn)
        bx.read(txn)
        bn.wrote(txn)
        p.wait(GP, txn)
        tw = p.dma(GP, dst_b[idx * 128:(idx + 1) * 128, :], xnbf[slot][:, :], w_res)
        bn.read(tw)
        return tw

    for i in range(NCH):
        emit_ln(xo, xn_b, w_xno[i % 2], i, i % 2)
        quant_reduce_step(1)
    for i in range(NPC):
        emit_ln(xp, xnp_b, w_xnp[i % 2], i, i % 2)
        quant_reduce_step(2)
    t_xno_all = [p.full(w_xno[0]), p.full(w_xno[1])]
    t_xnp_all = [p.full(w_xnp[0]), p.full(w_xnp[1])]

    p.waits(SC, t_xno_all)
    for f in range(NF):
        p.dmat(SC, xn_fm[:, f, :], xn_b[0:T_OWN, f * 128:(f + 1) * 128], r_xnfm)
    t_xnfm_all = p.full(r_xnfm)

    B_wrs = Buf()

    def load_W(w_dram):
        B_wrs.before_write(p, GP)
        for f in range(NF):
            p.dma(GP, wrs[:, f, :], w_dram[f * 128:(f + 1) * 128, :], r_wrs)
        tok = p.full(r_wrs)
        B_wrs.wrote(tok)
        return tok

    pbank = {}

    def proj_chunk(xn_t, banks, out_write, wait_toks):
        etoks = []
        for n in range(4):
            bk_i = banks[n % len(banks)]
            bk = bank(bk_i)
            p.waits(TE, wait_toks)
            if pbank.get(bk_i) is not None:
                p.wait(TE, pbank[bk_i])
            tmm = None
            for f in range(NF):
                A = xn_t(f)
                B = wrs[:, f, n * 512:(n + 1) * 512]
                st = (f == 0); sp = (f == NF - 1)
                if sp:
                    tmm = p.op(TE, lambda t, A=A, B=B, bk=bk, st=st: t.matmul(bk, A, B, start=st, stop=True))
                else:
                    p.opq(TE, lambda t, A=A, B=B, bk=bk, st=st: t.matmul(bk, A, B, start=st, stop=False))
            p.wait(SC, tmm)
            te = out_write(n, bk)
            pbank[bk_i] = te
            etoks.append(te)
        return etoks

    def head3d(t, base_c, width=32):
        return t[:, :].rearrange("p (h d) -> p h d", d=128)[:, :, base_c:base_c + width]

    def rep3d(t, width=32):
        return t[:, :].rearrange("p (h d) -> p h d", d=32)[:, :, 0:width]

    def emit_rotary(dst32, cosb, sinb):
        x1 = head3d(dst32, 0); x2 = head3d(dst32, 64)
        tm1 = rotmp[:, 0:512].rearrange("p (h d) -> p h d", d=32)
        tm2 = rotmp[:, 512:1024].rearrange("p (h d) -> p h d", d=32)
        cs = rep3d(cosb); sn = rep3d(sinb)
        p.opq(VE, lambda v: v.tensor_tensor(tm1, x2, sn, ALU.mult))
        p.opq(VE, lambda v: v.tensor_tensor(tm2, x1, sn, ALU.mult))
        p.opq(VE, lambda v: v.tensor_tensor(x2, x2, cs, ALU.mult))
        p.opq(VE, lambda v: v.tensor_tensor(x2, x2, tm2, ALU.subtract))
        p.opq(VE, lambda v: v.tensor_tensor(x1, x1, cs, ALU.mult))
        return p.op(VE, lambda v: v.tensor_tensor(x1, x1, tm1, ALU.add))

    def load_trig(slot, cos_src, sin_src, j):
        bt = B_trig[slot]
        bt.before_write(p, SY)
        p.dma(SY, trigc[slot][:, :], cos_src[j * 128:(j + 1) * 128, :], r_trig[slot])
        p.dma(SY, trigs[slot][:, :], sin_src[j * 128:(j + 1) * 128, :], r_trig[slot])
        t2 = p.full(r_trig[slot])
        bt.wrote(t2)
        return t2

    def evac_pj(slot):
        def w(n, bk):
            if n == 0:
                B_pj[slot].before_write(p, SC)
            return p.op(SC, lambda s, bk=bk, n=n: s.copy(pj32[slot][:, n * 512:(n + 1) * 512], bk))
        return w

    p.waits(GP, t_xno_all + t_xnp_all)
    t_wk = load_W(wk_d)
    wrs_readers = []

    def kproj(xn_t, slot, wait_toks, trig_tok, bounce_kd, bounce_k, dbg_j=None):
        et = proj_chunk(xn_t, (0, 1, 2, 3), evac_pj(slot), wait_toks)
        B_pj[slot].wrote(et[-1])
        wrs_readers.append(et[-1])
        if dbg_j is not None and pjp_d is not None:
            p.waits(GP, et)
            tdp = p.dma(GP, pjp_d[dbg_j * 128:(dbg_j + 1) * 128, :], pj32[slot][:, :], r_pjd)
            B_pj[slot].read(tdp)
            p.waits(VE, [tdp])
        p.waits(VE, et + [trig_tok, t_kdec])
        trot = emit_rotary(pj32[slot], trigc[slot], trigs[slot])
        B_trig[slot].read(trot)
        if dbg_j is not None and pjr_d is not None:
            p.wait(GP, trot)
            tdr = p.dma(GP, pjr_d[dbg_j * 128:(dbg_j + 1) * 128, :], pj32[slot][:, :], r_pjd)
            B_pj[slot].read(tdr)
        bkv = B_kv[slot]
        bkv.before_write(p, VE)
        tkr = p.op(VE, lambda v, slot=slot: v.tensor_copy(kvstr[slot][:, E:2 * E], pj32[slot][:, :]))
        tkd = p.op(VE, lambda v, slot=slot: v.tensor_tensor(kvstr[slot][:, 0:E],
                                                            kvstr[slot][:, E:2 * E],
                                                            kdec[:, :], ALU.mult))
        toks = [tkr, tkd]
        B_pj[slot].read(toks[-1])
        bkv.wrote(toks[-1])
        if kvs_d is not None and dbg_j == 22:
            p.waits(SY, toks)
            tks = p.dma(SY, kvs_d, kvstr[slot][:, :], r_kvs)
            bkv.read(tks)
        p.waits(SY, toks)
        p.dma(SY, bounce_kd, kvstr[slot][:, 0:E], w_kk[slot])
        if bounce_k is not None:
            p.dma(SY, bounce_k, kvstr[slot][:, E:2 * E], w_kk[slot])
        bkv.read(p.full(w_kk[slot]))

    for j in range(NCH):
        slot = j % 2
        tt = load_trig(slot, cos_o, sin_o, j)
        kproj(lambda f, j=j: xn_fm[:, f, j * 128:(j + 1) * 128], slot,
              [t_wk, t_xnfm_all], tt, kd_b[j * 128:(j + 1) * 128, :],
              k_b[j * 128:(j + 1) * 128, :])
        quant_reduce_step(3)

    B_xnp = Buf()
    for g in range(T_PRE // 256):
        B_xnp.before_write(p, SC)
        p.waits(SC, t_xnp_all)
        for f in range(NF):
            p.dmat(SC, xnp_g[:, f, :], xnp_b[g * 256:(g + 1) * 256, f * 128:(f + 1) * 128], r_xnp)
        tg = p.full(r_xnp)
        B_xnp.wrote(tg)
        for jj in range(2):
            j = g * 2 + jj
            slot = jj % 2
            tt = load_trig(slot, cos_p, sin_p, j)
            kproj(lambda f, jj=jj: xnp_g[:, f, jj * 128:(jj + 1) * 128], slot,
                  [t_wk, tg], tt, kdp_b[j * 128:(j + 1) * 128, :], None, dbg_j=j)
            B_xnp.read(wrs_readers[-1])
            quant_reduce_step(2)


    quant_reduce_step(len(qred_jobs))
    p.waits(VE, [p.ctok(SC)])
    p.opq(VE, lambda v: v.tensor_reduce(pred[:, 0:1], part1[:, :], AX.X, ALU.add))
    tpr = p.op(VE, lambda v: v.tensor_reduce(pred[:, 1:2], part2[:, :], AX.X, ALU.add))
    p.waits(TE, [tpr])
    if pbank.get(2) is not None:
        p.wait(TE, pbank[2])
    tsum = p.op(TE, lambda t: t.matmul(ps[:, 1024:1026], ones128f[:, :], pred[:, 0:2],
                                       start=True, stop=True))
    p.wait(VE, tsum)
    NW = float(DFF * E)
    p.opq(VE, lambda v: v.tensor_scalar(sc12[:, 0:2], ps[:, 1024:1026], 1.0 / NW, 1e-8,
                                        ALU.mult, ALU.max))
    tq0_box[0] = p.op(VE, lambda v: v.reciprocal(inv12[:, 0:2], sc12[:, 0:2]))
    pbank[2] = tq0_box[0]

    p.waits(GP, wrs_readers)
    for t in wrs_readers:
        B_wrs.read(t)
    t_wv = load_W(wv_d)
    wrs_readers = []

    def evac_kv(slot):
        def w(n, bk):
            if n == 0:
                B_kv[slot].before_write(p, SC)
            return p.op(SC, lambda s, bk=bk, n=n, slot=slot:
                        s.copy(kvstr[slot][:, n * 512:(n + 1) * 512], bk))
        return w

    def vproj(xn_t, banks, slot, wait_toks, dst):
        et = proj_chunk(xn_t, banks, evac_kv(slot), wait_toks)
        wrs_readers.append(et[-1])
        B_kv[slot].wrote(et[-1])
        p.waits(GP, et)
        p.dma(GP, dst, kvstr[slot][:, 0:E], w_kv[slot])
        tw = p.full(w_kv[slot])
        B_kv[slot].read(tw)
        return tw

    for j in range(NCH):
        vproj(lambda f, j=j: xn_fm[:, f, j * 128:(j + 1) * 128],
              (0, 1, 2, 3), j % 2, [t_wv], v_b[j * 128:(j + 1) * 128, :])
        quant_write_step(2)
    for g in range(T_PRE // 256):
        B_xnp.before_write(p, SC)
        for f in range(NF):
            p.dmat(SC, xnp_g[:, f, :], xnp_b[g * 256:(g + 1) * 256, f * 128:(f + 1) * 128], r_xnp)
        tg = p.full(r_xnp)
        B_xnp.wrote(tg)
        for jj in range(2):
            j = g * 2 + jj
            tw = vproj(lambda f, jj=jj: xnp_g[:, f, jj * 128:(jj + 1) * 128],
                       (0, 1, 2, 3), jj % 2, [t_wv, tg], vp_b[j * 128:(j + 1) * 128, :])
            B_xnp.read(tw)
            quant_write_step(2)

    p.waits(GP, wrs_readers)
    for t in wrs_readers:
        B_wrs.read(t)
    t_wq = load_W(wq_d)
    wrs_readers = []
    for j in range(NCH):
        slot = j % 2
        tt = load_trig(slot, cos_o, sin_o, j)
        et = proj_chunk(lambda f, j=j: xn_fm[:, f, j * 128:(j + 1) * 128],
                        (0, 1, 2, 3), evac_pj(slot), [t_wq])
        B_pj[slot].wrote(et[-1])
        wrs_readers.append(et[-1])
        p.waits(VE, et + [tt])
        trot = emit_rotary(pj32[slot], trigc[slot], trigs[slot])
        B_trig[slot].read(trot)
        bkv = B_kv[slot]
        bkv.before_write(p, VE)
        tqr = p.op(VE, lambda v, slot=slot: v.tensor_copy(kvstr[slot][:, 0:E], pj32[slot][:, :]))
        B_pj[slot].read(tqr)
        bkv.wrote(tqr)
        p.wait(GP, tqr)
        p.dma(GP, q_b[j * 128:(j + 1) * 128, :], kvstr[slot][:, 0:E], w_kv[slot])
        bkv.read(p.full(w_kv[slot]))
        quant_write_step(2)

    t_bounce_all = [p.full(w_kv[0]), p.full(w_kv[1]), p.full(w_kk[0]), p.full(w_kk[1])]
    t_kv_upd = None
    t_kv_par = [None, None]
    for b_ in range(4):
        if pbank.get(b_) is not None:
            p.wait(TE, pbank[b_])
            pbank[b_] = None
    for l in range(NPC):
        slot = l % 2
        base = 2048 * (l % 2)
        bkv = B_kv[slot]
        bkv.before_write(p, SY)
        p.waits(SY, t_bounce_all)
        p.dma(SY, kvstr[slot][:, 0:E], kdp_b[l * 128:(l + 1) * 128, :], r_kvld[slot])
        p.dma(SY, kvstr[slot][:, E:2 * E], vp_b[l * 128:(l + 1) * 128, :], r_kvld[slot])
        tv = p.full(r_kvld[slot])
        bkv.wrote(tv)
        p.waits(TE, [tv])
        if t_kv_par[l % 2] is not None:
            p.wait(TE, t_kv_par[l % 2])
        tdel = None
        for h in range(H):
            A = kvstr[slot][:, h * 128:(h + 1) * 128]
            B = kvstr[slot][:, E + h * 128:E + (h + 1) * 128]
            bk = ps[:, base + h * 128:base + (h + 1) * 128]
            if h == H - 1:
                tdel = p.op(TE, lambda t, A=A, B=B, bk=bk: t.matmul(bk, A, B, start=True, stop=True))
            else:
                p.opq(TE, lambda t, A=A, B=B, bk=bk: t.matmul(bk, A, B, start=True, stop=True))
        bkv.read(tdel)
        p.waits(VE, [tdel, t_tbl])
        for hx in range(H):
            p.opq(VE, lambda v, hx=hx: v.tensor_scalar(
                KV32[:, hx * 128:(hx + 1) * 128], KV32[:, hx * 128:(hx + 1) * 128],
                _R128[hx], 0.0, ALU.mult, ALU.add))
        for bq_ in range(4):
            t_kv_upd = p.op(VE, lambda v, b=bq_, base=base: v.tensor_tensor(
                KV32[:, b * 512:(b + 1) * 512], KV32[:, b * 512:(b + 1) * 512],
                ps[:, base + b * 512:base + (b + 1) * 512], ALU.add))
        t_kv_par[l % 2] = t_kv_upd
        quant_write_step(1)

    gb = [None] * 8
    p.wait(TE, t_kv_upd)
    p.waits(SC, [p.ctok(VE)])
    B_wts = Buf()
    B_w2s = Buf()
    t_gadd = None
    g_last = None
    for sp_i in range(16):
        B_wts.before_write(p, SC)
        p.waits(SC, [qwr_tok[4 * sp_i + 2], qwr_tok[4 * sp_i + 3]])
        for f in range(NF):
            p.dmat(SC, wtile_sp[:, f, :],
                   w1q_b[sp_i * 512:(sp_i + 1) * 512, f * 128:(f + 1) * 128], r_wt[0])
        t_wts = p.full(r_wt[0])
        B_wts.wrote(t_wts)
        B_w2s.before_write(p, SC)
        jlast = 64 + 16 * (sp_i // 4) + 15
        p.waits(SC, [qwr_tok[jlast - 1], qwr_tok[jlast]])
        for fo4 in range(4):
            for dp in range(2):
                p.dmat(SC, w2T[:, fo4, dp * 1024:(dp + 1) * 1024],
                       w2q_b[dp * 1024:(dp + 1) * 1024,
                             (sp_i * 4 + fo4) * 128:(sp_i * 4 + fo4 + 1) * 128], r_w2s[0])
        t_w2 = p.full(r_w2s[0])
        B_w2s.wrote(t_w2)
        p.waits(TE, [t_wts, t_xnfm_all])
        for b in range(8):
            if gb[b] is not None:
                p.wait(TE, gb[b])
            if sp_i == 0 and pbank.get(b) is not None:
                p.wait(TE, pbank[b])
                pbank[b] = None
        htok = None
        for f in range(NF):
            for fo4 in range(4):
                A = wtile_sp[:, f, fo4 * 128:(fo4 + 1) * 128]
                for th in range(2):
                    bk = bank(fo4 * 2 + th)
                    B = xn_fm[:, f, th * 512:(th + 1) * 512]
                    st = (f == 0); spx = (f == NF - 1)
                    if spx and fo4 == 3 and th == 1:
                        htok = p.op(TE, lambda t, A=A, B=B, bk=bk, st=st, spx=spx:
                                    t.matmul(bk, A, B, start=st, stop=spx))
                    else:
                        p.opq(TE, lambda t, A=A, B=B, bk=bk, st=st, spx=spx:
                              t.matmul(bk, A, B, start=st, stop=spx))
        B_wts.read(htok)
        p.waits(SC, [htok, t_tbl])
        if g_last is not None:
            p.wait(SC, g_last)
        ev_last = None
        for fo4 in range(4):
            for th in range(2):
                ev_last = p.op(SC, lambda s, fo4=fo4, th=th, sp_i=sp_i:
                               s.activation(h_sp[:, fo4, th * 512:(th + 1) * 512],
                                            bank(fo4 * 2 + th), AF.Gelu,
                                            bias=b1c[:, sp_i * 4 + fo4:sp_i * 4 + fo4 + 1],
                                            scale=1.0))
        for m in range(32):
            db, th, b = m // 2, m % 2, m % 8
            bk = bank(b)
            p.waits(TE, [ev_last, t_w2])
            if gb[b] is not None:
                p.wait(TE, gb[b])
            tmm = None
            for fo4 in range(4):
                A = w2T[:, fo4, db * 128:(db + 1) * 128]
                B = h_sp[:, fo4, th * 512:(th + 1) * 512]
                st = (fo4 == 0)
                if fo4 == 3:
                    tmm = p.op(TE, lambda t, A=A, B=B, bk=bk, st=st:
                               t.matmul(bk, A, B, start=st, stop=True))
                else:
                    p.opq(TE, lambda t, A=A, B=B, bk=bk, st=st:
                          t.matmul(bk, A, B, start=st, stop=False))
            p.wait(VE, tmm)
            if sp_i == 0:
                t_gadd = p.op(VE, lambda v, db=db, th=th, bk=bk:
                              v.tensor_copy(g32[:, db, th * 512:(th + 1) * 512], bk))
            else:
                t_gadd = p.op(VE, lambda v, db=db, th=th, bk=bk:
                              v.tensor_tensor(g32[:, db, th * 512:(th + 1) * 512],
                                              g32[:, db, th * 512:(th + 1) * 512], bk, ALU.add))
            gb[b] = t_gadd
        g_last = tmm
        B_w2s.read(g_last)
        quant_write_step(2)
    t_q_done = [p.full(w_qw[0]), p.full(w_qw[1]), p.full(r_qw[0]), p.full(r_qw[1]), last_tqo[0]]

    silu_tok = [None] * 32
    p.waits(SC, [t_gadd, t_tbl] + t_q_done)
    for d in range(NF):
        for half in range(2):
            silu_tok[d * 2 + half] = p.op(SC, lambda s, d=d, half=half:
                                          s.activation(g_fm[:, d, half * 512:(half + 1) * 512],
                                                       g32[:, d, half * 512:(half + 1) * 512],
                                                       AF.Silu, bias=b2c[:, d:d + 1], scale=1.0))
    last_mm_half = silu_tok[31]
    gbank = gb

    ffn_gate = [t for t in silu_tok if t is not None]
    p.waits(GP, ffn_gate)
    p.waits(TE, [t for t in gbank if t is not None])
    p.dma(GP, dmaskT[:, :], dmaskT_d, r_stbl)
    p.dma(GP, qdecF[:, :], qdecF_d, r_stbl)
    t_stbl = p.full(r_stbl)
    p.wait(VE, t_kv_upd)
    t_kvbf = p.op(VE, lambda v: v.tensor_copy(KVbf[:, :], KV32[:, :]))
    if kvpre_d is not None:
        r_dbg = p.res("dbg")
        p.wait(GP, t_kvbf)
        p.dma(GP, kvpre_d, KV32[:, :], r_dbg)

    B_kq = Buf()
    last_tsr = None
    last_yn = None
    scoreP_par = [None, None]
    yev_par = [None, None]
    tYp = [None, None]
    P_pair = [P_bf, P_bfB]
    qt_pair = [qtile, qtileB]
    for grp in range(2):
        B_kq.before_write(p, SY)
        B_kq.before_write(p, SC)
        p.waits(SY, ffn_gate + t_bounce_all)
        p.waits(SC, ffn_gate + t_bounce_all)
        for f in range(NF):
            p.dmat(SC, k4[:, f, :], k_b[grp * 512:(grp + 1) * 512, f * 128:(f + 1) * 128], r_kq)
            p.dmat(SC, q4[:, f, :], q_b[grp * 512:(grp + 1) * 512, f * 128:(f + 1) * 128], r_kq)
        t_kq_all = p.full(r_kq)
        B_kq.wrote(t_kq_all)
        if kq_d is not None and grp == 0:
            if r_dbg is None:
                r_dbg = p.res("dbg")
            p.wait(GP, t_kq_all)
            p.dma(GP, kq_d[0], k4[:, :, :].rearrange("p a b -> p (a b)"), r_dbg)
            p.dma(GP, kq_d[1], q4[:, :, :].rearrange("p a b -> p (a b)"), r_dbg)
        for jj in range(4):
            j = grp * 4 + jj
            slot = j % 2
            bkv = B_kv[slot]
            bkv.before_write(p, SY)
            p.dma(SY, kvstr[slot][:, 0:E], kd_b[j * 128:(j + 1) * 128, :], r_kvld[slot])
            p.dma(SY, kvstr[slot][:, E:2 * E], v_b[j * 128:(j + 1) * 128, :], r_kvld[slot])
            tv = p.full(r_kvld[slot])
            bkv.wrote(tv)
            t_yev = []
            tdel = None
            tS_tok = [None] * H

            def emit_S(h, jj=jj):
                par = h % 2
                bkS = ps[:, 0:128] if par == 0 else ps[:, 1024:1152]
                p.waits(TE, [t_kq_all])
                if scoreP_par[par] is not None:
                    p.wait(TE, scoreP_par[par])
                if par == 1 and last_tsr is not None:
                    p.wait(TE, last_tsr)
                tS_tok[h] = p.op(TE, lambda t, h=h, jj=jj, bkS=bkS:
                                 t.matmul(bkS, k4[:, h, jj * 128:(jj + 1) * 128],
                                          q4[:, h, jj * 128:(jj + 1) * 128],
                                          start=True, stop=True))

            emit_S(0)
            for h in range(H):
                par = h % 2
                bkS = ps[:, 0:128] if par == 0 else ps[:, 1024:1152]
                if h + 1 < H:
                    emit_S(h + 1)
                Pb = P_pair[par]
                qt = qt_pair[par]
                p.waits(VE, [tS_tok[h], t_stbl])
                if tYp[par] is not None:
                    p.wait(VE, tYp[par])
                scoreP_par[par] = p.op(VE, lambda v, h=h, bkS=bkS, Pb=Pb:
                                       v.tensor_tensor(Pb[:, :], bkS,
                                                       dmaskT[:, h * 128:(h + 1) * 128],
                                                       ALU.mult))
                tqd = p.op(VE, lambda v, h=h, jj=jj, qt=qt:
                           v.tensor_tensor(qt[:, :], q4[:, h, jj * 128:(jj + 1) * 128],
                                           qdecF[:, h * 128:(h + 1) * 128], ALU.mult))
                bkY = ps[:, 512:640] if par == 0 else ps[:, 1536:1664]
                p.waits(TE, [scoreP_par[par], tqd, tv, t_kvbf])
                if yev_par[par] is not None:
                    p.wait(TE, yev_par[par])
                p.opq(TE, lambda t, h=h, slot=slot, bkY=bkY, Pb=Pb:
                      t.matmul(bkY, kvstr[slot][:, E + h * 128:E + (h + 1) * 128], Pb[:, :],
                               start=True, stop=False))
                tY = p.op(TE, lambda t, h=h, bkY=bkY, qt=qt:
                          t.matmul(bkY, KVbf[:, h * 128:(h + 1) * 128], qt[:, :],
                                   start=False, stop=True))
                tYp[par] = tY
                if last_yn is not None:
                    p.wait(SC, last_yn)
                p.wait(SC, tY)
                yev_par[par] = p.op(SC, lambda s, h=h, bkY=bkY: s.copy(Y_sb[:, h, :], bkY))
                t_yev.append(yev_par[par])
                bkD = ps[:, 2048 + h * 128:2048 + (h + 1) * 128]
                if t_kv_upd is not None:
                    p.wait(TE, t_kv_upd)
                if h == H - 1:
                    tdel = p.op(TE, lambda t, h=h, slot=slot, bkD=bkD:
                                t.matmul(bkD, kvstr[slot][:, h * 128:(h + 1) * 128],
                                         kvstr[slot][:, E + h * 128:E + (h + 1) * 128],
                                         start=True, stop=True))
                else:
                    p.opq(TE, lambda t, h=h, slot=slot, bkD=bkD:
                          t.matmul(bkD, kvstr[slot][:, h * 128:(h + 1) * 128],
                                   kvstr[slot][:, E + h * 128:E + (h + 1) * 128],
                                   start=True, stop=True))
            bkv.read(tdel)
            B_kq.read(tdel)
            p.waits(VE, [tdel])
            for hx in range(H):
                p.opq(VE, lambda v, hx=hx: v.tensor_scalar(
                    KV32[:, hx * 128:(hx + 1) * 128], KV32[:, hx * 128:(hx + 1) * 128],
                    _R128[hx], 0.0, ALU.mult, ALU.add))
            for bq_ in range(4):
                t_kv_upd = p.op(VE, lambda v, b=bq_: v.tensor_tensor(
                    KV32[:, b * 512:(b + 1) * 512], KV32[:, b * 512:(b + 1) * 512],
                    ps[:, 2048 + b * 512:2048 + (b + 1) * 512], ALU.add))
            t_kvbf = p.op(VE, lambda v: v.tensor_copy(KVbf[:, :], KV32[:, :]))
            p.waits(SC, t_yev)
            if ysb_d is not None:
                if r_dbg is None:
                    r_dbg = p.res("dbg")
                p.waits(GP, t_yev)
                tyd = p.dma(GP, ysb_d[j * 128:(j + 1) * 128, :],
                            Y_sb[:, :, :].rearrange("p a b -> p (a b)"), r_dbg)
                p.wait(SC, tyd)
            tsq = p.op(SC, lambda s: s.activation(Ysq[:, :, :].rearrange("p a b -> p (a b)"),
                                                  Y_sb[:, :, :].rearrange("p a b -> p (a b)"),
                                                  AF.Square))
            bkR = ps[:, 1024:1152]
            p.waits(TE, [tsq])
            if scoreP_par[1] is not None:
                p.wait(TE, scoreP_par[1])
            if last_tsr is not None:
                p.wait(TE, last_tsr)
            tss = None
            for e in range(NF):
                st = (e == 0); sp = (e == NF - 1)
                if sp:
                    tss = p.op(TE, lambda t, e=e, bkR=bkR, st=st:
                               t.matmul(bkR, ones128[:, :], Ysq[:, e, :], start=st, stop=True))
                else:
                    p.opq(TE, lambda t, e=e, bkR=bkR, st=st:
                          t.matmul(bkR, ones128[:, :], Ysq[:, e, :], start=st, stop=False))
            p.wait(SC, tss)
            last_tsr = p.op(SC, lambda s, bkR=bkR: s.activation(rtmp[:, :], bkR, AF.Sqrt,
                                                                bias=epsrms[:, 0:1], scale=1.0 / E))
            p.waits(VE, [last_tsr, t_tbl])
            trc = p.op(VE, lambda v: v.reciprocal(invb[:, :], rtmp[:, :]))
            p.op(VE, lambda v: v.tensor_copy(invb_bf[:, :], invb[:, :]))
            for e in range(NF):
                last_yn = p.op(VE, lambda v, e=e, j=j:
                               v.scalar_tensor_tensor(Yn_all[:, e, j * 128:(j + 1) * 128],
                                                      Y_sb[:, e, :], rmsc[:, e:e + 1],
                                                      invb_bf[:, :], ALU.mult, ALU.mult))

    if yn_d is not None:
        if r_dbg is None:
            r_dbg = p.res("dbg")
        p.waits(GP, [last_yn, last_yev])
        p.dma(GP, yn_d, Yn_all[:, :, :].rearrange("p a b -> p (a b)"), r_dbg)
    if rct_d is not None:
        r_rct = p.res("rct")
        p.wait(SY, last_yn)
        p.dma(SY, rct_d, rCt[:, :], r_rct)

    p.waits(GP, [last_yn, t_kv_upd])
    B_wo = Buf()
    B_wo.before_write(p, GP)
    for f in range(NF):
        p.dma(GP, wo_sb[:, f, :], wo_d[f * 128:(f + 1) * 128, :], r_wo)
    t_wo = p.full(r_wo)
    B_wo.wrote(t_wo)
    t_iota = p.op(GP, lambda g2: (g2.iota(iota_p[:, :], [[0, 128]], channel_multiplier=1,
                                          allow_small_or_imprecise_dtypes=True),
                                  g2.iota(iota_f[:, :], [[1, 128]], channel_multiplier=0,
                                          allow_small_or_imprecise_dtypes=True))[-1])
    p.waits(VE, [t_iota, last_yn])
    t_ident = p.op(VE, lambda v: v.tensor_tensor(ident[:, :], iota_p[:, :], iota_f[:, :], ALU.is_equal))

    B_sg = Buf()
    tT_last = {}
    sgo_tok = [None, None]
    w_sg_half = []
    for half in range(2):
        for foc in range(NF):
            bk_i = foc % 2
            bk = bank(bk_i)
            p.waits(TE, [t_wo, last_yn])
            if pbank.get(bk_i) is not None:
                p.wait(TE, pbank[bk_i])
                pbank[bk_i] = None
            tS = None
            for f in range(NF):
                A = wo_sb[:, f, foc * 128:(foc + 1) * 128]
                B = Yn_all[:, f, half * 512:(half + 1) * 512]
                st = (f == 0); sp = (f == NF - 1)
                if sp:
                    tS = p.op(TE, lambda t, A=A, B=B, bk=bk, st=st: t.matmul(bk, A, B, start=st, stop=True))
                else:
                    p.opq(TE, lambda t, A=A, B=B, bk=bk, st=st: t.matmul(bk, A, B, start=st, stop=False))
            B_sg.before_write(p, VE)
            p.wait(VE, tS)
            tg = p.op(VE, lambda v, bk=bk, foc=foc, half=half:
                      v.tensor_tensor(sgld[:, 0:512], bk, g_fm[:, foc, half * 512:(half + 1) * 512],
                                      ALU.mult))
            pbank[bk_i] = tg
            B_sg.wrote(tg)
            bkT_i = 4 + (foc % 2)
            slot = foc % 2
            p.waits(TE, [tg, t_ident])
            if tT_last.get(bkT_i) is not None:
                p.wait(TE, tT_last[bkT_i])
            tT = None
            for cs in range(4):
                bkT = bank(bkT_i, 512)[:, cs * 128:(cs + 1) * 128]
                tT = p.op(TE, lambda t, cs=cs, bkT=bkT:
                          t.transpose(bkT, sgld[:, cs * 128:(cs + 1) * 128], ident[:, :]))
            B_sg.read(tT)
            p.wait(SC, tT)
            if sgo_tok[slot] is not None:
                p.wait(SC, sgo_tok[slot])
            tcp = p.op(SC, lambda s, bkT_i=bkT_i, slot=slot:
                       s.copy(sgout2[:, slot, :], bank(bkT_i, 512)))
            tT_last[bkT_i] = tcp
            p.wait(GP, tcp)
            dst = sg_b[half * 512:(half + 1) * 512,
                       foc * 128:(foc + 1) * 128].rearrange("(a p) c -> p a c", p=128)
            p.dma(GP, dst, sgout2[:, slot, :].rearrange("p (a c) -> p a c", c=128), w_sg[slot])
            sgo_tok[slot] = p.full(w_sg[slot])
        w_sg_half.append(p.full(w_sg[0]))
        w_sg_half.append(p.full(w_sg[1]))

    B_sgld = Buf()
    for cch in range(NCH):
        p.waits(SY, w_sg_half[2 * (cch // 4):2 * (cch // 4) + 2])
        B_sgld.before_write(p, SY)
        p.dma(SY, sgf[:, :], sg_b[cch * 128:(cch + 1) * 128, :], r_fin)
        p.dma(SY, xbf[:, :], xo[cch * 128:(cch + 1) * 128, :], r_fin)
        t_both = p.full(r_fin)
        B_sgld.wrote(t_both)
        p.waits(VE, [t_both])
        ta = p.op(VE, lambda v: v.tensor_tensor(sgf[:, :], sgf[:, :], xbf[:, :], ALU.add))
        B_sgld.read(ta)
        p.wait(GP, ta)
        tst = p.dma(GP, out_d[cch * 128:(cch + 1) * 128, :], sgf[:, :], w_out)
        B_sgld.read(tst)

    fin = w_out.cnt
    p.ops[GP].append(lambda g2, sem=w_out.sem, v=fin: g2.wait_ge(sem, v))

    block = ctx.enter_context(nc.Block())
    p.finalize(block)
    return nc


def kernel(L0_attn_x, L0_attn_Wq, L0_attn_Wk, L0_attn_Wv, L0_attn_Wo, L0_attn_rms_w,
           L0_ffn_w1, L0_ffn_b1, L0_ffn_w2, L0_ffn_b2):
    x = np.ascontiguousarray(np.asarray(L0_attn_x, dtype=np.float32))
    wq = np.ascontiguousarray(np.asarray(L0_attn_Wq, dtype=np.float32))
    wk = np.ascontiguousarray(np.asarray(L0_attn_Wk, dtype=np.float32))
    wv = np.ascontiguousarray(np.asarray(L0_attn_Wv, dtype=np.float32))
    wo = np.ascontiguousarray(np.asarray(L0_attn_Wo, dtype=np.float32))
    rms_w = np.asarray(L0_attn_rms_w, dtype=np.float32)
    w1 = np.ascontiguousarray(np.asarray(L0_ffn_w1, dtype=np.float32))
    b1 = np.asarray(L0_ffn_b1, dtype=np.float32)
    w2 = np.ascontiguousarray(np.asarray(L0_ffn_w2, dtype=np.float32))
    b2 = np.asarray(L0_ffn_b2, dtype=np.float32)

    qdecF, kdec, dmaskT, rCt, freq = _host_tables()
    b1c = np.ascontiguousarray(b1.reshape(DFF // 128, 128).T)
    b2c = np.ascontiguousarray(b2.reshape(NF, 128).T)
    rmsc = np.ascontiguousarray(rms_w.reshape(NF, 128).T)

    in_maps = []
    for core in range(8):
        b, s = core // 4, core % 4
        xo_h = np.ascontiguousarray(x[b, s * T_OWN:(s + 1) * T_OWN])
        pre_rows = s * T_OWN
        xp_h = np.zeros((T_PRE, E), dtype=np.float32)
        if pre_rows > 0:
            take = min(pre_rows, T_PRE)
            xp_h[T_PRE - take:] = x[b, pre_rows - take:pre_rows]
        t_own = s * T_OWN + np.arange(T_OWN)
        t_pre = s * T_OWN - T_PRE + np.arange(T_PRE)
        cos_o, sin_o = _trig(t_own, freq)
        cos_p, sin_p = _trig(t_pre, freq)
        in_maps.append({
            "xo": xo_h, "xp": xp_h,
            "wq": wq, "wk": wk, "wv": wv, "wo": wo,
            "w1": w1, "w2": w2,
            "b1c": b1c, "b2c": b2c, "rmsc": rmsc,
            "cos_o": cos_o, "sin_o": sin_o, "cos_p": cos_p, "sin_p": sin_p,
            "qdecF": qdecF, "kdec": kdec, "dmaskT": dmaskT, "rCt": rCt,
        })

    with ExitStack() as ctx:
        nc = build_program(ctx)
        res = run_bass_kernel_spmd(nc, in_maps, core_ids=list(range(8)))
    global LAST_RES
    LAST_RES = res

    out = np.empty((2, 4096, E), dtype=np.float32)
    for core in range(8):
        b, s = core // 4, core % 4
        out[b, s * T_OWN:(s + 1) * T_OWN] = res.results[core]["out"]
    return out


# version 17 (final)
# speedup vs baseline: 1.1596x; 1.0334x over previous
import math
import numpy as np
from contextlib import ExitStack

import concourse.bass as bass
import concourse.mybir as mybir
from concourse.bass_utils import run_bass_kernel_spmd

F32 = mybir.dt.float32
BF16 = mybir.dt.bfloat16
AF = mybir.ActivationFunctionType
ALU = mybir.AluOpType
AX = mybir.AxisListType

E = 2048
H = 16
D = 128
C = 128
T_OWN = 1024
T_PRE = 3072
NCH = T_OWN // C
NPC = T_PRE // C
DFF = 8192
NF = E // 128
RMS_EPS = 1e-5
LN_EPS = 1e-5
SY, SC, VE, TE, GP = "sync", "scalar", "vector", "tensor", "gpsimd"
KB = 1024
DBG = []
LAST_RES = None


_R128 = None


def _decay_r128():
    start = 2.0 ** (-(2.0 ** -(math.log2(H) - 3)))
    slopes = np.array([start * start ** i for i in range(H)], dtype=np.float64)
    r = np.exp(-slopes)
    return [float(np.float32(rv ** 128.0)) for rv in r]


def _host_tables():
    start = 2.0 ** (-(2.0 ** -(math.log2(H) - 3)))
    slopes = np.array([start * start ** i for i in range(H)], dtype=np.float64)
    r = np.exp(-slopes)
    c_idx = np.arange(C, dtype=np.float64)
    qdecF = np.zeros((128, E), dtype=np.float64)
    kdec = np.zeros((128, E), dtype=np.float64)
    dmaskT = np.zeros((128, E), dtype=np.float64)
    rCt = np.zeros((128, E), dtype=np.float64)
    for h in range(H):
        qdecF[:, h * 128:(h + 1) * 128] = (r[h] ** (c_idx + 1.0))[None, :]
        kdec[:, h * 128:(h + 1) * 128] = (r[h] ** (C - 1.0 - c_idx))[:, None]
        diff = c_idx[None, :] - c_idx[:, None]
        dm = np.where(diff >= 0, r[h] ** np.maximum(diff, 0.0), 0.0)
        dmaskT[:, h * 128:(h + 1) * 128] = dm
        rCt[:, h * 128:(h + 1) * 128] = r[h] ** float(C)
    freq = (1.0 / 1024.0) ** np.linspace(0.0, 1.0, D // 4, dtype=np.float64)
    return (qdecF.astype(np.float32), kdec.astype(np.float32),
            dmaskT.astype(np.float32), rCt.astype(np.float32), freq)


def _trig(t_abs, freq):
    theta = np.asarray(t_abs, dtype=np.float64)[:, None] * freq[None, :]
    cos = np.tile(np.cos(theta), (1, H)).astype(np.float32)
    sin = np.tile(np.sin(theta), (1, H)).astype(np.float32)
    return np.ascontiguousarray(cos), np.ascontiguousarray(sin)


class Res:
    __slots__ = ("name", "sem", "cnt")

    def __init__(self, name, sem):
        self.name = name
        self.sem = sem
        self.cnt = 0


class Prog:
    CENGS = (SC, VE, TE, GP)

    def __init__(self, nc, ctx):
        self.nc = nc
        self.ctx = ctx
        self.ops = {e: [] for e in (SY, SC, VE, TE, GP)}
        self.csems = {e: ctx.enter_context(nc.semaphore("c_" + e)) for e in self.CENGS}
        self.ccnt = {e: 0 for e in self.CENGS}
        self.waited = {e: {} for e in (SY, SC, VE, TE, GP)}

    def res(self, name):
        return Res(name, self.ctx.enter_context(self.nc.semaphore("r_" + name)))

    def dma(self, eng, out, in_, r):
        r.cnt += 16
        val = r.cnt
        self.ops[eng].append(lambda e, out=out, in_=in_, s=r.sem:
                             e.dma_start(out=out, in_=in_).then_inc(s, 16))
        return ("r_" + r.name, r.sem, val)

    def dmat(self, eng, out, in_, r):
        r.cnt += 16
        val = r.cnt
        self.ops[eng].append(lambda e, out=out, in_=in_, s=r.sem:
                             e.dma_start_transpose(out, in_).then_inc(s, 16))
        return ("r_" + r.name, r.sem, val)

    SELFSYNC = (VE, SC, GP)

    def op(self, eng, fn):
        sem = self.csems[eng]
        if eng in self.SELFSYNC and self.ccnt[eng] > 0:
            prev = self.ccnt[eng]
            self.ops[eng].append(lambda e, sem=sem, v=prev: e.wait_ge(sem, v))
        self.ccnt[eng] += 1
        val = self.ccnt[eng]
        self.ops[eng].append(lambda e, fn=fn, sem=sem: fn(e).then_inc(sem, 1))
        return ("c_" + eng, sem, val)

    def opq(self, eng, fn):
        if eng in self.SELFSYNC:
            self.op(eng, fn)
        else:
            self.ops[eng].append(fn)

    def ctok(self, eng):
        return ("c_" + eng, self.csems[eng], self.ccnt[eng])

    def wait(self, eng, tok):
        if tok is None:
            return
        key, sem, v = tok
        if key == "c_" + eng:
            return
        if self.waited[eng].get(key, 0) >= v:
            return
        self.waited[eng][key] = v
        self.ops[eng].append(lambda e, sem=sem, v=v: e.wait_ge(sem, v))

    def waits(self, eng, toks):
        for t in toks:
            self.wait(eng, t)

    def full(self, r):
        return ("r_" + r.name, r.sem, r.cnt)

    def finalize(self, block):
        for name in (SY, SC, VE, TE, GP):
            ops = self.ops[name]
            getattr(block, name)(lambda e, ops=ops: [f(e) for f in ops])


class Buf:
    def __init__(self):
        self.rd = []
        self.wr = None

    def before_write(self, p, eng):
        p.waits(eng, self.rd)
        if self.wr is not None:
            p.wait(eng, self.wr)

    def wrote(self, tok):
        self.wr = tok
        self.rd = []

    def read(self, tok):
        if tok is not None:
            self.rd.append(tok)


def build_program(ctx):
    global _R128
    _R128 = _decay_r128()
    nc = bass.Bass("TRN2", target_bir_lowering=False, dynamic_dma_scratch_size=16384)

    def din(name, shape, dtype=F32):
        return nc.dram_tensor(name, list(shape), dtype, kind="ExternalInput").ap()

    def dint(name, shape, dtype=BF16):
        kind = "ExternalOutput" if name in DBG else "Internal"
        return nc.dram_tensor(name, list(shape), dtype, kind=kind).ap()

    def sb(name, shape, dtype=F32):
        return ctx.enter_context(nc.sbuf_tensor(name, list(shape), dtype))

    xo = din("xo", (T_OWN, E))
    xp = din("xp", (T_PRE, E))
    wq_d = din("wq", (E, E)); wk_d = din("wk", (E, E))
    wv_d = din("wv", (E, E)); wo_d = din("wo", (E, E))
    w1_d = din("w1", (DFF, E)); w2_d = din("w2", (E, DFF))
    b1c_d = din("b1c", (128, DFF // 128)); b2c_d = din("b2c", (128, NF))
    rmsc_d = din("rmsc", (128, NF))
    cos_o = din("cos_o", (T_OWN, 512)); sin_o = din("sin_o", (T_OWN, 512))
    cos_p = din("cos_p", (T_PRE, 512)); sin_p = din("sin_p", (T_PRE, 512))
    qdecF_d = din("qdecF", (128, E)); kdec_d = din("kdec", (128, E))
    dmaskT_d = din("dmaskT", (128, E))
    out_d = nc.dram_tensor("out", [T_OWN, E], F32, kind="ExternalOutput").ap()

    xn_b = dint("xn_b", (T_OWN, E))
    xnp_b = dint("xnp_b", (T_PRE, E))
    q_b = dint("q_b", (T_OWN, E)); k_b = dint("k_b", (T_OWN, E))
    kd_b = dint("kd_b", (T_OWN, E)); v_b = dint("v_b", (T_OWN, E))
    kdp_b = dint("kdp_b", (T_PRE, E)); vp_b = dint("vp_b", (T_PRE, E))
    w1q_b = dint("w1q_b", (DFF, E)); w2q_b = dint("w2q_b", (E, DFF))
    sg_b = dint("sg_b", (T_OWN, E), F32)
    kvpre_d = dint("kvpre_d", (128, E), F32) if "kvpre_d" in DBG else None
    yn_d = dint("yn_d", (128, NF * 1024)) if "yn_d" in DBG else None
    ysb_d = dint("ysb_d", (T_OWN, E)) if "ysb_d" in DBG else None
    kq_d = dint("kq_d", (2, 128, NF * 512)) if "kq_d" in DBG else None
    pjp_d = dint("pjp_d", (T_PRE, E), F32) if "pjp_d" in DBG else None
    pjr_d = dint("pjr_d", (T_PRE, E), F32) if "pjr_d" in DBG else None
    ksb_d = dint("ksb_d", (128, E)) if "ksb_d" in DBG else None
    kvs_d = dint("kvs_d", (128, 2 * E)) if "kvs_d" in DBG else None
    rct_d = dint("rct_d", (128, E), F32) if "rct_d" in DBG else None
    r_dbg = None

    KV32 = sb("KV32", (128, E))
    kvstr = [sb(f"kvstr{i}", (128, 2 * E), BF16) for i in range(2)]
    g_fm = sb("g_fm", (128, NF, 1024), BF16)
    b2c = sb("b2c_s", (128, NF)); rmsc = sb("rmsc_s", (128, NF))
    b1c = sb("b1c_s", (128, DFF // 128))
    ones128 = sb("ones128", (128, 128), BF16)
    ones128f = sb("ones128f", (128, 128))
    epsln = sb("epsln", (128, 1)); epsrms = sb("epsrms", (128, 1))
    stat = sb("stat", (128, 16))
    part1 = sb("part1", (128, 64)); part2 = sb("part2", (128, 64))
    pred = sb("pred", (128, 2)); sc12 = sb("sc12", (128, 2)); inv12 = sb("inv12", (128, 2))

    base = (nc.sbuf_base + 31) & ~31
    SB_TOP = 229344
    R1, R2, R3 = base, base + 32 * KB, base + 96 * KB
    need = R3 + 48 * KB
    assert need <= SB_TOP, f"scratch overflow: need {need}, top {SB_TOP}"

    def at(name, off, shape, dtype=F32):
        return nc.alloc_sbuf_tensor_at(name, list(shape), dtype, offset=off)

    xn_fm = at("xn_fm", R1, (128, NF, 1024), BF16)
    Yn_all = at("Yn_all", R1, (128, NF, 1024), BF16)
    xbuf = [at(f"xbuf{i}", R2 + i * 8 * KB, (128, E)) for i in range(2)]
    xnbf = [at(f"xnbf{i}", R2 + 16 * KB + i * 4 * KB, (128, E), BF16) for i in range(2)]
    junk = at("junk", R2 + 24 * KB, (128, E))
    wrs = at("wrs", R2, (128, NF, E), BF16)
    g_addr = nc.lookup_mloc("g_fm").addr
    qw32 = [at(f"qw32{i}", g_addr + i * 8 * KB, (128, E)) for i in range(2)]
    junkq = at("junkq", g_addr + 16 * KB, (128, E))
    qout = [at(f"qout{i}", g_addr + 24 * KB + i * 4 * KB, (128, E), BF16) for i in range(2)]
    qmsk = at("qmsk", g_addr + 16 * KB, (128, 2 * E), BF16)
    g32 = at("g32", R2, (128, NF, 1024))
    k4 = at("k4", R2, (128, NF, 512), BF16)
    q4 = at("q4", R2 + 16 * KB, (128, NF, 512), BF16)
    wo_sb = at("wo_sb", R2, (128, NF, E), BF16)
    xnp_g = [at(f"xnp_g{i}", R3 + (0 if i == 0 else 40) * KB, (128, NF, 256), BF16)
             for i in range(2)]
    kdec = at("kdec_s", R3 + 36 * KB, (128, E), BF16)
    KVbf = at("KVbf", R3 + 24 * KB, (128, E), BF16)
    pj32 = [at(f"pj32{i}", R3 + 8 * KB + i * 8 * KB, (128, E)) for i in range(2)]
    trigc = [at(f"trigc{i}", R3 + 24 * KB + i * 2 * KB, (128, 512)) for i in range(2)]
    trigs = [at(f"trigs{i}", R3 + 28 * KB + i * 2 * KB, (128, 512)) for i in range(2)]
    rotmp = at("rotmp", R3 + 32 * KB, (128, 1024))
    h_sp = at("h_sp", R3, (128, 4, 1024), BF16)
    wtile_sp = at("wtile_sp", R3 + 8 * KB, (128, 16, 512), BF16)
    w2T = at("w2T", R3 + 24 * KB, (128, 4, 2048), BF16)
    Y_sb = at("Y_sb", R3, (128, NF, 128), BF16)
    Ysq = at("Ysq", R3 + 4 * KB, (128, NF, 128), BF16)
    qtile = at("qtile", R3 + 8 * KB, (128, 128), BF16)
    P_bf = at("P_bf", R3 + 8 * KB + 256, (128, 128), BF16)
    invb_bf = at("invb_bf", R3 + 8 * KB + 512, (128, 128), BF16)
    P_bfB = at("P_bfB", R3 + 8 * KB + 768, (128, 128), BF16)
    qtileB = at("qtileB", R3 + 18 * KB, (128, 128), BF16)
    invb = at("invb", R3 + 9 * KB, (128, 128))
    rtmp = at("rtmp", R3 + 9 * KB + 512, (128, 128))
    dmaskT = at("dmaskT_s", R3 + 10 * KB, (128, E), BF16)
    qdecF = at("qdecF_s", R3 + 14 * KB, (128, E), BF16)
    sgout2 = at("sgout2", R3 + 20 * KB, (128, 2, 512))
    sgf = at("sgf", R3 + 24 * KB, (128, E))
    xbf = at("xbf", R3 + 12 * KB, (128, E))
    ident = at("ident", R3 + KB, (128, 128))
    iota_p = at("iota_p", R3 + 2 * KB, (128, 128))
    iota_f = at("iota_f", R3 + 3 * KB, (128, 128))
    sgld = at("sgld", R3 + 4 * KB, (128, E))
    xb2 = at("xb2", R3 + 12 * KB, (128, E))

    ps = ctx.enter_context(nc.psum_tensor("ps", [128, 4096], F32))

    def bank(i, w=512):
        return ps[:, i * 512:i * 512 + w]

    p = Prog(nc, ctx)

    r_tbl = p.res("tbl")
    r_tblg = p.res("tblg")
    r_x = [p.res("x0"), p.res("x1")]
    r_trig = [p.res("tg0"), p.res("tg1")]
    r_qw = [p.res("qw0"), p.res("qw1")]
    r_kvld = [p.res("kv0"), p.res("kv1")]
    r_wt = [p.res("wt0"), p.res("wt1")]
    r_w2s = [p.res("w2s0")]
    r_wrs = p.res("wrsl")
    r_xnp = [p.res("xnpl0"), p.res("xnpl1")]
    r_xnfm = p.res("xnfm")
    r_kq = p.res("kq")
    r_wo = p.res("wol")
    r_stbl = p.res("stbl")
    r_fin = p.res("fin")
    w_xno = [p.res("wxno0"), p.res("wxno1")]
    w_xnp = [p.res("wxnp0"), p.res("wxnp1")]
    w_kv = [p.res("wkv0"), p.res("wkv1")]
    w_kk = [p.res("wkk0"), p.res("wkk1")]
    w_qw = [p.res("wqw0"), p.res("wqw1")]
    w_sg = [p.res("wsg0"), p.res("wsg1")]
    w_out = p.res("wout")
    r_pjd = p.res("pjd") if (pjp_d is not None or pjr_d is not None) else None
    r_kvs = p.res("kvs") if kvs_d is not None else None

    p.dma(GP, kdec[:, :], kdec_d, r_tblg)
    p.dma(SY, b2c[:, :], b2c_d, r_tbl)
    p.dma(SY, rmsc[:, :], rmsc_d, r_tbl)
    p.dma(SY, b1c[:, :], b1c_d, r_tbl)
    t_tbl = p.full(r_tbl)
    t_kdec = p.full(r_tblg)
    if ksb_d is not None:
        r_ksb = p.res("ksb")
        p.wait(SY, t_kdec)
        p.dma(SY, ksb_d, kdec[:, :], r_ksb)

    def init_fn(v):
        v.memset(KV32[:, :], 0.0)
        v.memset(ones128[:, :], 1.0)
        v.memset(ones128f[:, :], 1.0)
        v.memset(epsln[:, 0:1], LN_EPS)
        return v.memset(epsrms[:, 0:1], RMS_EPS)
    p.op(VE, init_fn)

    B_x = [Buf(), Buf()]
    B_xnbf = [Buf(), Buf()]
    B_trig = [Buf(), Buf()]
    B_pj = [Buf(), Buf()]
    B_kv = [Buf(), Buf()]
    B_qw = [Buf(), Buf()]
    B_qo = [Buf(), Buf()]
    B_wt = [Buf(), Buf()]
    B_stat = [Buf(), Buf()]

    t_sent = None
    if "sent" in DBG:
        r_sent = p.res("sent")
        tms = p.op(VE, lambda v: v.memset(kvstr[0][:, 0:E], 7.0))
        p.wait(SY, tms)
        for j in range(NPC):
            p.dma(SY, kdp_b[j * 128:(j + 1) * 128, :], kvstr[0][:, 0:E], r_sent)
        t_sent = p.full(r_sent)
        B_kv[0].read(t_sent)
        p.waits(GP, [t_sent])

    w1_tiles = [w1_d[i * 128:(i + 1) * 128, :] for i in range(64)]
    w2_tiles = [w2_d[(i // 4) * 128:(i // 4 + 1) * 128, (i % 4) * 2048:(i % 4 + 1) * 2048]
                for i in range(64)]
    w1q_tiles = [w1q_b[i * 128:(i + 1) * 128, :] for i in range(64)]
    w2q_tiles = [w2q_b[(i // 4) * 128:(i // 4 + 1) * 128, (i % 4) * 2048:(i % 4 + 1) * 2048]
                 for i in range(64)]
    qred_jobs = [(t, part1, i) for i, t in enumerate(w1_tiles)] + \
                [(t, part2, i) for i, t in enumerate(w2_tiles)]
    qred_pos = [0]

    def quant_reduce_step(nsteps):
        for _ in range(nsteps):
            i = qred_pos[0]
            if i >= len(qred_jobs):
                return
            qred_pos[0] = i + 1
            tile, part, col = qred_jobs[i]
            slot = i % 2
            bq = B_qw[slot]
            bq.before_write(p, SY)
            tl = p.dma(SY, qw32[slot][:, :], tile, r_qw[slot])
            bq.wrote(tl)
            if i % 2 == 0:
                p.wait(VE, tl)
                tr = p.op(VE, lambda v, slot=slot, part=part, col=col:
                          v.scalar_tensor_tensor(qw32[slot][:, :], qw32[slot][:, :], -1.0,
                                                 qw32[slot][:, :], ALU.mult, ALU.max,
                                                 accum_out=part[:, col:col + 1]))
            else:
                p.wait(SC, tl)
                tr = p.op(SC, lambda s, slot=slot, part=part, col=col:
                          s.activation(qw32[slot][:, :], qw32[slot][:, :], AF.Abs,
                                       accum_out=part[:, col:col + 1]))
            bq.read(tr)

    w2_order = [r * 4 + cb for cb in range(4) for r in range(16)]
    qwr_jobs = list(zip(w1_tiles, w1q_tiles, [0] * 64)) + \
               [(w2_tiles[i], w2q_tiles[i], 1) for i in w2_order]
    qwr_pos = [0]
    qwr_tok = [None] * len(qwr_jobs)
    tq0_box = [None]
    last_tqo = [None]

    def quant_write_step(nsteps):
        tq0 = tq0_box[0]
        if tq0 is None:
            return
        for _ in range(nsteps):
            i = qwr_pos[0]
            if i >= len(qwr_jobs):
                return
            qwr_pos[0] = i + 1
            tile, dst, which = qwr_jobs[i]
            inv_ap = inv12[:, which:which + 1]
            wres = w_qw[i % 2]
            slot = i % 2
            bq = B_qw[slot]
            bq.before_write(p, SY)
            tl = p.dma(SY, qw32[slot][:, :], tile, r_qw[slot])
            bq.wrote(tl)
            p.waits(VE, [tl, tq0])
            p.opq(VE, lambda v, slot=slot, inv_ap=inv_ap:
                  v.tensor_scalar(qmsk[:, 0:E], qw32[slot][:, :], inv_ap, 0.5,
                                  ALU.mult, ALU.is_ge))
            tm2 = p.op(VE, lambda v, slot=slot, inv_ap=inv_ap:
                       v.tensor_scalar(qmsk[:, E:2 * E], qw32[slot][:, :], inv_ap, -0.5,
                                       ALU.mult, ALU.is_le))
            bq.read(tm2)
            bo = B_qo[slot]
            bo.before_write(p, VE)
            tqo = p.op(VE, lambda v, slot=slot: v.tensor_tensor(qout[slot][:, :], qmsk[:, 0:E],
                                                                qmsk[:, E:2 * E], ALU.subtract))
            bo.wrote(tqo)
            last_tqo[0] = tqo
            p.wait(GP, tqo)
            tw = p.dma(GP, dst, qout[slot][:, :], wres)
            qwr_tok[i] = tw
            bo.read(tw)

    def emit_ln(src, dst_b, w_res, idx, slot):
        bx = B_x[slot]
        bx.before_write(p, SY)
        tl = p.dma(SY, xbuf[slot][:, :], src[idx * 128:(idx + 1) * 128, :], r_x[slot])
        bx.wrote(tl)
        x_ap = xbuf[slot][:, :]
        st8 = 8 * slot
        p.wait(VE, tl)
        B_stat[slot].before_write(p, VE)
        p.opq(VE, lambda v: v.tensor_scalar(junk[:, :], x_ap, 1.0, 0.0, ALU.mult, ALU.add,
                                            accum_out=stat[:, st8 + 0:st8 + 1]))
        p.opq(VE, lambda v: v.scalar_tensor_tensor(junk[:, :], x_ap, 1.0, x_ap,
                                                   ALU.mult, ALU.mult, accum_out=stat[:, st8 + 1:st8 + 2]))
        p.opq(VE, lambda v: v.tensor_scalar(stat[:, st8 + 2:st8 + 3], stat[:, st8 + 0:st8 + 1], 1.0 / E, 0.0, ALU.mult, ALU.add))
        p.opq(VE, lambda v: v.scalar_tensor_tensor(stat[:, st8 + 3:st8 + 4], stat[:, st8 + 2:st8 + 3], -1.0, stat[:, st8 + 2:st8 + 3],
                                                   ALU.mult, ALU.mult))
        tv = p.op(VE, lambda v: v.scalar_tensor_tensor(stat[:, st8 + 4:st8 + 5], stat[:, st8 + 1:st8 + 2], 1.0 / E,
                                                       stat[:, st8 + 3:st8 + 4], ALU.mult, ALU.add))
        p.wait(SC, tv)
        tsq = p.op(SC, lambda s: s.activation(stat[:, st8 + 5:st8 + 6], stat[:, st8 + 4:st8 + 5], AF.Sqrt,
                                              bias=epsln[:, 0:1], scale=1.0))
        p.wait(VE, tsq)
        p.opq(VE, lambda v: v.reciprocal(stat[:, st8 + 6:st8 + 7], stat[:, st8 + 5:st8 + 6]))
        tnm = p.op(VE, lambda v: v.scalar_tensor_tensor(stat[:, st8 + 7:st8 + 8], stat[:, st8 + 2:st8 + 3], -1.0,
                                                        stat[:, st8 + 6:st8 + 7], ALU.mult, ALU.mult))
        bn = B_xnbf[slot]
        bn.before_write(p, SC)
        p.wait(SC, tnm)
        txn = p.op(SC, lambda s: s.activation(xnbf[slot][:, :], x_ap, AF.Identity,
                                              bias=stat[:, st8 + 7:st8 + 8], scale=stat[:, st8 + 6:st8 + 7]))
        B_stat[slot].wrote(txn)
        bx.read(txn)
        bn.wrote(txn)
        p.wait(GP, txn)
        tw = p.dma(GP, dst_b[idx * 128:(idx + 1) * 128, :], xnbf[slot][:, :], w_res)
        bn.read(tw)
        return tw

    for i in range(NCH):
        emit_ln(xo, xn_b, w_xno[i % 2], i, i % 2)
        quant_reduce_step(1)
    for i in range(NPC):
        emit_ln(xp, xnp_b, w_xnp[i % 2], i, i % 2)
        quant_reduce_step(2)
    t_xno_all = [p.full(w_xno[0]), p.full(w_xno[1])]
    t_xnp_all = [p.full(w_xnp[0]), p.full(w_xnp[1])]

    p.waits(SC, t_xno_all)
    for f in range(NF):
        p.dmat(SC, xn_fm[:, f, :], xn_b[0:T_OWN, f * 128:(f + 1) * 128], r_xnfm)
    t_xnfm_all = p.full(r_xnfm)

    B_wrs = Buf()

    def load_W(w_dram):
        B_wrs.before_write(p, GP)
        for f in range(NF):
            p.dma(GP, wrs[:, f, :], w_dram[f * 128:(f + 1) * 128, :], r_wrs)
        tok = p.full(r_wrs)
        B_wrs.wrote(tok)
        return tok

    pbank = {}

    def proj_chunk(xn_t, banks, out_write, wait_toks):
        etoks = []
        for n in range(4):
            bk_i = banks[n % len(banks)]
            bk = bank(bk_i)
            p.waits(TE, wait_toks)
            if pbank.get(bk_i) is not None:
                p.wait(TE, pbank[bk_i])
            tmm = None
            for f in range(NF):
                A = xn_t(f)
                B = wrs[:, f, n * 512:(n + 1) * 512]
                st = (f == 0); sp = (f == NF - 1)
                if sp:
                    tmm = p.op(TE, lambda t, A=A, B=B, bk=bk, st=st: t.matmul(bk, A, B, start=st, stop=True))
                else:
                    p.opq(TE, lambda t, A=A, B=B, bk=bk, st=st: t.matmul(bk, A, B, start=st, stop=False))
            p.wait(SC, tmm)
            te = out_write(n, bk)
            pbank[bk_i] = te
            etoks.append(te)
        return etoks

    def head3d(t, base_c, width=32):
        return t[:, :].rearrange("p (h d) -> p h d", d=128)[:, :, base_c:base_c + width]

    def rep3d(t, width=32):
        return t[:, :].rearrange("p (h d) -> p h d", d=32)[:, :, 0:width]

    def emit_rotary(dst32, cosb, sinb):
        x1 = head3d(dst32, 0); x2 = head3d(dst32, 64)
        tm1 = rotmp[:, 0:512].rearrange("p (h d) -> p h d", d=32)
        tm2 = rotmp[:, 512:1024].rearrange("p (h d) -> p h d", d=32)
        cs = rep3d(cosb); sn = rep3d(sinb)
        p.opq(VE, lambda v: v.tensor_tensor(tm1, x2, sn, ALU.mult))
        p.opq(VE, lambda v: v.tensor_tensor(tm2, x1, sn, ALU.mult))
        p.opq(VE, lambda v: v.tensor_tensor(x2, x2, cs, ALU.mult))
        p.opq(VE, lambda v: v.tensor_tensor(x2, x2, tm2, ALU.subtract))
        p.opq(VE, lambda v: v.tensor_tensor(x1, x1, cs, ALU.mult))
        return p.op(VE, lambda v: v.tensor_tensor(x1, x1, tm1, ALU.add))

    def load_trig(slot, cos_src, sin_src, j):
        bt = B_trig[slot]
        bt.before_write(p, SY)
        p.dma(SY, trigc[slot][:, :], cos_src[j * 128:(j + 1) * 128, :], r_trig[slot])
        p.dma(SY, trigs[slot][:, :], sin_src[j * 128:(j + 1) * 128, :], r_trig[slot])
        t2 = p.full(r_trig[slot])
        bt.wrote(t2)
        return t2

    def evac_pj(slot):
        def w(n, bk):
            if n == 0:
                B_pj[slot].before_write(p, SC)
            return p.op(SC, lambda s, bk=bk, n=n: s.copy(pj32[slot][:, n * 512:(n + 1) * 512], bk))
        return w

    p.waits(GP, t_xno_all + t_xnp_all)
    t_wk = load_W(wk_d)
    wrs_readers = []

    def kproj(xn_t, slot, wait_toks, trig_tok, bounce_kd, bounce_k, dbg_j=None):
        et = proj_chunk(xn_t, (0, 1, 2, 3), evac_pj(slot), wait_toks)
        B_pj[slot].wrote(et[-1])
        wrs_readers.append(et[-1])
        if dbg_j is not None and pjp_d is not None:
            p.waits(GP, et)
            tdp = p.dma(GP, pjp_d[dbg_j * 128:(dbg_j + 1) * 128, :], pj32[slot][:, :], r_pjd)
            B_pj[slot].read(tdp)
            p.waits(VE, [tdp])
        p.waits(VE, et + [trig_tok, t_kdec])
        trot = emit_rotary(pj32[slot], trigc[slot], trigs[slot])
        B_trig[slot].read(trot)
        if dbg_j is not None and pjr_d is not None:
            p.wait(GP, trot)
            tdr = p.dma(GP, pjr_d[dbg_j * 128:(dbg_j + 1) * 128, :], pj32[slot][:, :], r_pjd)
            B_pj[slot].read(tdr)
        bkv = B_kv[slot]
        bkv.before_write(p, VE)
        tkr = p.op(VE, lambda v, slot=slot: v.tensor_copy(kvstr[slot][:, E:2 * E], pj32[slot][:, :]))
        tkd = p.op(VE, lambda v, slot=slot: v.tensor_tensor(kvstr[slot][:, 0:E],
                                                            kvstr[slot][:, E:2 * E],
                                                            kdec[:, :], ALU.mult))
        toks = [tkr, tkd]
        B_pj[slot].read(toks[-1])
        bkv.wrote(toks[-1])
        if kvs_d is not None and dbg_j == 22:
            p.waits(SY, toks)
            tks = p.dma(SY, kvs_d, kvstr[slot][:, :], r_kvs)
            bkv.read(tks)
        p.waits(SY, toks)
        p.dma(SY, bounce_kd, kvstr[slot][:, 0:E], w_kk[slot])
        if bounce_k is not None:
            p.dma(SY, bounce_k, kvstr[slot][:, E:2 * E], w_kk[slot])
        bkv.read(p.full(w_kk[slot]))

    for j in range(NCH):
        slot = j % 2
        tt = load_trig(slot, cos_o, sin_o, j)
        kproj(lambda f, j=j: xn_fm[:, f, j * 128:(j + 1) * 128], slot,
              [t_wk, t_xnfm_all], tt, kd_b[j * 128:(j + 1) * 128, :],
              k_b[j * 128:(j + 1) * 128, :])
        quant_reduce_step(3)

    B_xnp = [Buf(), Buf()]
    for g in range(T_PRE // 256):
        gs = g % 2
        B_xnp[gs].before_write(p, SC)
        p.waits(SC, t_xnp_all)
        for f in range(NF):
            p.dmat(SC, xnp_g[gs][:, f, :], xnp_b[g * 256:(g + 1) * 256, f * 128:(f + 1) * 128],
                   r_xnp[gs])
        tg = p.full(r_xnp[gs])
        B_xnp[gs].wrote(tg)
        for jj in range(2):
            j = g * 2 + jj
            slot = jj % 2
            tt = load_trig(slot, cos_p, sin_p, j)
            kproj(lambda f, jj=jj, gs=gs: xnp_g[gs][:, f, jj * 128:(jj + 1) * 128], slot,
                  [t_wk, tg], tt, kdp_b[j * 128:(j + 1) * 128, :], None, dbg_j=j)
            B_xnp[gs].read(wrs_readers[-1])
            quant_reduce_step(2)


    quant_reduce_step(len(qred_jobs))
    p.waits(VE, [p.ctok(SC)])
    p.opq(VE, lambda v: v.tensor_reduce(pred[:, 0:1], part1[:, :], AX.X, ALU.add))
    tpr = p.op(VE, lambda v: v.tensor_reduce(pred[:, 1:2], part2[:, :], AX.X, ALU.add))
    p.waits(TE, [tpr])
    if pbank.get(2) is not None:
        p.wait(TE, pbank[2])
    tsum = p.op(TE, lambda t: t.matmul(ps[:, 1024:1026], ones128f[:, :], pred[:, 0:2],
                                       start=True, stop=True))
    p.wait(VE, tsum)
    NW = float(DFF * E)
    p.opq(VE, lambda v: v.tensor_scalar(sc12[:, 0:2], ps[:, 1024:1026], 1.0 / NW, 1e-8,
                                        ALU.mult, ALU.max))
    tq0_box[0] = p.op(VE, lambda v: v.reciprocal(inv12[:, 0:2], sc12[:, 0:2]))
    pbank[2] = tq0_box[0]

    p.waits(GP, wrs_readers)
    for t in wrs_readers:
        B_wrs.read(t)
    t_wv = load_W(wv_d)
    wrs_readers = []

    def evac_kv(slot):
        def w(n, bk):
            if n == 0:
                B_kv[slot].before_write(p, SC)
            return p.op(SC, lambda s, bk=bk, n=n, slot=slot:
                        s.copy(kvstr[slot][:, n * 512:(n + 1) * 512], bk))
        return w

    def vproj(xn_t, banks, slot, wait_toks, dst):
        et = proj_chunk(xn_t, banks, evac_kv(slot), wait_toks)
        wrs_readers.append(et[-1])
        B_kv[slot].wrote(et[-1])
        p.waits(GP, et)
        p.dma(GP, dst, kvstr[slot][:, 0:E], w_kv[slot])
        tw = p.full(w_kv[slot])
        B_kv[slot].read(tw)
        return tw

    for j in range(NCH):
        vproj(lambda f, j=j: xn_fm[:, f, j * 128:(j + 1) * 128],
              (0, 1, 2, 3), j % 2, [t_wv], v_b[j * 128:(j + 1) * 128, :])
        quant_write_step(2)
    for g in range(T_PRE // 256):
        gs = g % 2
        B_xnp[gs].before_write(p, SC)
        for f in range(NF):
            p.dmat(SC, xnp_g[gs][:, f, :], xnp_b[g * 256:(g + 1) * 256, f * 128:(f + 1) * 128],
                   r_xnp[gs])
        tg = p.full(r_xnp[gs])
        B_xnp[gs].wrote(tg)
        for jj in range(2):
            j = g * 2 + jj
            tw = vproj(lambda f, jj=jj, gs=gs: xnp_g[gs][:, f, jj * 128:(jj + 1) * 128],
                       (0, 1, 2, 3), jj % 2, [t_wv, tg], vp_b[j * 128:(j + 1) * 128, :])
            B_xnp[gs].read(tw)
            quant_write_step(2)

    p.waits(GP, wrs_readers)
    for t in wrs_readers:
        B_wrs.read(t)
    t_wq = load_W(wq_d)
    wrs_readers = []
    for j in range(NCH):
        slot = j % 2
        tt = load_trig(slot, cos_o, sin_o, j)
        et = proj_chunk(lambda f, j=j: xn_fm[:, f, j * 128:(j + 1) * 128],
                        (0, 1, 2, 3), evac_pj(slot), [t_wq])
        B_pj[slot].wrote(et[-1])
        wrs_readers.append(et[-1])
        p.waits(VE, et + [tt])
        trot = emit_rotary(pj32[slot], trigc[slot], trigs[slot])
        B_trig[slot].read(trot)
        bkv = B_kv[slot]
        bkv.before_write(p, VE)
        tqr = p.op(VE, lambda v, slot=slot: v.tensor_copy(kvstr[slot][:, 0:E], pj32[slot][:, :]))
        B_pj[slot].read(tqr)
        bkv.wrote(tqr)
        p.wait(GP, tqr)
        p.dma(GP, q_b[j * 128:(j + 1) * 128, :], kvstr[slot][:, 0:E], w_kv[slot])
        bkv.read(p.full(w_kv[slot]))
        quant_write_step(2)

    t_bounce_all = [p.full(w_kv[0]), p.full(w_kv[1]), p.full(w_kk[0]), p.full(w_kk[1])]
    t_kv_upd = None
    t_kv_par = [None, None]
    for b_ in range(4):
        if pbank.get(b_) is not None:
            p.wait(TE, pbank[b_])
            pbank[b_] = None
    for l in range(NPC):
        slot = l % 2
        base = 2048 * (l % 2)
        bkv = B_kv[slot]
        bkv.before_write(p, SY)
        p.waits(SY, t_bounce_all)
        p.dma(SY, kvstr[slot][:, 0:E], kdp_b[l * 128:(l + 1) * 128, :], r_kvld[slot])
        p.dma(SY, kvstr[slot][:, E:2 * E], vp_b[l * 128:(l + 1) * 128, :], r_kvld[slot])
        tv = p.full(r_kvld[slot])
        bkv.wrote(tv)
        p.waits(TE, [tv])
        if t_kv_par[l % 2] is not None:
            p.wait(TE, t_kv_par[l % 2])
        tdel = None
        for h in range(H):
            A = kvstr[slot][:, h * 128:(h + 1) * 128]
            B = kvstr[slot][:, E + h * 128:E + (h + 1) * 128]
            bk = ps[:, base + h * 128:base + (h + 1) * 128]
            if h == H - 1:
                tdel = p.op(TE, lambda t, A=A, B=B, bk=bk: t.matmul(bk, A, B, start=True, stop=True))
            else:
                p.opq(TE, lambda t, A=A, B=B, bk=bk: t.matmul(bk, A, B, start=True, stop=True))
        bkv.read(tdel)
        p.waits(VE, [tdel, t_tbl])
        for hx in range(H):
            t_kv_upd = p.op(VE, lambda v, hx=hx, base=base: v.scalar_tensor_tensor(
                KV32[:, hx * 128:(hx + 1) * 128], KV32[:, hx * 128:(hx + 1) * 128],
                _R128[hx], ps[:, base + hx * 128:base + (hx + 1) * 128],
                ALU.mult, ALU.add))
        t_kv_par[l % 2] = t_kv_upd
        quant_write_step(1)

    gb = [None] * 8
    p.wait(TE, t_kv_upd)
    p.waits(SC, [p.ctok(VE)])
    B_wts = Buf()
    B_w2s = Buf()
    t_gadd = None
    g_last = None
    for sp_i in range(16):
        B_wts.before_write(p, SC)
        p.waits(SC, [qwr_tok[4 * sp_i + 2], qwr_tok[4 * sp_i + 3]])
        for f in range(NF):
            p.dmat(SC, wtile_sp[:, f, :],
                   w1q_b[sp_i * 512:(sp_i + 1) * 512, f * 128:(f + 1) * 128], r_wt[0])
        t_wts = p.full(r_wt[0])
        B_wts.wrote(t_wts)
        B_w2s.before_write(p, SC)
        jlast = 64 + 16 * (sp_i // 4) + 15
        p.waits(SC, [qwr_tok[jlast - 1], qwr_tok[jlast]])
        for fo4 in range(4):
            for dp in range(2):
                p.dmat(SC, w2T[:, fo4, dp * 1024:(dp + 1) * 1024],
                       w2q_b[dp * 1024:(dp + 1) * 1024,
                             (sp_i * 4 + fo4) * 128:(sp_i * 4 + fo4 + 1) * 128], r_w2s[0])
        t_w2 = p.full(r_w2s[0])
        B_w2s.wrote(t_w2)
        p.waits(TE, [t_wts, t_xnfm_all])
        for b in range(8):
            if gb[b] is not None:
                p.wait(TE, gb[b])
            if sp_i == 0 and pbank.get(b) is not None:
                p.wait(TE, pbank[b])
                pbank[b] = None
        htok = None
        for f in range(NF):
            for fo4 in range(4):
                A = wtile_sp[:, f, fo4 * 128:(fo4 + 1) * 128]
                for th in range(2):
                    bk = bank(fo4 * 2 + th)
                    B = xn_fm[:, f, th * 512:(th + 1) * 512]
                    st = (f == 0); spx = (f == NF - 1)
                    if spx and fo4 == 3 and th == 1:
                        htok = p.op(TE, lambda t, A=A, B=B, bk=bk, st=st, spx=spx:
                                    t.matmul(bk, A, B, start=st, stop=spx))
                    else:
                        p.opq(TE, lambda t, A=A, B=B, bk=bk, st=st, spx=spx:
                              t.matmul(bk, A, B, start=st, stop=spx))
        B_wts.read(htok)
        p.waits(SC, [htok, t_tbl])
        if g_last is not None:
            p.wait(SC, g_last)
        ev_last = None
        for fo4 in range(4):
            for th in range(2):
                ev_last = p.op(SC, lambda s, fo4=fo4, th=th, sp_i=sp_i:
                               s.activation(h_sp[:, fo4, th * 512:(th + 1) * 512],
                                            bank(fo4 * 2 + th), AF.Gelu,
                                            bias=b1c[:, sp_i * 4 + fo4:sp_i * 4 + fo4 + 1],
                                            scale=1.0))
        for m in range(32):
            db, th, b = m // 2, m % 2, m % 8
            bk = bank(b)
            p.waits(TE, [ev_last, t_w2])
            if gb[b] is not None:
                p.wait(TE, gb[b])
            tmm = None
            for fo4 in range(4):
                A = w2T[:, fo4, db * 128:(db + 1) * 128]
                B = h_sp[:, fo4, th * 512:(th + 1) * 512]
                st = (fo4 == 0)
                if fo4 == 3:
                    tmm = p.op(TE, lambda t, A=A, B=B, bk=bk, st=st:
                               t.matmul(bk, A, B, start=st, stop=True))
                else:
                    p.opq(TE, lambda t, A=A, B=B, bk=bk, st=st:
                          t.matmul(bk, A, B, start=st, stop=False))
            p.wait(VE, tmm)
            if sp_i == 0:
                t_gadd = p.op(VE, lambda v, db=db, th=th, bk=bk:
                              v.tensor_copy(g32[:, db, th * 512:(th + 1) * 512], bk))
            else:
                t_gadd = p.op(VE, lambda v, db=db, th=th, bk=bk:
                              v.tensor_tensor(g32[:, db, th * 512:(th + 1) * 512],
                                              g32[:, db, th * 512:(th + 1) * 512], bk, ALU.add))
            gb[b] = t_gadd
        g_last = tmm
        B_w2s.read(g_last)
        quant_write_step(2)
    t_q_done = [p.full(w_qw[0]), p.full(w_qw[1]), p.full(r_qw[0]), p.full(r_qw[1]), last_tqo[0]]

    silu_tok = [None] * 32
    p.waits(SC, [t_gadd, t_tbl] + t_q_done)
    for d in range(NF):
        for half in range(2):
            silu_tok[d * 2 + half] = p.op(SC, lambda s, d=d, half=half:
                                          s.activation(g_fm[:, d, half * 512:(half + 1) * 512],
                                                       g32[:, d, half * 512:(half + 1) * 512],
                                                       AF.Silu, bias=b2c[:, d:d + 1], scale=1.0))
    last_mm_half = silu_tok[31]
    gbank = gb

    ffn_gate = [t for t in silu_tok if t is not None]
    p.waits(GP, ffn_gate)
    p.waits(TE, [t for t in gbank if t is not None])
    p.dma(GP, dmaskT[:, :], dmaskT_d, r_stbl)
    p.dma(GP, qdecF[:, :], qdecF_d, r_stbl)
    t_stbl = p.full(r_stbl)
    p.wait(VE, t_kv_upd)
    t_kvbf = p.op(VE, lambda v: v.tensor_copy(KVbf[:, :], KV32[:, :]))
    if kvpre_d is not None:
        r_dbg = p.res("dbg")
        p.wait(GP, t_kvbf)
        p.dma(GP, kvpre_d, KV32[:, :], r_dbg)

    B_kq = Buf()
    last_tsr = None
    last_yn = None
    scoreP_par = [None, None]
    yev_par = [None, None]
    tYp = [None, None]
    P_pair = [P_bf, P_bfB]
    qt_pair = [qtile, qtileB]
    for grp in range(2):
        B_kq.before_write(p, SY)
        B_kq.before_write(p, SC)
        p.waits(SY, ffn_gate + t_bounce_all)
        p.waits(SC, ffn_gate + t_bounce_all)
        for f in range(NF):
            p.dmat(SC, k4[:, f, :], k_b[grp * 512:(grp + 1) * 512, f * 128:(f + 1) * 128], r_kq)
            p.dmat(SC, q4[:, f, :], q_b[grp * 512:(grp + 1) * 512, f * 128:(f + 1) * 128], r_kq)
        t_kq_all = p.full(r_kq)
        B_kq.wrote(t_kq_all)
        if kq_d is not None and grp == 0:
            if r_dbg is None:
                r_dbg = p.res("dbg")
            p.wait(GP, t_kq_all)
            p.dma(GP, kq_d[0], k4[:, :, :].rearrange("p a b -> p (a b)"), r_dbg)
            p.dma(GP, kq_d[1], q4[:, :, :].rearrange("p a b -> p (a b)"), r_dbg)
        for jj in range(4):
            j = grp * 4 + jj
            slot = j % 2
            bkv = B_kv[slot]
            bkv.before_write(p, SY)
            p.dma(SY, kvstr[slot][:, 0:E], kd_b[j * 128:(j + 1) * 128, :], r_kvld[slot])
            p.dma(SY, kvstr[slot][:, E:2 * E], v_b[j * 128:(j + 1) * 128, :], r_kvld[slot])
            tv = p.full(r_kvld[slot])
            bkv.wrote(tv)
            t_yev = []
            tdel = None
            tS_tok = [None] * H

            def emit_S(h, jj=jj):
                par = h % 2
                bkS = ps[:, 0:128] if par == 0 else ps[:, 1024:1152]
                p.waits(TE, [t_kq_all])
                if scoreP_par[par] is not None:
                    p.wait(TE, scoreP_par[par])
                if par == 1 and last_tsr is not None:
                    p.wait(TE, last_tsr)
                tS_tok[h] = p.op(TE, lambda t, h=h, jj=jj, bkS=bkS:
                                 t.matmul(bkS, k4[:, h, jj * 128:(jj + 1) * 128],
                                          q4[:, h, jj * 128:(jj + 1) * 128],
                                          start=True, stop=True))

            emit_S(0)
            for h in range(H):
                par = h % 2
                bkS = ps[:, 0:128] if par == 0 else ps[:, 1024:1152]
                if h + 1 < H:
                    emit_S(h + 1)
                Pb = P_pair[par]
                qt = qt_pair[par]
                p.waits(VE, [tS_tok[h], t_stbl])
                if tYp[par] is not None:
                    p.wait(VE, tYp[par])
                scoreP_par[par] = p.op(VE, lambda v, h=h, bkS=bkS, Pb=Pb:
                                       v.tensor_tensor(Pb[:, :], bkS,
                                                       dmaskT[:, h * 128:(h + 1) * 128],
                                                       ALU.mult))
                tqd = p.op(VE, lambda v, h=h, jj=jj, qt=qt:
                           v.tensor_tensor(qt[:, :], q4[:, h, jj * 128:(jj + 1) * 128],
                                           qdecF[:, h * 128:(h + 1) * 128], ALU.mult))
                bkY = ps[:, 512:640] if par == 0 else ps[:, 1536:1664]
                p.waits(TE, [scoreP_par[par], tqd, tv, t_kvbf])
                if yev_par[par] is not None:
                    p.wait(TE, yev_par[par])
                p.opq(TE, lambda t, h=h, slot=slot, bkY=bkY, Pb=Pb:
                      t.matmul(bkY, kvstr[slot][:, E + h * 128:E + (h + 1) * 128], Pb[:, :],
                               start=True, stop=False))
                tY = p.op(TE, lambda t, h=h, bkY=bkY, qt=qt:
                          t.matmul(bkY, KVbf[:, h * 128:(h + 1) * 128], qt[:, :],
                                   start=False, stop=True))
                tYp[par] = tY
                if last_yn is not None:
                    p.wait(SC, last_yn)
                p.wait(SC, tY)
                yev_par[par] = p.op(SC, lambda s, h=h, bkY=bkY: s.copy(Y_sb[:, h, :], bkY))
                t_yev.append(yev_par[par])
                bkD = ps[:, 2048 + h * 128:2048 + (h + 1) * 128]
                if t_kv_upd is not None:
                    p.wait(TE, t_kv_upd)
                if h == H - 1:
                    tdel = p.op(TE, lambda t, h=h, slot=slot, bkD=bkD:
                                t.matmul(bkD, kvstr[slot][:, h * 128:(h + 1) * 128],
                                         kvstr[slot][:, E + h * 128:E + (h + 1) * 128],
                                         start=True, stop=True))
                else:
                    p.opq(TE, lambda t, h=h, slot=slot, bkD=bkD:
                          t.matmul(bkD, kvstr[slot][:, h * 128:(h + 1) * 128],
                                   kvstr[slot][:, E + h * 128:E + (h + 1) * 128],
                                   start=True, stop=True))
            bkv.read(tdel)
            B_kq.read(tdel)
            p.waits(VE, [tdel])
            for hx in range(H):
                t_kv_upd = p.op(VE, lambda v, hx=hx: v.scalar_tensor_tensor(
                    KV32[:, hx * 128:(hx + 1) * 128], KV32[:, hx * 128:(hx + 1) * 128],
                    _R128[hx], ps[:, 2048 + hx * 128:2048 + (hx + 1) * 128],
                    ALU.mult, ALU.add))
            t_kvbf = p.op(VE, lambda v: v.tensor_copy(KVbf[:, :], KV32[:, :]))
            p.waits(SC, t_yev)
            if ysb_d is not None:
                if r_dbg is None:
                    r_dbg = p.res("dbg")
                p.waits(GP, t_yev)
                tyd = p.dma(GP, ysb_d[j * 128:(j + 1) * 128, :],
                            Y_sb[:, :, :].rearrange("p a b -> p (a b)"), r_dbg)
                p.wait(SC, tyd)
            tsq = p.op(SC, lambda s: s.activation(Ysq[:, :, :].rearrange("p a b -> p (a b)"),
                                                  Y_sb[:, :, :].rearrange("p a b -> p (a b)"),
                                                  AF.Square))
            bkR = ps[:, 1024:1152]
            p.waits(TE, [tsq])
            if scoreP_par[1] is not None:
                p.wait(TE, scoreP_par[1])
            if last_tsr is not None:
                p.wait(TE, last_tsr)
            tss = None
            for e in range(NF):
                st = (e == 0); sp = (e == NF - 1)
                if sp:
                    tss = p.op(TE, lambda t, e=e, bkR=bkR, st=st:
                               t.matmul(bkR, ones128[:, :], Ysq[:, e, :], start=st, stop=True))
                else:
                    p.opq(TE, lambda t, e=e, bkR=bkR, st=st:
                          t.matmul(bkR, ones128[:, :], Ysq[:, e, :], start=st, stop=False))
            p.wait(SC, tss)
            last_tsr = p.op(SC, lambda s, bkR=bkR: s.activation(rtmp[:, :], bkR, AF.Sqrt,
                                                                bias=epsrms[:, 0:1], scale=1.0 / E))
            p.waits(VE, [last_tsr, t_tbl])
            trc = p.op(VE, lambda v: v.reciprocal(invb[:, :], rtmp[:, :]))
            p.op(VE, lambda v: v.tensor_copy(invb_bf[:, :], invb[:, :]))
            for e in range(NF):
                last_yn = p.op(VE, lambda v, e=e, j=j:
                               v.scalar_tensor_tensor(Yn_all[:, e, j * 128:(j + 1) * 128],
                                                      Y_sb[:, e, :], rmsc[:, e:e + 1],
                                                      invb_bf[:, :], ALU.mult, ALU.mult))

    if yn_d is not None:
        if r_dbg is None:
            r_dbg = p.res("dbg")
        p.waits(GP, [last_yn, last_yev])
        p.dma(GP, yn_d, Yn_all[:, :, :].rearrange("p a b -> p (a b)"), r_dbg)
    if rct_d is not None:
        r_rct = p.res("rct")
        p.wait(SY, last_yn)
        p.dma(SY, rct_d, rCt[:, :], r_rct)

    p.waits(GP, [last_yn, t_kv_upd])
    B_wo = Buf()
    B_wo.before_write(p, GP)
    for f in range(NF):
        p.dma(GP, wo_sb[:, f, :], wo_d[f * 128:(f + 1) * 128, :], r_wo)
    t_wo = p.full(r_wo)
    B_wo.wrote(t_wo)
    t_iota = p.op(GP, lambda g2: (g2.iota(iota_p[:, :], [[0, 128]], channel_multiplier=1,
                                          allow_small_or_imprecise_dtypes=True),
                                  g2.iota(iota_f[:, :], [[1, 128]], channel_multiplier=0,
                                          allow_small_or_imprecise_dtypes=True))[-1])
    p.waits(VE, [t_iota, last_yn])
    t_ident = p.op(VE, lambda v: v.tensor_tensor(ident[:, :], iota_p[:, :], iota_f[:, :], ALU.is_equal))

    B_sg = Buf()
    tT_last = {}
    sgo_tok = [None, None]
    w_sg_half = []
    for half in range(2):
        for foc in range(NF):
            bk_i = foc % 2
            bk = bank(bk_i)
            p.waits(TE, [t_wo, last_yn])
            if pbank.get(bk_i) is not None:
                p.wait(TE, pbank[bk_i])
                pbank[bk_i] = None
            tS = None
            for f in range(NF):
                A = wo_sb[:, f, foc * 128:(foc + 1) * 128]
                B = Yn_all[:, f, half * 512:(half + 1) * 512]
                st = (f == 0); sp = (f == NF - 1)
                if sp:
                    tS = p.op(TE, lambda t, A=A, B=B, bk=bk, st=st: t.matmul(bk, A, B, start=st, stop=True))
                else:
                    p.opq(TE, lambda t, A=A, B=B, bk=bk, st=st: t.matmul(bk, A, B, start=st, stop=False))
            B_sg.before_write(p, VE)
            p.wait(VE, tS)
            tg = p.op(VE, lambda v, bk=bk, foc=foc, half=half:
                      v.tensor_tensor(sgld[:, 0:512], bk, g_fm[:, foc, half * 512:(half + 1) * 512],
                                      ALU.mult))
            pbank[bk_i] = tg
            B_sg.wrote(tg)
            bkT_i = 4 + (foc % 2)
            slot = foc % 2
            p.waits(TE, [tg, t_ident])
            if tT_last.get(bkT_i) is not None:
                p.wait(TE, tT_last[bkT_i])
            tT = None
            for cs in range(4):
                bkT = bank(bkT_i, 512)[:, cs * 128:(cs + 1) * 128]
                tT = p.op(TE, lambda t, cs=cs, bkT=bkT:
                          t.transpose(bkT, sgld[:, cs * 128:(cs + 1) * 128], ident[:, :]))
            B_sg.read(tT)
            p.wait(SC, tT)
            if sgo_tok[slot] is not None:
                p.wait(SC, sgo_tok[slot])
            tcp = p.op(SC, lambda s, bkT_i=bkT_i, slot=slot:
                       s.copy(sgout2[:, slot, :], bank(bkT_i, 512)))
            tT_last[bkT_i] = tcp
            p.wait(GP, tcp)
            dst = sg_b[half * 512:(half + 1) * 512,
                       foc * 128:(foc + 1) * 128].rearrange("(a p) c -> p a c", p=128)
            p.dma(GP, dst, sgout2[:, slot, :].rearrange("p (a c) -> p a c", c=128), w_sg[slot])
            sgo_tok[slot] = p.full(w_sg[slot])
        w_sg_half.append(p.full(w_sg[0]))
        w_sg_half.append(p.full(w_sg[1]))

    B_sgld = Buf()
    for cch in range(NCH):
        p.waits(SY, w_sg_half[2 * (cch // 4):2 * (cch // 4) + 2])
        B_sgld.before_write(p, SY)
        p.dma(SY, sgf[:, :], sg_b[cch * 128:(cch + 1) * 128, :], r_fin)
        p.dma(SY, xbf[:, :], xo[cch * 128:(cch + 1) * 128, :], r_fin)
        t_both = p.full(r_fin)
        B_sgld.wrote(t_both)
        p.waits(VE, [t_both])
        ta = p.op(VE, lambda v: v.tensor_tensor(sgf[:, :], sgf[:, :], xbf[:, :], ALU.add))
        B_sgld.read(ta)
        p.wait(GP, ta)
        tst = p.dma(GP, out_d[cch * 128:(cch + 1) * 128, :], sgf[:, :], w_out)
        B_sgld.read(tst)

    fin = w_out.cnt
    p.ops[GP].append(lambda g2, sem=w_out.sem, v=fin: g2.wait_ge(sem, v))

    block = ctx.enter_context(nc.Block())
    p.finalize(block)
    return nc


def kernel(L0_attn_x, L0_attn_Wq, L0_attn_Wk, L0_attn_Wv, L0_attn_Wo, L0_attn_rms_w,
           L0_ffn_w1, L0_ffn_b1, L0_ffn_w2, L0_ffn_b2):
    x = np.ascontiguousarray(np.asarray(L0_attn_x, dtype=np.float32))
    wq = np.ascontiguousarray(np.asarray(L0_attn_Wq, dtype=np.float32))
    wk = np.ascontiguousarray(np.asarray(L0_attn_Wk, dtype=np.float32))
    wv = np.ascontiguousarray(np.asarray(L0_attn_Wv, dtype=np.float32))
    wo = np.ascontiguousarray(np.asarray(L0_attn_Wo, dtype=np.float32))
    rms_w = np.asarray(L0_attn_rms_w, dtype=np.float32)
    w1 = np.ascontiguousarray(np.asarray(L0_ffn_w1, dtype=np.float32))
    b1 = np.asarray(L0_ffn_b1, dtype=np.float32)
    w2 = np.ascontiguousarray(np.asarray(L0_ffn_w2, dtype=np.float32))
    b2 = np.asarray(L0_ffn_b2, dtype=np.float32)

    qdecF, kdec, dmaskT, rCt, freq = _host_tables()
    b1c = np.ascontiguousarray(b1.reshape(DFF // 128, 128).T)
    b2c = np.ascontiguousarray(b2.reshape(NF, 128).T)
    rmsc = np.ascontiguousarray(rms_w.reshape(NF, 128).T)

    in_maps = []
    for core in range(8):
        b, s = core // 4, core % 4
        xo_h = np.ascontiguousarray(x[b, s * T_OWN:(s + 1) * T_OWN])
        pre_rows = s * T_OWN
        xp_h = np.zeros((T_PRE, E), dtype=np.float32)
        if pre_rows > 0:
            take = min(pre_rows, T_PRE)
            xp_h[T_PRE - take:] = x[b, pre_rows - take:pre_rows]
        t_own = s * T_OWN + np.arange(T_OWN)
        t_pre = s * T_OWN - T_PRE + np.arange(T_PRE)
        cos_o, sin_o = _trig(t_own, freq)
        cos_p, sin_p = _trig(t_pre, freq)
        in_maps.append({
            "xo": xo_h, "xp": xp_h,
            "wq": wq, "wk": wk, "wv": wv, "wo": wo,
            "w1": w1, "w2": w2,
            "b1c": b1c, "b2c": b2c, "rmsc": rmsc,
            "cos_o": cos_o, "sin_o": sin_o, "cos_p": cos_p, "sin_p": sin_p,
            "qdecF": qdecF, "kdec": kdec, "dmaskT": dmaskT, "rCt": rCt,
        })

    with ExitStack() as ctx:
        nc = build_program(ctx)
        res = run_bass_kernel_spmd(nc, in_maps, core_ids=list(range(8)))
    global LAST_RES
    LAST_RES = res

    out = np.empty((2, 4096, E), dtype=np.float32)
    for core in range(8):
        b, s = core // 4, core % 4
        out[b, s * T_OWN:(s + 1) * T_OWN] = res.results[core]["out"]
    return out
